# version 02
# baseline (speedup 1.0000x reference)
import numpy as np
import contextlib
import concourse.bass as bass
import concourse.mybir as mybir
from concourse.bass_utils import run_bass_kernel_spmd

F32 = mybir.dt.float32
BF16 = mybir.dt.bfloat16
AF = mybir.ActivationFunctionType
ALU = mybir.AluOpType
AXX = mybir.AxisListType.X

D = 1024
DJ = 8
FJ = 32
T = 2048
NSEQ = 2
NTOK = NSEQ * T
NE = 8
NEG = -1.0e30
SCALE = 128.0 ** -0.5


class Sem:
    def __init__(self, nc, name):
        self.h = nc.alloc_semaphore(name)
        self.n = 0
        self.name = name


class Builder:
    ENGINES = ("sync", "gpsimd", "tensor", "scalar", "vector")

    def __init__(self, phases=("A", "F1", "C", "F2"), debug_xfm=False, n_barrier=True):
        self.phases = phases
        self.debug_xfm = debug_xfm
        nc = self.nc = bass.Bass("TRN2", target_bir_lowering=False)
        self.ops = {e: [] for e in self.ENGINES}
        self.hwm = {e: {} for e in self.ENGINES}

        dram = lambda n, s: nc.dram_tensor(n, s, F32, kind="ExternalInput")
        self.x_in = dram("x", [NSEQ, T, D])
        self.W1p = dram("W1p", [8, 128, 8, 512])
        self.W2p = dram("W2p", [8, 128, 8, 512])
        self.Wzp = dram("Wzp", [32, 128, 32, 128])
        self.Whp = dram("Whp", [32, 128, 32, 128])
        self.Wop = dram("Wop", [8, 128, 32, 128])
        self.b1h = dram("b1h", [128, 32])
        self.b2h = dram("b2h", [128, 32])
        self.bzh = dram("bzh", [128, 32])
        self.bhh = dram("bhh", [128, 32])
        self.boh = dram("boh", [128, 8])
        self.wqp = dram("wqp", [8, 128, 8, 128])
        self.wkdp = dram("wkdp", [128, 8, 64])
        self.wvdp = dram("wvdp", [128, 8, 64])
        self.wkup = dram("wkup", [32, 128])
        self.wvup = dram("wvup", [32, 128])
        self.woap = dram("woap", [8, 128, 8, 128])
        self.wrgp = dram("wrgp", [128, 8, 16])
        self.wgp = dram("wgp", [NE, 8, 128, 8, 512])
        self.wup = dram("wup", [NE, 8, 128, 8, 512])
        self.wdp = dram("wdp", [NE, 8, 128, 32, 128])
        self.onesh = dram("onesh", [128, 128])
        self.identh = dram("identh", [128, 128])
        self.maskh = dram("maskh", [4, 128, 512])

        self.y_out = nc.dram_tensor("y", [NSEQ, T, D], F32, kind="ExternalOutput")
        if debug_xfm:
            self.dbg = nc.dram_tensor("dbg_xfm", [DJ, 128, NTOK], F32, kind="ExternalOutput")
            self.dbg_cwT = nc.dram_tensor("dbg_cwT", [64, 128], F32, kind="ExternalOutput")
            self.dbg_cwb = nc.dram_tensor("dbg_cwb", [128, 1024], F32, kind="ExternalOutput")
            self.dbg_xn = nc.dram_tensor("dbg_xn", [128, 1024], F32, kind="ExternalOutput")
            self.dbg_h = nc.dram_tensor("dbg_h", [128, 1024], F32, kind="ExternalOutput")
            self.dbg_s = nc.dram_tensor("dbg_s", [128, 1024], F32, kind="ExternalOutput")
            self.dbg_t0 = nc.dram_tensor("dbg_t0", [128, 48], F32, kind="ExternalOutput")
            self.dbgf2 = nc.alloc_sbuf_tensor("dbgf2", [128, 48], F32)
            self.dbgf = nc.alloc_sbuf_tensor("dbgf", [128, 1024], F32)
        self.xfm = nc.dram_tensor("xfm", [DJ, 128, NTOK], F32, kind="Internal")
        cache = lambda n, s: nc.dram_tensor(n, s, BF16, kind="Internal")
        self.cW1 = cache("cW1", [8, 128, 8, 512])
        self.cW2 = cache("cW2", [8, 128, 8, 512])
        self.cWz = cache("cWz", [32, 128, 32, 128])
        self.cWh = cache("cWh", [32, 128, 32, 128])
        self.cWo = cache("cWo", [8, 128, 32, 128])
        self.cwq = cache("cwq", [8, 128, 8, 128])
        self.cwoa = cache("cwoa", [8, 128, 8, 128])
        self.cwg = cache("cwg", [NE, 8, 128, 8, 512])
        self.cwu = cache("cwu", [NE, 8, 128, 8, 512])
        self.cwd = cache("cwd", [NE, 8, 128, 32, 128])
        self.cv_tok = {}

        self.x_tm = self.x_in.ap().rearrange("s (tb p) d -> s tb p d", p=128)
        self.y_tm = self.y_out.ap().rearrange("s (tb p) d -> s tb p d", p=128)
        self.xfm_t = self.xfm.ap().transpose([1, 0, 2])

        self.s_wt = Sem(nc, "s_wt")
        self.s_cv = Sem(nc, "s_cv")
        self.s_x = Sem(nc, "s_x")
        self.s_st = Sem(nc, "s_st")
        self.s_pe = Sem(nc, "s_pe")
        self.s_act = Sem(nc, "s_act")
        self.s_dve = Sem(nc, "s_dve")

        self.ones_f = nc.alloc_sbuf_tensor("ones_f", [128, 128], F32)
        self.ones_b = nc.alloc_sbuf_tensor("ones_b", [128, 128], BF16)
        self.ident_f = nc.alloc_sbuf_tensor("ident_f", [128, 128], F32)
        self.ident_b = nc.alloc_sbuf_tensor("ident_b", [128, 128], BF16)
        self.spacer = nc.alloc_sbuf_tensor("spcr", [1, 8], F32)
        self.eps = nc.alloc_sbuf_tensor("epsv", [128, 1], F32)
        self.carry = nc.alloc_sbuf_tensor("carry", [128, 32], F32)

        self.store_gate = 0

    def emit(self, eng, fn, waits=(), inc=None, inc_by=1):
        self.ops[eng].append((fn, tuple(waits), inc, inc_by))
        if inc is not None:
            inc.n += inc_by
            return inc.n
        return None

    def barrier(self):
        ops = self.ops
        self.ops = {e: [] for e in self.ENGINES}
        if not any(ops.values()):
            return
        with self.nc.Block() as blk:
            for eng in self.ENGINES:
                lst = ops[eng]
                if not lst:
                    continue

                def body(E, lst=lst, eng=eng):
                    hw = self.hwm[eng]
                    for fn, waits, inc, k in lst:
                        for sem, val in waits:
                            if val <= 0 or val <= hw.get(sem.name, 0):
                                continue
                            E.wait_ge(sem.h, val)
                            hw[sem.name] = val
                        ins = fn(E)
                        if inc is not None:
                            ins.then_inc(inc.h, k)

                getattr(blk, eng)(body)

    def dve_spacer(self, inc=None):
        return self.emit("vector", lambda v: v.memset(self.spacer[:], 0.0), inc=inc)

    def ln_from_fm(self, gtok0, tlen, xn_dst, psf, ba, bb, load_dst, sqb, mstat,
                   xf_war=False):
        E = self.emit
        s_x, s_act, s_dve, s_pe, s_st = self.s_x, self.s_act, self.s_dve, self.s_pe, self.s_st
        psa = psf[:, ba * 512:(ba + 1) * 512]
        psb_ = psf[:, bb * 512:(bb + 1) * 512]
        sq_free = [0, 0]
        stats_free = 0
        tkc_prev = 0
        for tt in range(tlen // 512):
            g0 = gtok0 + tt * 512
            sl = slice(tt * 512, (tt + 1) * 512)
            dst = load_dst(tt)
            w = [(s_st, self.store_gate)]
            if xf_war and tkc_prev:
                w.append((s_act, tkc_prev))
            tok = E("scalar", lambda e, g0=g0, dst=dst: e.dma_start(
                out=dst, in_=self.xfm_t[:, :, g0:g0 + 512]), waits=w, inc=s_x, inc_by=16)
            tkc = E("scalar", lambda e, sl=sl, dst=dst: e.copy(xn_dst[:, :, sl], dst),
                    waits=[(s_x, tok)], inc=s_act)
            tkc_prev = tkc
            for dj in range(DJ):
                w2 = [(s_act, tkc)] if dj == 0 else []
                if dj == 0 and stats_free:
                    w2.append((s_dve, stats_free))
                E("tensor", lambda e, dj=dj, sl=sl, psa=psa: e.matmul(
                    psa, self.ones_b[:], xn_dst[:, dj, sl],
                    start=(dj == 0), stop=(dj == DJ - 1)), waits=w2)
            last = None
            for dj in range(DJ):
                w2 = [(s_act, tkc)]
                if sq_free[dj % 2]:
                    w2.append((s_pe, sq_free[dj % 2]))
                tk2 = E("scalar", lambda e, dj=dj, sl=sl: e.activation(
                    sqb[:, dj % 2, :], xn_dst[:, dj, sl], AF.Square), waits=w2, inc=s_act)
                last = E("tensor", lambda e, dj=dj, psb_=psb_: e.matmul(
                    psb_, self.ones_b[:], sqb[:, dj % 2, :],
                    start=(dj == 0), stop=(dj == DJ - 1)), waits=[(s_act, tk2)], inc=s_pe)
                sq_free[dj % 2] = last
            E("vector", lambda e, psa=psa: e.tensor_scalar_mul(mstat[:, 0, :], psa, 1.0 / D),
              waits=[(s_pe, last)])
            E("vector", lambda e, psb_=psb_: e.tensor_scalar_mul(mstat[:, 1, :], psb_, 1.0 / D))
            E("vector", lambda e: e.tensor_mul(mstat[:, 2, :], mstat[:, 0, :], mstat[:, 0, :]))
            tk3 = E("vector", lambda e: e.tensor_sub(mstat[:, 1, :], mstat[:, 1, :], mstat[:, 2, :]),
                    inc=s_dve)
            stats_free = tk3
            tk4 = E("scalar", lambda e: e.activation(mstat[:, 3, :], mstat[:, 1, :], AF.Sqrt,
                                                     bias=self.eps[:]), waits=[(s_dve, tk3)], inc=s_act)
            E("vector", lambda e: e.reciprocal(mstat[:, 1, :], mstat[:, 3, :]), waits=[(s_act, tk4)])
            self.dve_spacer()
            for dj in range(DJ):
                E("vector", lambda e, dj=dj, sl=sl: e.tensor_sub(
                    mstat[:, 2, :], xn_dst[:, dj, sl], mstat[:, 0, :]))
                E("vector", lambda e, dj=dj, sl=sl: e.tensor_mul(
                    xn_dst[:, dj, sl], mstat[:, 2, :], mstat[:, 1, :]),
                  inc=(s_dve if dj == DJ - 1 else None))
        return s_dve.n


    def phase_A(self):
        nc, E = self.nc, self.emit
        s_wt, s_x, s_st, s_pe, s_act, s_dve = (self.s_wt, self.s_x, self.s_st,
                                               self.s_pe, self.s_act, self.s_dve)
        with contextlib.ExitStack() as _es:
            xn = _es.enter_context(nc.sbuf_tensor("a_xn", [128, DJ, 512], BF16))
            x_in = _es.enter_context(nc.sbuf_tensor("a_xin", [128, FJ, 512], BF16))
            t1s = _es.enter_context(nc.sbuf_tensor("a_t1", [128, FJ, 512], BF16))
            zbuf = _es.enter_context(nc.sbuf_tensor("a_zb", [128, 2, 512], F32))
            abuf = _es.enter_context(nc.sbuf_tensor("a_ab", [128, 2, 512], F32))
            bbuf = _es.enter_context(nc.sbuf_tensor("a_bb", [128, 2, 512], F32))
            hbuf = _es.enter_context(nc.sbuf_tensor("a_hb", [128, 2, 512], F32))
            wA = _es.enter_context(nc.sbuf_tensor("a_wA", [128, 2, 8, 512], BF16))
            wz = _es.enter_context(nc.sbuf_tensor("a_wz", [128, 2, 32, 128], BF16))
            wh = _es.enter_context(nc.sbuf_tensor("a_wh", [128, 2, 32, 128], BF16))
            xtm = _es.enter_context(nc.sbuf_tensor("a_xtm", [128, 2, 1024], F32))
            sqb = _es.enter_context(nc.sbuf_tensor("a_sqb", [128, 2, 512], BF16))
            mstat = _es.enter_context(nc.sbuf_tensor("a_mst", [128, 6, 512], F32))
            xold = _es.enter_context(nc.sbuf_tensor("a_xold", [128, DJ, 512], F32))
            xnew = _es.enter_context(nc.sbuf_tensor("a_xnew", [128, 2, 512], F32))
            abias = _es.enter_context(nc.sbuf_tensor("a_bias", [128, 5, 32], F32))
            aboT = _es.enter_context(nc.sbuf_tensor("a_bo", [128, 8], F32))
            ps = _es.enter_context(nc.psum_tensor("a_ps", [128, 8, 512], F32))
            psf = ps.ap().rearrange("p a b -> p (a b)")
            for src, di in [(self.b1h, 0), (self.b2h, 1), (self.bzh, 2), (self.bhh, 4)]:
                E("sync", lambda e, src=src, di=di: e.dma_start(out=abias[:, di, :], in_=src.ap()),
                  inc=s_x, inc_by=16)
            tkb = E("sync", lambda e: e.dma_start(out=aboT[:], in_=self.boh.ap()), inc=s_x, inc_by=16)
            E("vector", lambda e: e.tensor_scalar_mul(abias[:, 3, :], abias[:, 2, :], -1.0),
              waits=[(s_x, s_x.n)], inc=s_dve)

            wA_tok = [0, 0]; wA_free = [0, 0]
            wz_tok = [0, 0]; wz_free = [0, 0]
            wh_tok = [0, 0]; wh_free = [0, 0]
            ln_evac = [0, 0]
            st12_evac = [0, 0]
            st4_evac = [0, 0]
            st4_stored = [0, 0]
            st4_dma = [0, 0]
            stats_read = 0
            sq_free = [0, 0]

            for p in range(8):
                s = p // 4
                tok0 = (p % 4) * 512
                gtok0 = s * T + tok0
                self.barrier()
                xb = x_in.ap()[:, 0:8, :]
                for j in range(4):
                    par = j % 2
                    w = [(s_act, ln_evac[par])] if ln_evac[par] else []
                    tok = E("scalar", lambda e, s=s, j=j, par=par, tok0=tok0: e.dma_start(
                        out=xtm[:, par, :], in_=self.x_tm[s, tok0 // 128 + j]),
                        waits=w, inc=s_x, inc_by=16)
                    last = None
                    for dj in range(DJ):
                        w2 = [(s_x, tok)] if dj == 0 else []
                        last = E("tensor", lambda e, par=par, dj=dj: e.transpose(
                            ps[:, 2 * par + dj // 4, (dj % 4) * 128:(dj % 4 + 1) * 128],
                            xtm[:, par, dj * 128:(dj + 1) * 128], self.ident_f[:]),
                            waits=w2, inc=(s_pe if dj == DJ - 1 else None))
                    ln_evac[par] = E("scalar", lambda e, par=par, j=j, xb=xb: e.copy(
                        xb[:, :, j * 128:(j + 1) * 128],
                        psf[:, 2 * par * 512:(2 * par + 2) * 512].rearrange(
                            "p (dj t) -> p dj t", t=128)),
                        waits=[(s_pe, last)], inc=s_act)
                xb_done = s_act.n
                for dj in range(DJ):
                    w2 = [(s_act, xb_done)] if dj == 0 else []
                    if dj == 0 and stats_read:
                        w2.append((s_dve, stats_read))
                    E("tensor", lambda e, dj=dj, xb=xb: e.matmul(
                        ps[:, 6, :], self.ones_b[:], xb[:, dj, :],
                        start=(dj == 0), stop=(dj == DJ - 1)), waits=w2)
                last = None
                for dj in range(DJ):
                    w2 = [(s_act, xb_done)]
                    if sq_free[dj % 2]:
                        w2.append((s_pe, sq_free[dj % 2]))
                    tk2 = E("scalar", lambda e, dj=dj, xb=xb: e.activation(
                        sqb[:, dj % 2, :], xb[:, dj, :], AF.Square), waits=w2, inc=s_act)
                    last = E("tensor", lambda e, dj=dj: e.matmul(
                        ps[:, 7, :], self.ones_b[:], sqb[:, dj % 2, :],
                        start=(dj == 0), stop=(dj == DJ - 1)), waits=[(s_act, tk2)], inc=s_pe)
                    sq_free[dj % 2] = last
                E("vector", lambda e: e.tensor_scalar_mul(mstat[:, 0, :], ps[:, 6, :], 1.0 / D),
                  waits=[(s_pe, last)])
                E("vector", lambda e: e.tensor_scalar_mul(mstat[:, 1, :], ps[:, 7, :], 1.0 / D))
                E("vector", lambda e: e.tensor_mul(mstat[:, 2, :], mstat[:, 0, :], mstat[:, 0, :]))
                tk3 = E("vector", lambda e: e.tensor_sub(mstat[:, 3, :], mstat[:, 1, :],
                                                         mstat[:, 2, :]), inc=s_dve)
                stats_read = tk3
                tk4 = E("scalar", lambda e: e.activation(mstat[:, 4, :], mstat[:, 3, :], AF.Sqrt,
                                                         bias=self.eps[:]), waits=[(s_dve, tk3)], inc=s_act)
                E("vector", lambda e: e.reciprocal(mstat[:, 5, :], mstat[:, 4, :]),
                  waits=[(s_act, tk4)])
                self.dve_spacer()
                for dj in range(DJ):
                    E("vector", lambda e, dj=dj, xb=xb: e.tensor_sub(
                        mstat[:, 2, :], xb[:, dj, :], mstat[:, 0, :]))
                    E("vector", lambda e, dj=dj: e.tensor_mul(
                        xn[:, dj, :], mstat[:, 2, :], mstat[:, 5, :]),
                      inc=(s_dve if dj == DJ - 1 else None))
                xn_tok = s_dve.n
                self.barrier()
                for wsrc, dstbuf, bslot in [(self.cW1, x_in, 0), (self.cW2, t1s, 1)]:
                    for mb in range(8):
                        buf = mb % 2
                        w = [(s_pe, wA_free[buf])] if wA_free[buf] else []
                        w.append((self.s_cv, self.cv_tok["W2"]))
                        wA_tok[buf] = E("sync", lambda e, wsrc=wsrc, mb=mb, buf=buf: e.dma_start(
                            out=wA[:, buf, :, :], in_=wsrc.ap()[mb]), waits=w, inc=s_wt, inc_by=16)
                        for mo in range(4):
                            fo = mb * 4 + mo
                            par = fo % 2
                            last = None
                            for k in range(DJ):
                                w2 = []
                                if k == 0:
                                    w2 = [(s_wt, wA_tok[buf]), (s_dve, xn_tok)]
                                    if st12_evac[par]:
                                        w2.append((s_act, st12_evac[par]))
                                last = E("tensor", lambda e, mo=mo, k=k, buf=buf, par=par: e.matmul(
                                    ps[:, par, :], wA[:, buf, k, mo * 128:(mo + 1) * 128],
                                    xn[:, k, :], start=(k == 0), stop=(k == DJ - 1)),
                                    waits=w2, inc=(s_pe if k == DJ - 1 else None))
                            st12_evac[par] = E("scalar", lambda e, fo=fo, par=par, dstbuf=dstbuf,
                                               bslot=bslot: e.activation(
                                dstbuf[:, fo, :], ps[:, par, :], AF.Silu,
                                bias=abias[:, bslot, fo:fo + 1]),
                                waits=[(s_pe, last)], inc=s_act)
                        wA_free[buf] = s_pe.n
                xin_done = s_act.n
                self.barrier()
                zread = [0, 0]
                bread = [0, 0]
                scand = [0, 0]
                for fo in range(FJ):
                    par = fo % 2
                    buf = fo % 2
                    w = [(s_pe, wz_free[buf])] if wz_free[buf] else []
                    w.append((self.s_cv, self.cv_tok["Wz"]))
                    wz_tok[buf] = E("sync", lambda e, fo=fo, buf=buf: e.dma_start(
                        out=wz[:, buf, :, :], in_=self.cWz.ap()[fo]), waits=w, inc=s_wt, inc_by=16)
                    w = [(s_pe, wh_free[buf])] if wh_free[buf] else []
                    w.append((self.s_cv, self.cv_tok["Wh"]))
                    wh_tok[buf] = E("sync", lambda e, fo=fo, buf=buf: e.dma_start(
                        out=wh[:, buf, :, :], in_=self.cWh.ap()[fo]), waits=w, inc=s_wt, inc_by=16)
                    lastz = None
                    for k in range(FJ):
                        w2 = []
                        if k == 0:
                            w2 = [(s_wt, wz_tok[buf]), (s_act, xin_done)]
                            if zread[par]:
                                w2.append((s_act, zread[par]))
                        lastz = E("tensor", lambda e, k=k, buf=buf, par=par: e.matmul(
                            ps[:, par, :], wz[:, buf, k, :], x_in[:, k, :],
                            start=(k == 0), stop=(k == FJ - 1)),
                            waits=w2, inc=(s_pe if k == FJ - 1 else None))
                    lasth = None
                    for k in range(FJ):
                        w2 = []
                        if k == 0:
                            w2 = [(s_wt, wh_tok[buf])]
                            if bread[par]:
                                w2.append((s_dve, bread[par]))
                        lasth = E("tensor", lambda e, k=k, buf=buf, par=par: e.matmul(
                            ps[:, 2 + par, :], wh[:, buf, k, :], x_in[:, k, :],
                            start=(k == 0), stop=(k == FJ - 1)),
                            waits=w2, inc=(s_pe if k == FJ - 1 else None))
                    wz_free[buf] = wh_free[buf] = s_pe.n
                    wz1 = [(s_pe, lastz)]
                    if bread[par]:
                        wz1.append((s_dve, bread[par]))
                    E("scalar", lambda e, fo=fo, par=par: e.activation(
                        zbuf[:, par, :], ps[:, par, :], AF.Sigmoid, bias=abias[:, 2, fo:fo + 1]),
                        waits=wz1)
                    wa1 = []
                    if scand[par]:
                        wa1.append((s_dve, scand[par]))
                    zread[par] = E("scalar", lambda e, fo=fo, par=par: e.activation(
                        abuf[:, par, :], ps[:, par, :], AF.Sigmoid, bias=abias[:, 3, fo:fo + 1],
                        scale=-1.0), waits=wa1, inc=s_act)
                    bread[par] = E("vector", lambda e, fo=fo, par=par: e.scalar_tensor_tensor(
                        bbuf[:, par, :], ps[:, 2 + par, :], abias[:, 4, fo:fo + 1], zbuf[:, par, :],
                        ALU.add, ALU.mult), waits=[(s_pe, lasth), (s_act, zread[par])], inc=s_dve)
                    init = 0.0 if (p % 4) == 0 else self.carry.ap()[:, fo:fo + 1]
                    scand[par] = E("vector", lambda e, par=par, init=init: e.tensor_tensor_scan(
                        hbuf[:, par, :], abuf[:, par, :], bbuf[:, par, :], init, ALU.mult, ALU.add),
                        inc=s_dve)
                    E("vector", lambda e, fo=fo, par=par: e.tensor_tensor(
                        t1s[:, fo, :], t1s[:, fo, :], hbuf[:, par, :], ALU.add),
                      waits=[(s_act, xin_done)])
                    E("vector", lambda e, fo=fo, par=par: e.tensor_copy(
                        self.carry[:, fo:fo + 1], hbuf[:, par, 511:512]), inc=s_dve)
                t1_done = s_dve.n
                self.barrier()
                for j in range(4):
                    par = j % 2
                    w = [(s_act, st4_evac[par])] if st4_evac[par] else []
                    tok = E("scalar", lambda e, s=s, j=j, par=par, tok0=tok0: e.dma_start(
                        out=xtm[:, par, :], in_=self.x_tm[s, tok0 // 128 + j]),
                        waits=w, inc=s_x, inc_by=16)
                    last = None
                    for dj in range(DJ):
                        w2 = [(s_x, tok)] if dj == 0 else []
                        if j == 0 and dj == 0:
                            w2.append((s_dve, t1_done))
                        last = E("tensor", lambda e, par=par, dj=dj: e.transpose(
                            ps[:, 2 + 2 * par + dj // 4, (dj % 4) * 128:(dj % 4 + 1) * 128],
                            xtm[:, par, dj * 128:(dj + 1) * 128], self.ident_f[:]),
                            waits=w2, inc=(s_pe if dj == DJ - 1 else None))
                    st4_evac[par] = E("scalar", lambda e, par=par, j=j: e.copy(
                        xold[:, :, j * 128:(j + 1) * 128],
                        psf[:, (2 + 2 * par) * 512:(4 + 2 * par) * 512].rearrange(
                            "p (dj t) -> p dj t", t=128)),
                        waits=[(s_pe, last)], inc=s_act)
                xold_done = s_act.n
                for ob in range(8):
                    buf = ob % 2
                    par = ob % 2
                    w = [(s_pe, wz_free[buf])] if wz_free[buf] else []
                    w.append((self.s_cv, self.cv_tok["Wo"]))
                    wz_tok[buf] = E("sync", lambda e, ob=ob, buf=buf: e.dma_start(
                        out=wz[:, buf, :, :], in_=self.cWo.ap()[ob]), waits=w, inc=s_wt, inc_by=16)
                    last = None
                    for k in range(FJ):
                        w2 = []
                        if k == 0:
                            w2 = [(s_wt, wz_tok[buf]), (s_dve, t1_done)]
                            if st4_stored[par]:
                                w2.append((s_dve, st4_stored[par]))
                        last = E("tensor", lambda e, k=k, buf=buf, par=par: e.matmul(
                            ps[:, par, :], wz[:, buf, k, :], t1s[:, k, :],
                            start=(k == 0), stop=(k == FJ - 1)),
                            waits=w2, inc=(s_pe if k == FJ - 1 else None))
                    wz_free[buf] = s_pe.n
                    wv = [(s_pe, last), (s_act, xold_done)]
                    if st4_dma[par]:
                        wv.append((s_st, st4_dma[par]))
                    st4_stored[par] = E("vector", lambda e, ob=ob, par=par: e.scalar_tensor_tensor(
                        xnew[:, par, :], ps[:, par, :], aboT[:, ob:ob + 1], xold[:, ob, :],
                        ALU.add, ALU.add), waits=wv, inc=s_dve)
                    st4_dma[par] = E("scalar", lambda e, ob=ob, par=par, gtok0=gtok0: e.dma_start(
                        out=self.xfm.ap()[ob, :, gtok0:gtok0 + 512], in_=xnew[:, par, :]),
                        waits=[(s_dve, st4_stored[par])], inc=s_st, inc_by=16)
                self.barrier()

    def phase_F(self, tag, write_output):
        nc, E = self.nc, self.emit
        s_wt, s_x, s_st, s_pe, s_act, s_dve = (self.s_wt, self.s_x, self.s_st,
                                               self.s_pe, self.s_act, self.s_dve)
        with contextlib.ExitStack() as _es:
            xn = _es.enter_context(nc.sbuf_tensor(tag + "_xn", [128, DJ, 1024], BF16))
            acc = _es.enter_context(nc.sbuf_tensor(tag + "_acc", [128, DJ, 1024], F32))
            hf = _es.enter_context(nc.sbuf_tensor(tag + "_h", [128, FJ, 1024], BF16))
            wg = _es.enter_context(nc.sbuf_tensor(tag + "_wg", [128, 2, 8, 512], BF16))
            wu = _es.enter_context(nc.sbuf_tensor(tag + "_wu", [128, 2, 8, 512], BF16))
            wd = _es.enter_context(nc.sbuf_tensor(tag + "_wd", [128, 2, 32, 128], BF16))
            wrg = _es.enter_context(nc.sbuf_tensor(tag + "_wrg", [128, 8, 16], BF16))
            sqb = _es.enter_context(nc.sbuf_tensor(tag + "_sqb", [128, 2, 512], BF16))
            mstat = _es.enter_context(nc.sbuf_tensor(tag + "_mst", [128, 4, 512], F32))
            ssb = _es.enter_context(nc.sbuf_tensor(tag + "_s", [128, 1024], F32))
            cwtm = _es.enter_context(nc.sbuf_tensor(tag + "_cwt", [128, 18, 8], F32))
            cwall = _es.enter_context(nc.sbuf_tensor(tag + "_cwa", [128, 128], F32))
            cwT = _es.enter_context(nc.sbuf_tensor(tag + "_cwT", [64, 128], F32))
            cwrow = _es.enter_context(nc.sbuf_tensor(tag + "_cwr", [1, 1024], F32))
            cwb = _es.enter_context(nc.sbuf_tensor(tag + "_cwb", [128, 1024], F32))
            ftmp = _es.enter_context(nc.sbuf_tensor(tag + "_tmp", [128, 1024], F32))
            st1 = _es.enter_context(nc.sbuf_tensor(tag + "_st1", [128, 8, 1], F32))
            ytm = _es.enter_context(nc.sbuf_tensor(tag + "_ytm", [128, 2, 1024], F32))
            ps = _es.enter_context(nc.psum_tensor(tag + "_ps", [128, 8, 512], F32))
            psf = ps.ap().rearrange("p a b -> p (a b)")
            tkw = E("gpsimd", lambda e: e.dma_start(out=wrg[:], in_=self.wrgp.ap()),
                    inc=s_wt, inc_by=16)
            wg_tok = [0, 0]; wg_free = [0, 0]
            wu_tok = [0, 0]; wu_free = [0, 0]
            wd_tok = [0, 0]; wd_free = [0, 0]
            silu_tok = 0
            hmul_tok = 0
            dn_evac = 0
            bc_read = 0
            cwrow_pe = 0
            ytm_dma = [0, 0]
            ytm_evac = [0, 0]
            for c in range(4):
                gtok0 = c * 1024
                seq = c // 2
                self.barrier()
                xn_tok = self.ln_from_fm(gtok0, 1024, xn, psf, 6, 7,
                                         lambda tt: acc[:, :, tt * 512:(tt + 1) * 512],
                                         sqb, mstat)
                self.barrier()
                for ti in range(8):
                    for k in range(DJ):
                        w = [(s_dve, xn_tok), (s_wt, tkw)] if (k == 0 and ti == 0) else []
                        E("tensor", lambda e, ti=ti, k=k: e.matmul(
                            ps[:, 4, ti * 16:ti * 16 + 16], xn[:, k, ti * 128:(ti + 1) * 128],
                            wrg[:, k, :], start=(ti == 0 and k == 0), stop=(k == DJ - 1)),
                            waits=w, inc=(s_pe if (k == DJ - 1 and ti == 7) else None))
                ract = E("scalar", lambda e: e.copy(
                    cwtm[:, 0:16, :], ps[:, 4, 0:128].rearrange("p (a b) -> p a b", b=8)),
                    waits=[(s_pe, s_pe.n)], inc=s_act)
                cw_fin = 0
                tke2 = 0
                dbg_this = self.debug_xfm and tag == "f1" and c == 0
                for ti in range(8):
                    lr = cwtm.ap()[:, 2 * ti, :]
                    lg = cwtm.ap()[:, 2 * ti + 1, :]
                    er = cwtm.ap()[:, 16, :]
                    tprod = cwtm.ap()[:, 17, :]
                    w = [(s_act, ract)] if ti == 0 else [(s_act, tke2)]
                    E("vector", lambda e, lg=lg: e.reduce_max(st1[:, 1, :], lg, axis=AXX,
                                                              negate=True), waits=w)
                    E("vector", lambda e, lr=lr: e.reduce_max(st1[:, 0, :], lr, axis=AXX,
                                                              negate=True))
                    self.dve_spacer()
                    E("vector", lambda e: e.tensor_scalar_mul(st1[:, 2, :], st1[:, 1, :], 10.0))
                    self.dve_spacer()
                    tk = self.dve_spacer(inc=s_dve)
                    E("scalar", lambda e, lr=lr, er=er: e.activation(
                        er, lr, AF.Exp, bias=st1[:, 0, :], accum_out=st1[:, 3, :]),
                        waits=[(s_dve, tk)])
                    tke2 = E("scalar", lambda e, lg=lg: e.activation(
                        lg, lg, AF.Exp, bias=st1[:, 2, :], scale=10.0, accum_out=st1[:, 4, :]),
                        inc=s_act)
                    if dbg_this and ti == 0:
                        E("vector", lambda e, er=er: e.tensor_copy(self.dbgf2[:, 0:8], er),
                          waits=[(s_act, tke2)])
                        E("vector", lambda e, lg=lg: e.tensor_copy(self.dbgf2[:, 8:16], lg))
                        E("vector", lambda e, lr=lr: e.tensor_copy(self.dbgf2[:, 16:24], lr))
                    E("vector", lambda e, lg=lg, er=er, tprod=tprod: e.tensor_mul(tprod, er, lg),
                      waits=[(s_act, tke2)])
                    E("vector", lambda e: e.tensor_mul(st1[:, 6, :], st1[:, 3, :], st1[:, 4, :]))
                    self.dve_spacer()
                    E("vector", lambda e, tprod=tprod: e.reduce_sum(st1[:, 5, :], tprod, axis=AXX))
                    self.dve_spacer()
                    self.dve_spacer()
                    E("vector", lambda e: e.scalar_tensor_tensor(
                        st1[:, 6, :], st1[:, 6, :], 1e-9, st1[:, 5, :], ALU.mult, ALU.add))
                    self.dve_spacer()
                    self.dve_spacer()
                    E("vector", lambda e: e.reciprocal(st1[:, 7, :], st1[:, 6, :]))
                    self.dve_spacer()
                    self.dve_spacer()
                    if dbg_this and ti == 0:
                        E("vector", lambda e: e.tensor_copy(
                            self.dbgf2[:, 24:32], st1[:, :, 0].transpose([0, 1]) if False else st1.ap().rearrange("p a b -> p (a b)")))
                        E("vector", lambda e, tprod=tprod: e.tensor_copy(self.dbgf2[:, 32:40], tprod))
                    cw_fin = E("vector", lambda e, ti=ti, tprod=tprod: e.tensor_scalar_mul(
                        cwall[:, ti * 8:(ti + 1) * 8], tprod, st1[:, 7, :]), inc=s_dve)
                    if dbg_this and ti == 0:
                        tdd = E("vector", lambda e: e.tensor_copy(
                            self.dbgf2[:, 40:48], cwall[:, 0:8]), inc=s_dve)
                        E("sync", lambda e: e.dma_start(out=self.dbg_t0.ap(), in_=self.dbgf2[:]),
                          waits=[(s_dve, tdd)], inc=s_st, inc_by=16)
                if dbg_this:
                    tda = E("vector", lambda e: e.tensor_copy(self.dbgf[:, 0:128], cwall[:]),
                            waits=[(s_dve, cw_fin)], inc=s_dve)
                    E("sync", lambda e: e.dma_start(out=self.dbg_s.ap()[:, 0:128], in_=self.dbgf[:, 0:128]),
                      waits=[(s_dve, tda)], inc=s_st, inc_by=16)
                wt2 = [(s_dve, cw_fin)]
                if cwrow_pe:
                    wt2.append((s_pe, cwrow_pe))
                tkt = E("tensor", lambda e: e.transpose(ps[:, 5, 0:128], cwall[:], self.ident_f[:]),
                        waits=wt2, inc=s_pe)
                cw_done = E("scalar", lambda e: e.copy(cwT[:], ps[0:64, 5, 0:128]),
                            waits=[(s_pe, tkt)], inc=s_act)
                self.barrier()
                for ex in range(NE):
                    wv = [(s_act, cw_done)]
                    if cwrow_pe:
                        wv.append((s_pe, cwrow_pe))
                    tkr = None
                    for ti in range(8):
                        tkr = E("sync", lambda e, ex=ex, ti=ti: e.dma_start(
                            out=cwrow[:, ti * 128:(ti + 1) * 128],
                            in_=cwT[ti * 8 + ex:ti * 8 + ex + 1, :]),
                            waits=(wv if ti == 0 else []), inc=s_x, inc_by=16)
                    last = None
                    for q in range(4):
                        w = []
                        if q == 0:
                            w = [(s_x, tkr)]
                            if bc_read:
                                w.append((s_dve, bc_read))
                        last = E("tensor", lambda e, q=q: e.matmul(
                            psf[:, 4 * 512 + q * 256:4 * 512 + (q + 1) * 256],
                            self.ones_f[0:1, :], cwrow[:, q * 256:(q + 1) * 256],
                            start=True, stop=True), waits=w, inc=(s_pe if q == 3 else None))
                    cwrow_pe = last
                    bc_read = E("vector", lambda e: e.tensor_copy(
                        cwb[:], psf[:, 4 * 512:6 * 512]), waits=[(s_pe, last)], inc=s_dve)
                    if self.debug_xfm and tag == "f1" and c == 0 and ex == 0:
                        E("sync", lambda e: e.dma_start(out=self.dbg_cwT.ap(), in_=cwT[:]),
                          waits=[(s_act, cw_done)], inc=s_st, inc_by=16)
                        E("sync", lambda e: e.dma_start(out=self.dbg_cwb.ap(), in_=cwb[:]),
                          waits=[(s_dve, bc_read)], inc=s_st, inc_by=16)
                        tdx = E("vector", lambda e: e.tensor_copy(self.dbgf[:], xn[:, 0, :]),
                                waits=[(s_dve, xn_tok)], inc=s_dve)
                        E("sync", lambda e: e.dma_start(out=self.dbg_xn.ap(), in_=self.dbgf[:]),
                          waits=[(s_dve, tdx)], inc=s_st, inc_by=16)
                    for fo in range(FJ):
                        mb, mo = fo // 4, fo % 4
                        buf = mb % 2
                        if mo == 0:
                            w = [(s_pe, wg_free[buf])] if wg_free[buf] else []
                            w.append((self.s_cv, self.cv_tok[f"wg{ex}"]))
                            wg_tok[buf] = E("sync", lambda e, ex=ex, mb=mb, buf=buf: e.dma_start(
                                out=wg[:, buf, :, :], in_=self.cwg.ap()[ex, mb]),
                                waits=w, inc=s_wt, inc_by=16)
                            w = [(s_pe, wu_free[buf])] if wu_free[buf] else []
                            w.append((self.s_cv, self.cv_tok[f"wu{ex}"]))
                            wu_tok[buf] = E("sync", lambda e, ex=ex, mb=mb, buf=buf: e.dma_start(
                                out=wu[:, buf, :, :], in_=self.cwu.ap()[ex, mb]),
                                waits=w, inc=s_wt, inc_by=16)
                        lastg = None
                        for k in range(DJ):
                            for tt in range(2):
                                w2 = []
                                if k == 0 and tt == 0:
                                    w2 = [(s_wt, wg_tok[buf]), (s_dve, xn_tok)]
                                    if silu_tok:
                                        w2.append((s_act, silu_tok))
                                lastg = E("tensor", lambda e, mo=mo, k=k, tt=tt, buf=buf: e.matmul(
                                    ps[:, tt, :], wg[:, buf, k, mo * 128:(mo + 1) * 128],
                                    xn[:, k, tt * 512:(tt + 1) * 512],
                                    start=(k == 0), stop=(k == DJ - 1)),
                                    waits=w2, inc=(s_pe if (k == DJ - 1 and tt == 1) else None))
                        lastu = None
                        for k in range(DJ):
                            for tt in range(2):
                                w2 = []
                                if k == 0 and tt == 0:
                                    w2 = [(s_wt, wu_tok[buf])]
                                    if hmul_tok:
                                        w2.append((s_dve, hmul_tok))
                                lastu = E("tensor", lambda e, mo=mo, k=k, tt=tt, buf=buf: e.matmul(
                                    ps[:, 2 + tt, :], wu[:, buf, k, mo * 128:(mo + 1) * 128],
                                    xn[:, k, tt * 512:(tt + 1) * 512],
                                    start=(k == 0), stop=(k == DJ - 1)),
                                    waits=w2, inc=(s_pe if (k == DJ - 1 and tt == 1) else None))
                        if mo == 3:
                            wg_free[buf] = wu_free[buf] = s_pe.n
                        wsi = [(s_pe, lastg)]
                        if hmul_tok:
                            wsi.append((s_dve, hmul_tok))
                        silu_tok = E("scalar", lambda e: e.activation(
                            ssb[:], psf[:, 0:1024], AF.Silu), waits=wsi, inc=s_act)
                        hmul_tok = E("vector", lambda e, fo=fo: e.tensor_tensor(
                            hf[:, fo, :], ssb[:], psf[:, 1024:2048], ALU.mult),
                            waits=[(s_act, silu_tok), (s_pe, lastu)], inc=s_dve)
                    h_done = hmul_tok
                    if self.debug_xfm and tag == "f1" and c == 0 and ex == 0:
                        tdh = E("vector", lambda e: e.tensor_copy(self.dbgf[:], hf[:, 0, :]),
                                waits=[(s_dve, h_done)], inc=s_dve)
                        E("sync", lambda e: e.dma_start(out=self.dbg_h.ap(), in_=self.dbgf[:]),
                          waits=[(s_dve, tdh)], inc=s_st, inc_by=16)
                    for o in range(8):
                        buf = o % 2
                        w = [(s_pe, wd_free[buf])] if wd_free[buf] else []
                        w.append((self.s_cv, self.cv_tok[f"wd{ex}"]))
                        wd_tok[buf] = E("sync", lambda e, ex=ex, o=o, buf=buf: e.dma_start(
                            out=wd[:, buf, :, :], in_=self.cwd.ap()[ex, o]),
                            waits=w, inc=s_wt, inc_by=16)
                        last = None
                        for k in range(FJ):
                            for tt in range(2):
                                w2 = []
                                if k == 0 and tt == 0:
                                    w2 = [(s_wt, wd_tok[buf]), (s_dve, h_done)]
                                    if dn_evac:
                                        w2.append((s_dve, dn_evac))
                                last = E("tensor", lambda e, k=k, tt=tt, buf=buf: e.matmul(
                                    ps[:, 6 + tt, :], wd[:, buf, k, :],
                                    hf[:, k, tt * 512:(tt + 1) * 512],
                                    start=(k == 0), stop=(k == FJ - 1)),
                                    waits=w2, inc=(s_pe if (k == FJ - 1 and tt == 1) else None))
                        wd_free[buf] = s_pe.n
                        E("vector", lambda e: e.tensor_mul(ftmp[:], psf[:, 3072:4096], cwb[:]),
                          waits=[(s_pe, last), (s_dve, bc_read)])
                        dn_evac = E("vector", lambda e, o=o: e.tensor_tensor(
                            acc[:, o, :], acc[:, o, :], ftmp[:], ALU.add), inc=s_dve)
                if not write_output:
                    for dj in range(DJ):
                        E("scalar", lambda e, dj=dj, gtok0=gtok0: e.dma_start(
                            out=self.xfm.ap()[dj, :, gtok0:gtok0 + 1024], in_=acc[:, dj, :]),
                          waits=[(s_dve, dn_evac)], inc=s_st, inc_by=16)
                else:
                    for j in range(8):
                        par = j % 2
                        last = None
                        for dj in range(DJ):
                            w2 = [(s_dve, dn_evac)] if dj == 0 else []
                            if dj == 0 and ytm_evac[par]:
                                w2.append((s_act, ytm_evac[par]))
                            last = E("tensor", lambda e, j=j, dj=dj, par=par: e.transpose(
                                ps[:, 2 * par + dj // 4, (dj % 4) * 128:(dj % 4 + 1) * 128],
                                acc[:, dj, j * 128:(j + 1) * 128], self.ident_f[:]),
                                waits=w2, inc=(s_pe if dj == DJ - 1 else None))
                        wv = [(s_pe, last)]
                        if ytm_dma[par]:
                            wv.append((s_st, ytm_dma[par]))
                        ytm_evac[par] = E("scalar", lambda e, par=par: e.copy(
                            ytm[:, par, :].rearrange("p (dj t) -> p dj t", t=128),
                            psf[:, 2 * par * 512:(2 * par + 2) * 512].rearrange(
                                "p (dj t) -> p dj t", t=128)),
                            waits=wv, inc=s_act)
                        tb = (gtok0 % T) // 128 + j
                        ytm_dma[par] = E("scalar", lambda e, seq=seq, tb=tb, par=par: e.dma_start(
                            out=self.y_tm[seq, tb], in_=ytm[:, par, :]),
                            waits=[(s_act, ytm_evac[par])], inc=s_st, inc_by=16)
                self.barrier()

    def phase_C(self):
        nc, E = self.nc, self.emit
        s_wt, s_x, s_st, s_pe, s_act, s_dve = (self.s_wt, self.s_x, self.s_st,
                                               self.s_pe, self.s_act, self.s_dve)
        with contextlib.ExitStack() as _es:
            xn = _es.enter_context(nc.sbuf_tensor("c_xn", [128, DJ, T], BF16))
            qsb = _es.enter_context(nc.sbuf_tensor("c_q", [128, 8, T], BF16))
            kb = _es.enter_context(nc.sbuf_tensor("c_kb", [128, 2, T], BF16))
            vb = _es.enter_context(nc.sbuf_tensor("c_vb", [128, 2, T], BF16))
            vt = _es.enter_context(nc.sbuf_tensor("c_vt", [128, 2, 16, 128], BF16))
            klat = _es.enter_context(nc.sbuf_tensor("c_klat", [64, T], BF16))
            vlat = _es.enter_context(nc.sbuf_tensor("c_vlat", [64, T], BF16))
            Ssb = _es.enter_context(nc.sbuf_tensor("c_S", [128, 2, T], F32))
            Pb = _es.enter_context(nc.sbuf_tensor("c_P", [128, 2, T], BF16))
            PT = _es.enter_context(nc.sbuf_tensor("c_PT", [128, 16, 128], BF16))
            yall = _es.enter_context(nc.sbuf_tensor("c_y", [128, 8, 512], BF16))
            woa = _es.enter_context(nc.sbuf_tensor("c_woa", [128, 8, 8, 128], BF16))
            wqs = _es.enter_context(nc.sbuf_tensor("c_wq", [128, 2, 8, 128], BF16))
            wkd = _es.enter_context(nc.sbuf_tensor("c_wkd", [128, 8, 64], BF16))
            wvd = _es.enter_context(nc.sbuf_tensor("c_wvd", [128, 8, 64], BF16))
            kup = _es.enter_context(nc.sbuf_tensor("c_kup", [64, 128], BF16))
            vup = _es.enter_context(nc.sbuf_tensor("c_vup", [64, 128], BF16))
            masks = _es.enter_context(nc.sbuf_tensor("c_mask", [128, 4, 512], F32))
            sqb = _es.enter_context(nc.sbuf_tensor("c_sqb", [128, 2, 512], BF16))
            mstat = _es.enter_context(nc.sbuf_tensor("c_mst", [128, 4, 512], F32))
            xold = _es.enter_context(nc.sbuf_tensor("c_xold", [128, DJ, 512], F32))
            xnw = _es.enter_context(nc.sbuf_tensor("c_xnw", [128, 2, 512], F32))
            st1 = _es.enter_context(nc.sbuf_tensor("c_st1", [128, 4, 1], F32))
            ps = _es.enter_context(nc.psum_tensor("c_ps", [128, 6, 512], F32))
            psb = _es.enter_context(nc.psum_tensor("c_psb", [128, 2, 1024], BF16))
            psf = ps.ap().rearrange("p a b -> p (a b)")
            tk_m = E("sync", lambda e: e.dma_start(out=masks[:], in_=self.maskh.ap().transpose(
                [1, 0, 2])), inc=s_x, inc_by=16)
            tk_woa = E("sync", lambda e: e.dma_start(out=woa[:], in_=self.cwoa.ap().transpose(
                [1, 0, 2, 3])), waits=[(self.s_cv, self.cv_tok["woa"])], inc=s_wt, inc_by=16)
            tk_wkd = E("gpsimd", lambda e: e.dma_start(out=wkd[:], in_=self.wkdp.ap()),
                       inc=s_wt, inc_by=16)
            tk_wvd = E("gpsimd", lambda e: e.dma_start(out=wvd[:], in_=self.wvdp.ap()),
                       inc=s_wt, inc_by=16)
            E("gpsimd", lambda e: e.dma_start(out=kup[0:32, :], in_=self.wkup.ap()),
              inc=s_wt, inc_by=16)
            tk_kup = E("gpsimd", lambda e: e.dma_start(out=kup[32:64, :], in_=self.wkup.ap()),
                       inc=s_wt, inc_by=16)
            E("gpsimd", lambda e: e.dma_start(out=vup[0:32, :], in_=self.wvup.ap()),
              inc=s_wt, inc_by=16)
            tk_vup = E("gpsimd", lambda e: e.dma_start(out=vup[32:64, :], in_=self.wvup.ap()),
                       inc=s_wt, inc_by=16)
            wq_tok = [0, 0]; wq_free = [0, 0]
            for seq in range(NSEQ):
                self.barrier()
                xn_tok = self.ln_from_fm(seq * T, T, xn, psf, 4, 5, lambda tt: xold[:], sqb, mstat, xf_war=True)
                self.barrier()
                qev = 0
                for hb in range(8):
                    buf = hb % 2
                    w = [(s_pe, wq_free[buf])] if wq_free[buf] else []
                    w.append((self.s_cv, self.cv_tok["wq"]))
                    wq_tok[buf] = E("sync", lambda e, hb=hb, buf=buf: e.dma_start(
                        out=wqs[:, buf, :, :], in_=self.cwq.ap()[hb]), waits=w, inc=s_wt, inc_by=16)
                    last = None
                    for k in range(DJ):
                        for tt in range(4):
                            w2 = []
                            if k == 0 and tt == 0:
                                w2 = [(s_wt, wq_tok[buf]), (s_dve, xn_tok)]
                                if qev:
                                    w2.append((s_act, qev))
                            last = E("tensor", lambda e, k=k, tt=tt, buf=buf: e.matmul(
                                ps[:, tt, :], wqs[:, buf, k, :], xn[:, k, tt * 512:(tt + 1) * 512],
                                start=(k == 0), stop=(k == DJ - 1)),
                                waits=w2, inc=(s_pe if (k == DJ - 1 and tt == 3) else None))
                    wq_free[buf] = s_pe.n
                    qev = E("scalar", lambda e, hb=hb: e.copy(
                        qsb[:, hb, :], psf[:, 0:4 * 512]), waits=[(s_pe, last)], inc=s_act)
                klev = 0
                for wsrc, wtok, ldst in [(wkd, tk_wkd, klat), (wvd, tk_wvd, vlat)]:
                    for tt in range(4):
                        last = None
                        for k in range(DJ):
                            w2 = []
                            if k == 0:
                                w2 = [(s_wt, wtok), (s_dve, xn_tok)]
                                if klev:
                                    w2.append((s_act, klev))
                            last = E("tensor", lambda e, k=k, tt=tt, wsrc=wsrc: e.matmul(
                                ps[0:64, 4, :], wsrc[:, k, :], xn[:, k, tt * 512:(tt + 1) * 512],
                                start=(k == 0), stop=(k == DJ - 1)),
                                waits=w2, inc=(s_pe if k == DJ - 1 else None))
                        klev = E("scalar", lambda e, tt=tt, ldst=ldst: e.copy(
                            ldst[:, tt * 512:(tt + 1) * 512], ps[0:64, 4, :]),
                            waits=[(s_pe, last)], inc=s_act)
                upev = 0
                for usrc, utok, latb, dstb in [(kup, tk_kup, klat, kb),
                                               (vup, tk_vup, vlat, vb)]:
                    for g in range(2):
                        for tt in range(4):
                            w2 = [(s_wt, utok), (s_act, klev)]
                            if upev:
                                w2.append((s_act, upev))
                            last = E("tensor", lambda e, tt=tt, usrc=usrc, latb=latb, g=g: e.matmul(
                                ps[:, 5, :], usrc[g * 32:(g + 1) * 32, :],
                                latb[g * 32:(g + 1) * 32, tt * 512:(tt + 1) * 512],
                                start=True, stop=True), waits=w2, inc=s_pe)
                            upev = E("scalar", lambda e, g=g, tt=tt, dstb=dstb: e.copy(
                                dstb[:, g, tt * 512:(tt + 1) * 512], ps[:, 5, :]),
                                waits=[(s_pe, last)], inc=s_act)
                vtev = 0
                for g in range(2):
                    for hw in range(2):
                        last = None
                        for j2 in range(hw * 8, hw * 8 + 8):
                            w2 = []
                            if j2 % 8 == 0:
                                w2 = [(s_act, upev)]
                                if vtev:
                                    w2.append((s_act, vtev))
                            last = E("tensor", lambda e, g=g, j2=j2, hw=hw: e.transpose(
                                psb[:, hw, (j2 % 8) * 128:(j2 % 8 + 1) * 128],
                                vb[:, g, j2 * 128:(j2 + 1) * 128], self.ident_b[:]),
                                waits=w2, inc=(s_pe if j2 % 8 == 7 else None))
                        vtev = E("scalar", lambda e, g=g, hw=hw: e.copy(
                            vt[:, g, hw * 8:hw * 8 + 8, :],
                            psb[:, hw, :].rearrange("p (a b) -> p a b", b=128)),
                            waits=[(s_pe, last)], inc=s_act)
                self.barrier()
                sc_prev = 0
                pb_done = [0, 0]
                pt_evac = 0
                y_evac = 0
                yproj = 0
                prj_dve = 0
                xnw_dma = [0, 0]
                for G in range(4):
                    g0 = seq * T + G * 512
                    wv = [(s_dve, prj_dve)] if prj_dve else []
                    wv.append((s_st, self.store_gate))
                    xold_tok = E("scalar", lambda e, g0=g0: e.dma_start(
                        out=xold[:], in_=self.xfm_t[:, :, g0:g0 + 512]),
                        waits=wv, inc=s_x, inc_by=16)
                    for i in range(4 * G, 4 * G + 4):
                        jmax = i // 4
                        length = (jmax + 1) * 512
                        for h in range(8):
                            par = (i * 8 + h) % 2
                            g = h // 4
                            last = None
                            for j in range(jmax + 1):
                                w2 = []
                                if j == 0 and sc_prev:
                                    w2.append((s_act, sc_prev))
                                last = E("tensor", lambda e, i=i, h=h, j=j, g=g: e.matmul(
                                    ps[:, j, :], qsb[:, h, i * 128:(i + 1) * 128],
                                    kb[:, g, j * 512:(j + 1) * 512], start=True, stop=True),
                                    waits=w2, inc=(s_pe if j == jmax else None))
                            wv2 = [(s_pe, last), (s_x, tk_m)]
                            if pb_done[par]:
                                wv2.append((s_dve, pb_done[par]))
                            sc_prev = E("scalar", lambda e, par=par, length=length: e.activation(
                                Ssb[:, par, 0:length], psf[:, 0:length], AF.Copy, scale=SCALE),
                                waits=wv2, inc=s_act)
                            E("vector", lambda e, par=par, i=i, jmax=jmax: e.tensor_tensor(
                                Ssb[:, par, jmax * 512:(jmax + 1) * 512],
                                Ssb[:, par, jmax * 512:(jmax + 1) * 512],
                                masks[:, i % 4, :], ALU.add), waits=[(s_act, sc_prev)])
                            self.dve_spacer()
                            E("vector", lambda e, par=par, length=length: e.reduce_max(
                                st1[:, 0, :], Ssb[:, par, 0:length], axis=AXX, negate=True))
                            self.dve_spacer()
                            tks = self.dve_spacer(inc=s_dve)
                            tke = E("scalar", lambda e, par=par, length=length: e.activation(
                                Ssb[:, par, 0:length], Ssb[:, par, 0:length], AF.Exp,
                                bias=st1[:, 0, :], accum_out=st1[:, 1, :]),
                                waits=[(s_dve, tks)], inc=s_act)
                            E("vector", lambda e: e.reciprocal(st1[:, 2, :], st1[:, 1, :]),
                              waits=[(s_act, tke)])
                            self.dve_spacer()
                            self.dve_spacer()
                            pb_done[par] = E("vector", lambda e, par=par, length=length:
                                             e.tensor_scalar_mul(
                                Pb[:, par, 0:length], Ssb[:, par, 0:length], st1[:, 2, :]),
                                inc=s_dve)
                            ntile = i + 1
                            for wv3 in range((ntile + 7) // 8):
                                lo = wv3 * 8
                                hi = min(lo + 8, ntile)
                                last = None
                                for j2 in range(lo, hi):
                                    w2 = []
                                    if j2 == lo:
                                        w2 = [(s_dve, pb_done[par])]
                                        if pt_evac:
                                            w2.append((s_act, pt_evac))
                                    last = E("tensor", lambda e, par=par, j2=j2, wv3=wv3: e.transpose(
                                        psb[:, wv3 % 2, (j2 % 8) * 128:(j2 % 8 + 1) * 128],
                                        Pb[:, par, j2 * 128:(j2 + 1) * 128], self.ident_b[:]),
                                        waits=w2, inc=(s_pe if j2 == hi - 1 else None))
                                pt_evac = E("scalar", lambda e, lo=lo, hi=hi, wv3=wv3: e.copy(
                                    PT[:, lo:hi, :], psb[:, wv3 % 2, 0:(hi - lo) * 128].rearrange(
                                        "p (a b) -> p a b", b=128)), waits=[(s_pe, last)], inc=s_act)
                            last = None
                            for j2 in range(ntile):
                                w2 = []
                                if j2 == 0:
                                    w2 = [(s_act, pt_evac), (s_act, vtev)]
                                    if y_evac:
                                        w2.append((s_act, y_evac))
                                last = E("tensor", lambda e, g=g, j2=j2, ntile=ntile: e.matmul(
                                    ps[:, 4, 0:128], vt[:, g, j2, :], PT[:, j2, :],
                                    start=(j2 == 0), stop=(j2 == ntile - 1)),
                                    waits=w2, inc=(s_pe if j2 == ntile - 1 else None))
                            wv4 = [(s_pe, last)]
                            if yproj:
                                wv4.append((s_pe, yproj))
                            y_evac = E("scalar", lambda e, h=h, i=i: e.copy(
                                yall[:, h, (i % 4) * 128:(i % 4 + 1) * 128], ps[:, 4, 0:128]),
                                waits=wv4, inc=s_act)
                    y_done = y_evac
                    for o in range(8):
                        par = o % 2
                        last = None
                        for k in range(8):
                            w2 = []
                            if k == 0:
                                w2 = [(s_act, y_done), (s_wt, tk_woa)]
                                if prj_dve:
                                    w2.append((s_dve, prj_dve))
                            last = E("tensor", lambda e, o=o, k=k: e.matmul(
                                ps[:, 5, :], woa[:, o, k, :], yall[:, k, :],
                                start=(k == 0), stop=(k == 7)),
                                waits=w2, inc=(s_pe if k == 7 else None))
                        if o == 7:
                            yproj = last
                        wv5 = [(s_pe, last), (s_x, xold_tok)]
                        if xnw_dma[par]:
                            wv5.append((s_st, xnw_dma[par]))
                        prj_dve = E("vector", lambda e, o=o, par=par: e.tensor_tensor(
                            xnw[:, par, :], ps[:, 5, :], xold[:, o, :], ALU.add),
                            waits=wv5, inc=s_dve)
                        xnw_dma[par] = E("scalar", lambda e, o=o, par=par, g0=g0: e.dma_start(
                            out=self.xfm.ap()[o, :, g0:g0 + 512], in_=xnw[:, par, :]),
                            waits=[(s_dve, prj_dve)], inc=s_st, inc_by=16)
                self.barrier()

    def build(self):
        E = self.emit
        s_x, s_dve, s_st = self.s_x, self.s_dve, self.s_st
        t0 = E("sync", lambda e: e.dma_start(out=self.ones_f[:], in_=self.onesh.ap()),
               inc=s_x, inc_by=16)
        t1 = E("sync", lambda e: e.dma_start(out=self.ident_f[:], in_=self.identh.ap()),
               inc=s_x, inc_by=16)
        E("vector", lambda e: e.tensor_copy(self.ones_b[:], self.ones_f[:]), waits=[(s_x, t0)])
        E("vector", lambda e: e.tensor_copy(self.ident_b[:], self.ident_f[:]), waits=[(s_x, t1)])
        E("vector", lambda e: e.memset(self.eps[:], 1e-5), inc=s_dve)
        s_cv = self.s_cv
        conv = [("W1", self.W1p.ap(), self.cW1.ap()), ("W2", self.W2p.ap(), self.cW2.ap()),
                ("Wz", self.Wzp.ap(), self.cWz.ap()), ("Wh", self.Whp.ap(), self.cWh.ap()),
                ("Wo", self.Wop.ap(), self.cWo.ap()), ("wq", self.wqp.ap(), self.cwq.ap()),
                ("woa", self.woap.ap(), self.cwoa.ap())]
        for ex in range(NE):
            conv.append((f"wg{ex}", self.wgp.ap()[ex], self.cwg.ap()[ex]))
            conv.append((f"wu{ex}", self.wup.ap()[ex], self.cwu.ap()[ex]))
            conv.append((f"wd{ex}", self.wdp.ap()[ex], self.cwd.ap()[ex]))
        for name, sap, dap in conv:
            self.cv_tok[name] = self.emit("gpsimd", lambda e, sap=sap, dap=dap: e.dma_start(
                out=dap, in_=sap), inc=s_cv, inc_by=16)
        self.barrier()
        if "A" in self.phases:
            self.phase_A()
            self.store_gate = self.s_st.n
        if "F1" in self.phases:
            self.phase_F("f1", write_output=False)
            self.store_gate = self.s_st.n
        if "C" in self.phases:
            self.phase_C()
            self.store_gate = self.s_st.n
        if "F2" in self.phases:
            self.phase_F("f2", write_output=True)
        self.barrier()
        if self.debug_xfm:
            E("sync", lambda e: e.dma_start(out=self.dbg.ap(), in_=self.xfm.ap()),
              waits=[(s_st, s_st.n)], inc=s_st, inc_by=16)
        E("sync", lambda e: e.wait_ge(self.s_st.h, self.s_st.n))
        self.barrier()
        return self.nc


def pack_w(W, w):
    K, M = W.shape
    return np.ascontiguousarray(W.reshape(K // 128, 128, M // w, w).transpose(2, 1, 0, 3))


def pack_bias(b):
    return np.ascontiguousarray(b.reshape(-1, 128).T)


def make_masks():
    m = np.zeros((4, 128, 512), np.float32)
    for v in range(4):
        for q in range(128):
            m[v, q, v * 128 + q + 1:] = NEG
    return m


def host_inputs(full):
    g = {}
    g["W1p"] = pack_w(full["L0_attn_a_W1"], 512)
    g["W2p"] = pack_w(full["L0_attn_a_W2"], 512)
    g["Wzp"] = pack_w(full["L0_attn_a_Wz"], 128)
    g["Whp"] = pack_w(full["L0_attn_a_Wh"], 128)
    g["Wop"] = pack_w(full["L0_attn_a_Wo"], 128)
    g["b1h"] = pack_bias(full["L0_attn_a_b1"])
    g["b2h"] = pack_bias(full["L0_attn_a_b2"])
    g["bzh"] = pack_bias(full["L0_attn_a_bz"])
    g["bhh"] = pack_bias(full["L0_attn_a_bh"])
    g["boh"] = pack_bias(full["L0_attn_a_bo"])
    g["wqp"] = pack_w(full["L0_attn_b_wq"], 128)
    g["wkdp"] = pack_w(full["L0_attn_b_wk_down"], 64)[0]
    g["wvdp"] = pack_w(full["L0_attn_b_wv_down"], 64)[0]
    g["wkup"] = np.ascontiguousarray(full["L0_attn_b_wk_up"])
    g["wvup"] = np.ascontiguousarray(full["L0_attn_b_wv_up"])
    g["woap"] = pack_w(full["L0_attn_b_wo"], 128)
    g["wrgp"] = pack_w(np.concatenate([np.asarray(full["L0_ffn_w_router"]),
                                       np.asarray(full["L0_ffn_w_gating"])], axis=1), 16)[0]
    g["wgp"] = np.stack([pack_w(np.asarray(full["L0_ffn_w_gate"])[e], 512) for e in range(NE)])
    g["wup"] = np.stack([pack_w(np.asarray(full["L0_ffn_w_up"])[e], 512) for e in range(NE)])
    g["wdp"] = np.stack([pack_w(np.asarray(full["L0_ffn_w_down"])[e], 128) for e in range(NE)])
    g["onesh"] = np.ones((128, 128), np.float32)
    g["identh"] = np.eye(128, dtype=np.float32)
    g["maskh"] = make_masks()
    g = {k: np.ascontiguousarray(v, dtype=np.float32) for k, v in g.items()}
    x = np.asarray(full["L0_attn_a_x"])
    xs = [np.ascontiguousarray(x[2 * c:2 * c + 2]) for c in range(8)]
    return g, xs


def kernel(**inputs):
    b = Builder()
    nc = b.build()
    g, xs = host_inputs(inputs)
    in_maps = [dict(g, x=xs[c]) for c in range(8)]
    res = run_bass_kernel_spmd(nc, in_maps, core_ids=list(range(8)))
    return np.concatenate([r["y"] for r in res.results], axis=0)


# version 03
# speedup vs baseline: 1.0725x; 1.0725x over previous
import numpy as np
import contextlib
import concourse.bass as bass
import concourse.mybir as mybir
from concourse.bass_utils import run_bass_kernel_spmd

F32 = mybir.dt.float32
BF16 = mybir.dt.bfloat16
AF = mybir.ActivationFunctionType
ALU = mybir.AluOpType
AXX = mybir.AxisListType.X

D = 1024
DJ = 8
FJ = 32
T = 2048
NSEQ = 2
NTOK = NSEQ * T
NE = 8
NEG = -1.0e30
SCALE = 128.0 ** -0.5


class Sem:
    def __init__(self, nc, name):
        self.h = nc.alloc_semaphore(name)
        self.n = 0
        self.name = name


class Builder:
    ENGINES = ("sync", "gpsimd", "tensor", "scalar", "vector")

    def __init__(self, phases=("A", "F1", "C", "F2"), debug_xfm=False, n_barrier=True):
        self.phases = phases
        self.debug_xfm = debug_xfm
        nc = self.nc = bass.Bass("TRN2", target_bir_lowering=False)
        self.ops = {e: [] for e in self.ENGINES}
        self.hwm = {e: {} for e in self.ENGINES}

        dram = lambda n, s: nc.dram_tensor(n, s, F32, kind="ExternalInput")
        self.x_in = dram("x", [NSEQ, T, D])
        self.W1p = dram("W1p", [8, 128, 8, 512])
        self.W2p = dram("W2p", [8, 128, 8, 512])
        self.Wzp = dram("Wzp", [32, 128, 32, 128])
        self.Whp = dram("Whp", [32, 128, 32, 128])
        self.Wop = dram("Wop", [8, 128, 32, 128])
        self.b1h = dram("b1h", [128, 32])
        self.b2h = dram("b2h", [128, 32])
        self.bzh = dram("bzh", [128, 32])
        self.bhh = dram("bhh", [128, 32])
        self.boh = dram("boh", [128, 8])
        self.wqp = dram("wqp", [8, 128, 8, 128])
        self.wkdp = dram("wkdp", [128, 8, 64])
        self.wvdp = dram("wvdp", [128, 8, 64])
        self.wkup = dram("wkup", [32, 128])
        self.wvup = dram("wvup", [32, 128])
        self.woap = dram("woap", [8, 128, 8, 128])
        self.wrgp = dram("wrgp", [128, 8, 16])
        self.wgp = dram("wgp", [NE, 8, 128, 8, 512])
        self.wup = dram("wup", [NE, 8, 128, 8, 512])
        self.wdp = dram("wdp", [NE, 8, 128, 32, 128])
        self.onesh = dram("onesh", [128, 128])
        self.identh = dram("identh", [128, 128])
        self.maskh = dram("maskh", [4, 128, 512])

        self.y_out = nc.dram_tensor("y", [NSEQ, T, D], F32, kind="ExternalOutput")
        if debug_xfm:
            self.dbg = nc.dram_tensor("dbg_xfm", [DJ, 128, NTOK], F32, kind="ExternalOutput")
            self.dbg_cwT = nc.dram_tensor("dbg_cwT", [64, 128], F32, kind="ExternalOutput")
            self.dbg_cwb = nc.dram_tensor("dbg_cwb", [128, 1024], F32, kind="ExternalOutput")
            self.dbg_xn = nc.dram_tensor("dbg_xn", [128, 1024], F32, kind="ExternalOutput")
            self.dbg_h = nc.dram_tensor("dbg_h", [128, 1024], F32, kind="ExternalOutput")
            self.dbg_s = nc.dram_tensor("dbg_s", [128, 1024], F32, kind="ExternalOutput")
            self.dbg_t0 = nc.dram_tensor("dbg_t0", [128, 48], F32, kind="ExternalOutput")
            self.dbgf2 = nc.alloc_sbuf_tensor("dbgf2", [128, 48], F32)
            self.dbgf = nc.alloc_sbuf_tensor("dbgf", [128, 1024], F32)
        self.xfm = nc.dram_tensor("xfm", [DJ, 128, NTOK], F32, kind="Internal")
        cache = lambda n, s: nc.dram_tensor(n, s, BF16, kind="Internal")
        self.cW1 = cache("cW1", [8, 128, 8, 512])
        self.cW2 = cache("cW2", [8, 128, 8, 512])
        self.cWz = cache("cWz", [32, 128, 32, 128])
        self.cWh = cache("cWh", [32, 128, 32, 128])
        self.cWo = cache("cWo", [8, 128, 32, 128])
        self.cwg = cache("cwg", [NE, 8, 128, 8, 512])
        self.cwu = cache("cwu", [NE, 8, 128, 8, 512])
        self.cwd = cache("cwd", [NE, 8, 128, 32, 128])
        self.cv_tok = {}

        self.x_tm = self.x_in.ap().rearrange("s (tb p) d -> s tb p d", p=128)
        self.y_tm = self.y_out.ap().rearrange("s (tb p) d -> s tb p d", p=128)
        self.xfm_t = self.xfm.ap().transpose([1, 0, 2])

        self.s_wt = Sem(nc, "s_wt")
        self.s_cv = Sem(nc, "s_cv")
        self.s_x = Sem(nc, "s_x")
        self.s_st = Sem(nc, "s_st")
        self.s_pe = Sem(nc, "s_pe")
        self.s_act = Sem(nc, "s_act")
        self.s_dve = Sem(nc, "s_dve")

        self.ones_f = nc.alloc_sbuf_tensor("ones_f", [128, 128], F32)
        self.ones_b = nc.alloc_sbuf_tensor("ones_b", [128, 128], BF16)
        self.ident_f = nc.alloc_sbuf_tensor("ident_f", [128, 128], F32)
        self.ident_b = nc.alloc_sbuf_tensor("ident_b", [128, 128], BF16)
        self.spacer = nc.alloc_sbuf_tensor("spcr", [1, 8], F32)
        self.eps = nc.alloc_sbuf_tensor("epsv", [128, 1], F32)
        self.carry = nc.alloc_sbuf_tensor("carry", [128, 32], F32)

        self.store_gate = 0

    def emit(self, eng, fn, waits=(), inc=None, inc_by=1):
        self.ops[eng].append((fn, tuple(waits), inc, inc_by))
        if inc is not None:
            inc.n += inc_by
            return inc.n
        return None

    def barrier(self):
        ops = self.ops
        self.ops = {e: [] for e in self.ENGINES}
        if not any(ops.values()):
            return
        with self.nc.Block() as blk:
            for eng in self.ENGINES:
                lst = ops[eng]
                if not lst:
                    continue

                def body(E, lst=lst, eng=eng):
                    hw = self.hwm[eng]
                    for fn, waits, inc, k in lst:
                        for sem, val in waits:
                            if val <= 0 or val <= hw.get(sem.name, 0):
                                continue
                            E.wait_ge(sem.h, val)
                            hw[sem.name] = val
                        ins = fn(E)
                        if inc is not None:
                            ins.then_inc(inc.h, k)

                getattr(blk, eng)(body)

    def wstream(self, dst, fp32_ap, cache_ap, key, prime, waits):
        E = self.emit
        if prime:
            tok = E("gpsimd", lambda e, dst=dst, fp32_ap=fp32_ap: e.dma_start(
                out=dst, in_=fp32_ap), waits=waits, inc=self.s_wt, inc_by=16)
            stok = E("sync", lambda e, dst=dst, cache_ap=cache_ap: e.dma_start(
                out=cache_ap, in_=dst), waits=[(self.s_wt, tok)], inc=self.s_cv, inc_by=16)
            self.cache_tok[key] = stok
            return tok, stok
        tok = E("sync", lambda e, dst=dst, cache_ap=cache_ap: e.dma_start(
            out=dst, in_=cache_ap), waits=list(waits) + [(self.s_cv, self.cache_tok[key])],
            inc=self.s_wt, inc_by=16)
        return tok, 0

    def dve_spacer(self, inc=None):
        return self.emit("vector", lambda v: v.memset(self.spacer[:], 0.0), inc=inc)

    def ln_from_fm(self, gtok0, tlen, xn_dst, psf, ba, bb, load_dst, sqb, mstat,
                   xf_war=False):
        E = self.emit
        s_x, s_act, s_dve, s_pe, s_st = self.s_x, self.s_act, self.s_dve, self.s_pe, self.s_st
        psa = psf[:, ba * 512:(ba + 1) * 512]
        psb_ = psf[:, bb * 512:(bb + 1) * 512]
        sq_free = [0, 0]
        stats_free = 0
        tkc_prev = 0
        for tt in range(tlen // 512):
            g0 = gtok0 + tt * 512
            sl = slice(tt * 512, (tt + 1) * 512)
            dst = load_dst(tt)
            w = [(s_st, self.store_gate)]
            if xf_war and tkc_prev:
                w.append((s_act, tkc_prev))
            tok = E("scalar", lambda e, g0=g0, dst=dst: e.dma_start(
                out=dst, in_=self.xfm_t[:, :, g0:g0 + 512]), waits=w, inc=s_x, inc_by=16)
            tkc = E("scalar", lambda e, sl=sl, dst=dst: e.copy(xn_dst[:, :, sl], dst),
                    waits=[(s_x, tok)], inc=s_act)
            tkc_prev = tkc
            for dj in range(DJ):
                w2 = [(s_act, tkc)] if dj == 0 else []
                if dj == 0 and stats_free:
                    w2.append((s_dve, stats_free))
                E("tensor", lambda e, dj=dj, sl=sl, psa=psa: e.matmul(
                    psa, self.ones_b[:], xn_dst[:, dj, sl],
                    start=(dj == 0), stop=(dj == DJ - 1)), waits=w2)
            last = None
            for dj in range(DJ):
                w2 = [(s_act, tkc)]
                if sq_free[dj % 2]:
                    w2.append((s_pe, sq_free[dj % 2]))
                tk2 = E("scalar", lambda e, dj=dj, sl=sl: e.activation(
                    sqb[:, dj % 2, :], xn_dst[:, dj, sl], AF.Square), waits=w2, inc=s_act)
                last = E("tensor", lambda e, dj=dj, psb_=psb_: e.matmul(
                    psb_, self.ones_b[:], sqb[:, dj % 2, :],
                    start=(dj == 0), stop=(dj == DJ - 1)), waits=[(s_act, tk2)], inc=s_pe)
                sq_free[dj % 2] = last
            E("vector", lambda e, psa=psa: e.tensor_scalar_mul(mstat[:, 0, :], psa, 1.0 / D),
              waits=[(s_pe, last)])
            E("vector", lambda e, psb_=psb_: e.tensor_scalar_mul(mstat[:, 1, :], psb_, 1.0 / D))
            E("vector", lambda e: e.tensor_mul(mstat[:, 2, :], mstat[:, 0, :], mstat[:, 0, :]))
            tk3 = E("vector", lambda e: e.tensor_sub(mstat[:, 1, :], mstat[:, 1, :], mstat[:, 2, :]),
                    inc=s_dve)
            stats_free = tk3
            tk4 = E("scalar", lambda e: e.activation(mstat[:, 3, :], mstat[:, 1, :], AF.Sqrt,
                                                     bias=self.eps[:]), waits=[(s_dve, tk3)], inc=s_act)
            E("vector", lambda e: e.reciprocal(mstat[:, 1, :], mstat[:, 3, :]), waits=[(s_act, tk4)])
            self.dve_spacer()
            for dj in range(DJ):
                E("vector", lambda e, dj=dj, sl=sl: e.tensor_sub(
                    mstat[:, 2, :], xn_dst[:, dj, sl], mstat[:, 0, :]))
                E("vector", lambda e, dj=dj, sl=sl: e.tensor_mul(
                    xn_dst[:, dj, sl], mstat[:, 2, :], mstat[:, 1, :]),
                  inc=(s_dve if dj == DJ - 1 else None))
        return s_dve.n


    def phase_A(self):
        nc, E = self.nc, self.emit
        s_wt, s_x, s_st, s_pe, s_act, s_dve = (self.s_wt, self.s_x, self.s_st,
                                               self.s_pe, self.s_act, self.s_dve)
        with contextlib.ExitStack() as _es:
            xn = _es.enter_context(nc.sbuf_tensor("a_xn", [128, DJ, 512], BF16))
            x_in = _es.enter_context(nc.sbuf_tensor("a_xin", [128, FJ, 512], BF16))
            t1s = _es.enter_context(nc.sbuf_tensor("a_t1", [128, FJ, 512], BF16))
            zbuf = _es.enter_context(nc.sbuf_tensor("a_zb", [128, 2, 512], F32))
            abuf = _es.enter_context(nc.sbuf_tensor("a_ab", [128, 2, 512], F32))
            bbuf = _es.enter_context(nc.sbuf_tensor("a_bb", [128, 2, 512], F32))
            hbuf = _es.enter_context(nc.sbuf_tensor("a_hb", [128, 2, 512], F32))
            wA = _es.enter_context(nc.sbuf_tensor("a_wA", [128, 2, 8, 512], BF16))
            wz = _es.enter_context(nc.sbuf_tensor("a_wz", [128, 2, 32, 128], BF16))
            wh = _es.enter_context(nc.sbuf_tensor("a_wh", [128, 2, 32, 128], BF16))
            xtm = _es.enter_context(nc.sbuf_tensor("a_xtm", [128, 2, 1024], F32))
            sqb = _es.enter_context(nc.sbuf_tensor("a_sqb", [128, 2, 512], BF16))
            mstat = _es.enter_context(nc.sbuf_tensor("a_mst", [128, 6, 512], F32))
            xold = _es.enter_context(nc.sbuf_tensor("a_xold", [128, DJ, 512], F32))
            xnew = _es.enter_context(nc.sbuf_tensor("a_xnew", [128, 2, 512], F32))
            abias = _es.enter_context(nc.sbuf_tensor("a_bias", [128, 5, 32], F32))
            aboT = _es.enter_context(nc.sbuf_tensor("a_bo", [128, 8], F32))
            ps = _es.enter_context(nc.psum_tensor("a_ps", [128, 8, 512], F32))
            psf = ps.ap().rearrange("p a b -> p (a b)")
            for src, di in [(self.b1h, 0), (self.b2h, 1), (self.bzh, 2), (self.bhh, 4)]:
                E("sync", lambda e, src=src, di=di: e.dma_start(out=abias[:, di, :], in_=src.ap()),
                  inc=s_x, inc_by=16)
            tkb = E("sync", lambda e: e.dma_start(out=aboT[:], in_=self.boh.ap()), inc=s_x, inc_by=16)
            E("vector", lambda e: e.tensor_scalar_mul(abias[:, 3, :], abias[:, 2, :], -1.0),
              waits=[(s_x, s_x.n)], inc=s_dve)

            wA_tok = [0, 0]; wA_free = [0, 0]; wA_stok = [0, 0]
            wz_tok = [0, 0]; wz_free = [0, 0]; wz_stok = [0, 0]
            wh_tok = [0, 0]; wh_free = [0, 0]; wh_stok = [0, 0]
            ln_evac = [0, 0]
            st12_evac = [0, 0]
            st4_evac = [0, 0]
            st4_stored = [0, 0]
            st4_dma = [0, 0]
            stats_read = 0
            sq_free = [0, 0]

            for p in range(8):
                s = p // 4
                tok0 = (p % 4) * 512
                gtok0 = s * T + tok0
                self.barrier()
                xb = x_in.ap()[:, 0:8, :]
                for j in range(4):
                    par = j % 2
                    w = [(s_act, ln_evac[par])] if ln_evac[par] else []
                    tok = E("scalar", lambda e, s=s, j=j, par=par, tok0=tok0: e.dma_start(
                        out=xtm[:, par, :], in_=self.x_tm[s, tok0 // 128 + j]),
                        waits=w, inc=s_x, inc_by=16)
                    last = None
                    for dj in range(DJ):
                        w2 = [(s_x, tok)] if dj == 0 else []
                        last = E("tensor", lambda e, par=par, dj=dj: e.transpose(
                            ps[:, 2 * par + dj // 4, (dj % 4) * 128:(dj % 4 + 1) * 128],
                            xtm[:, par, dj * 128:(dj + 1) * 128], self.ident_f[:]),
                            waits=w2, inc=(s_pe if dj == DJ - 1 else None))
                    ln_evac[par] = E("scalar", lambda e, par=par, j=j, xb=xb: e.copy(
                        xb[:, :, j * 128:(j + 1) * 128],
                        psf[:, 2 * par * 512:(2 * par + 2) * 512].rearrange(
                            "p (dj t) -> p dj t", t=128)),
                        waits=[(s_pe, last)], inc=s_act)
                xb_done = s_act.n
                for dj in range(DJ):
                    w2 = [(s_act, xb_done)] if dj == 0 else []
                    if dj == 0 and stats_read:
                        w2.append((s_dve, stats_read))
                    E("tensor", lambda e, dj=dj, xb=xb: e.matmul(
                        ps[:, 6, :], self.ones_b[:], xb[:, dj, :],
                        start=(dj == 0), stop=(dj == DJ - 1)), waits=w2)
                last = None
                for dj in range(DJ):
                    w2 = [(s_act, xb_done)]
                    if sq_free[dj % 2]:
                        w2.append((s_pe, sq_free[dj % 2]))
                    tk2 = E("scalar", lambda e, dj=dj, xb=xb: e.activation(
                        sqb[:, dj % 2, :], xb[:, dj, :], AF.Square), waits=w2, inc=s_act)
                    last = E("tensor", lambda e, dj=dj: e.matmul(
                        ps[:, 7, :], self.ones_b[:], sqb[:, dj % 2, :],
                        start=(dj == 0), stop=(dj == DJ - 1)), waits=[(s_act, tk2)], inc=s_pe)
                    sq_free[dj % 2] = last
                E("vector", lambda e: e.tensor_scalar_mul(mstat[:, 0, :], ps[:, 6, :], 1.0 / D),
                  waits=[(s_pe, last)])
                E("vector", lambda e: e.tensor_scalar_mul(mstat[:, 1, :], ps[:, 7, :], 1.0 / D))
                E("vector", lambda e: e.tensor_mul(mstat[:, 2, :], mstat[:, 0, :], mstat[:, 0, :]))
                tk3 = E("vector", lambda e: e.tensor_sub(mstat[:, 3, :], mstat[:, 1, :],
                                                         mstat[:, 2, :]), inc=s_dve)
                stats_read = tk3
                tk4 = E("scalar", lambda e: e.activation(mstat[:, 4, :], mstat[:, 3, :], AF.Sqrt,
                                                         bias=self.eps[:]), waits=[(s_dve, tk3)], inc=s_act)
                E("vector", lambda e: e.reciprocal(mstat[:, 5, :], mstat[:, 4, :]),
                  waits=[(s_act, tk4)])
                self.dve_spacer()
                for dj in range(DJ):
                    E("vector", lambda e, dj=dj, xb=xb: e.tensor_sub(
                        mstat[:, 2, :], xb[:, dj, :], mstat[:, 0, :]))
                    E("vector", lambda e, dj=dj: e.tensor_mul(
                        xn[:, dj, :], mstat[:, 2, :], mstat[:, 5, :]),
                      inc=(s_dve if dj == DJ - 1 else None))
                xn_tok = s_dve.n
                self.barrier()
                for wsrc, wsrcp, dstbuf, bslot in [(self.cW1, self.W1p, x_in, 0), (self.cW2, self.W2p, t1s, 1)]:
                    for mb in range(8):
                        buf = mb % 2
                        w = [(s_pe, wA_free[buf])] if wA_free[buf] else []
                        if wA_stok[buf]:
                            w.append((self.s_cv, wA_stok[buf]))
                        wA_tok[buf], wA_stok[buf] = self.wstream(
                            wA[:, buf, :, :], wsrcp.ap()[mb], wsrc.ap()[mb],
                            ("A", bslot, mb), p == 0, w)
                        for mo in range(4):
                            fo = mb * 4 + mo
                            par = fo % 2
                            last = None
                            for k in range(DJ):
                                w2 = []
                                if k == 0:
                                    w2 = [(s_wt, wA_tok[buf]), (s_dve, xn_tok)]
                                    if st12_evac[par]:
                                        w2.append((s_act, st12_evac[par]))
                                last = E("tensor", lambda e, mo=mo, k=k, buf=buf, par=par: e.matmul(
                                    ps[:, par, :], wA[:, buf, k, mo * 128:(mo + 1) * 128],
                                    xn[:, k, :], start=(k == 0), stop=(k == DJ - 1)),
                                    waits=w2, inc=(s_pe if k == DJ - 1 else None))
                            st12_evac[par] = E("scalar", lambda e, fo=fo, par=par, dstbuf=dstbuf,
                                               bslot=bslot: e.activation(
                                dstbuf[:, fo, :], ps[:, par, :], AF.Silu,
                                bias=abias[:, bslot, fo:fo + 1]),
                                waits=[(s_pe, last)], inc=s_act)
                        wA_free[buf] = s_pe.n
                xin_done = s_act.n
                self.barrier()
                zread = [0, 0]
                bread = [0, 0]
                scand = [0, 0]
                for fo in range(FJ):
                    par = fo % 2
                    buf = fo % 2
                    w = [(s_pe, wz_free[buf])] if wz_free[buf] else []
                    if wz_stok[buf]:
                        w.append((self.s_cv, wz_stok[buf]))
                    wz_tok[buf], wz_stok[buf] = self.wstream(
                        wz[:, buf, :, :], self.Wzp.ap()[fo], self.cWz.ap()[fo],
                        ("Wz", fo), p == 0, w)
                    w = [(s_pe, wh_free[buf])] if wh_free[buf] else []
                    if wh_stok[buf]:
                        w.append((self.s_cv, wh_stok[buf]))
                    wh_tok[buf], wh_stok[buf] = self.wstream(
                        wh[:, buf, :, :], self.Whp.ap()[fo], self.cWh.ap()[fo],
                        ("Wh", fo), p == 0, w)
                    lastz = None
                    for k in range(FJ):
                        w2 = []
                        if k == 0:
                            w2 = [(s_wt, wz_tok[buf]), (s_act, xin_done)]
                            if zread[par]:
                                w2.append((s_act, zread[par]))
                        lastz = E("tensor", lambda e, k=k, buf=buf, par=par: e.matmul(
                            ps[:, par, :], wz[:, buf, k, :], x_in[:, k, :],
                            start=(k == 0), stop=(k == FJ - 1)),
                            waits=w2, inc=(s_pe if k == FJ - 1 else None))
                    lasth = None
                    for k in range(FJ):
                        w2 = []
                        if k == 0:
                            w2 = [(s_wt, wh_tok[buf])]
                            if bread[par]:
                                w2.append((s_dve, bread[par]))
                        lasth = E("tensor", lambda e, k=k, buf=buf, par=par: e.matmul(
                            ps[:, 2 + par, :], wh[:, buf, k, :], x_in[:, k, :],
                            start=(k == 0), stop=(k == FJ - 1)),
                            waits=w2, inc=(s_pe if k == FJ - 1 else None))
                    wz_free[buf] = wh_free[buf] = s_pe.n
                    wz1 = [(s_pe, lastz)]
                    if bread[par]:
                        wz1.append((s_dve, bread[par]))
                    E("scalar", lambda e, fo=fo, par=par: e.activation(
                        zbuf[:, par, :], ps[:, par, :], AF.Sigmoid, bias=abias[:, 2, fo:fo + 1]),
                        waits=wz1)
                    wa1 = []
                    if scand[par]:
                        wa1.append((s_dve, scand[par]))
                    zread[par] = E("scalar", lambda e, fo=fo, par=par: e.activation(
                        abuf[:, par, :], ps[:, par, :], AF.Sigmoid, bias=abias[:, 3, fo:fo + 1],
                        scale=-1.0), waits=wa1, inc=s_act)
                    bread[par] = E("vector", lambda e, fo=fo, par=par: e.scalar_tensor_tensor(
                        bbuf[:, par, :], ps[:, 2 + par, :], abias[:, 4, fo:fo + 1], zbuf[:, par, :],
                        ALU.add, ALU.mult), waits=[(s_pe, lasth), (s_act, zread[par])], inc=s_dve)
                    init = 0.0 if (p % 4) == 0 else self.carry.ap()[:, fo:fo + 1]
                    scand[par] = E("vector", lambda e, par=par, init=init: e.tensor_tensor_scan(
                        hbuf[:, par, :], abuf[:, par, :], bbuf[:, par, :], init, ALU.mult, ALU.add),
                        inc=s_dve)
                    E("vector", lambda e, fo=fo, par=par: e.tensor_tensor(
                        t1s[:, fo, :], t1s[:, fo, :], hbuf[:, par, :], ALU.add),
                      waits=[(s_act, xin_done)])
                    E("vector", lambda e, fo=fo, par=par: e.tensor_copy(
                        self.carry[:, fo:fo + 1], hbuf[:, par, 511:512]), inc=s_dve)
                t1_done = s_dve.n
                self.barrier()
                for j in range(4):
                    par = j % 2
                    w = [(s_act, st4_evac[par])] if st4_evac[par] else []
                    tok = E("scalar", lambda e, s=s, j=j, par=par, tok0=tok0: e.dma_start(
                        out=xtm[:, par, :], in_=self.x_tm[s, tok0 // 128 + j]),
                        waits=w, inc=s_x, inc_by=16)
                    last = None
                    for dj in range(DJ):
                        w2 = [(s_x, tok)] if dj == 0 else []
                        if j == 0 and dj == 0:
                            w2.append((s_dve, t1_done))
                        last = E("tensor", lambda e, par=par, dj=dj: e.transpose(
                            ps[:, 2 + 2 * par + dj // 4, (dj % 4) * 128:(dj % 4 + 1) * 128],
                            xtm[:, par, dj * 128:(dj + 1) * 128], self.ident_f[:]),
                            waits=w2, inc=(s_pe if dj == DJ - 1 else None))
                    st4_evac[par] = E("scalar", lambda e, par=par, j=j: e.copy(
                        xold[:, :, j * 128:(j + 1) * 128],
                        psf[:, (2 + 2 * par) * 512:(4 + 2 * par) * 512].rearrange(
                            "p (dj t) -> p dj t", t=128)),
                        waits=[(s_pe, last)], inc=s_act)
                xold_done = s_act.n
                for ob in range(8):
                    buf = ob % 2
                    par = ob % 2
                    w = [(s_pe, wz_free[buf])] if wz_free[buf] else []
                    if wz_stok[buf]:
                        w.append((self.s_cv, wz_stok[buf]))
                    wz_tok[buf], wz_stok[buf] = self.wstream(
                        wz[:, buf, :, :], self.Wop.ap()[ob], self.cWo.ap()[ob],
                        ("Wo", ob), p == 0, w)
                    last = None
                    for k in range(FJ):
                        w2 = []
                        if k == 0:
                            w2 = [(s_wt, wz_tok[buf]), (s_dve, t1_done)]
                            if st4_stored[par]:
                                w2.append((s_dve, st4_stored[par]))
                        last = E("tensor", lambda e, k=k, buf=buf, par=par: e.matmul(
                            ps[:, par, :], wz[:, buf, k, :], t1s[:, k, :],
                            start=(k == 0), stop=(k == FJ - 1)),
                            waits=w2, inc=(s_pe if k == FJ - 1 else None))
                    wz_free[buf] = s_pe.n
                    wv = [(s_pe, last), (s_act, xold_done)]
                    if st4_dma[par]:
                        wv.append((s_st, st4_dma[par]))
                    st4_stored[par] = E("vector", lambda e, ob=ob, par=par: e.scalar_tensor_tensor(
                        xnew[:, par, :], ps[:, par, :], aboT[:, ob:ob + 1], xold[:, ob, :],
                        ALU.add, ALU.add), waits=wv, inc=s_dve)
                    st4_dma[par] = E("scalar", lambda e, ob=ob, par=par, gtok0=gtok0: e.dma_start(
                        out=self.xfm.ap()[ob, :, gtok0:gtok0 + 512], in_=xnew[:, par, :]),
                        waits=[(s_dve, st4_stored[par])], inc=s_st, inc_by=16)
                self.barrier()

    def phase_F(self, tag, write_output):
        nc, E = self.nc, self.emit
        s_wt, s_x, s_st, s_pe, s_act, s_dve = (self.s_wt, self.s_x, self.s_st,
                                               self.s_pe, self.s_act, self.s_dve)
        with contextlib.ExitStack() as _es:
            xn = _es.enter_context(nc.sbuf_tensor(tag + "_xn", [128, DJ, 1024], BF16))
            acc = _es.enter_context(nc.sbuf_tensor(tag + "_acc", [128, DJ, 1024], F32))
            hf = _es.enter_context(nc.sbuf_tensor(tag + "_h", [128, FJ, 1024], BF16))
            wg = _es.enter_context(nc.sbuf_tensor(tag + "_wg", [128, 2, 8, 512], BF16))
            wu = _es.enter_context(nc.sbuf_tensor(tag + "_wu", [128, 2, 8, 512], BF16))
            wd = _es.enter_context(nc.sbuf_tensor(tag + "_wd", [128, 2, 32, 128], BF16))
            wrg = _es.enter_context(nc.sbuf_tensor(tag + "_wrg", [128, 8, 16], BF16))
            sqb = _es.enter_context(nc.sbuf_tensor(tag + "_sqb", [128, 2, 512], BF16))
            mstat = _es.enter_context(nc.sbuf_tensor(tag + "_mst", [128, 4, 512], F32))
            ssb = _es.enter_context(nc.sbuf_tensor(tag + "_s", [128, 2, 1024], F32))
            cwtm = _es.enter_context(nc.sbuf_tensor(tag + "_cwt", [128, 18, 8], F32))
            cwall = _es.enter_context(nc.sbuf_tensor(tag + "_cwa", [128, 128], F32))
            cwT = _es.enter_context(nc.sbuf_tensor(tag + "_cwT", [64, 128], F32))
            cwrow = _es.enter_context(nc.sbuf_tensor(tag + "_cwr", [1, 1024], F32))
            cwb = _es.enter_context(nc.sbuf_tensor(tag + "_cwb", [128, 1024], F32))
            ftmp = _es.enter_context(nc.sbuf_tensor(tag + "_tmp", [128, 1024], F32))
            st1 = _es.enter_context(nc.sbuf_tensor(tag + "_st1", [128, 8, 1], F32))
            ytm = _es.enter_context(nc.sbuf_tensor(tag + "_ytm", [128, 2, 1024], F32))
            ps = _es.enter_context(nc.psum_tensor(tag + "_ps", [128, 8, 512], F32))
            psf = ps.ap().rearrange("p a b -> p (a b)")
            tkw = E("gpsimd", lambda e: e.dma_start(out=wrg[:], in_=self.wrgp.ap()),
                    inc=s_wt, inc_by=16)
            wg_tok = [0, 0]; wg_free = [0, 0]; wg_stok = [0, 0]
            wu_tok = [0, 0]; wu_free = [0, 0]; wu_stok = [0, 0]
            wd_tok = [0, 0]; wd_free = [0, 0]; wd_stok = [0, 0]
            silu_tok = [0, 0]
            hmul_tok = [0, 0]
            dn_evac = 0
            dn_par = [0, 0]
            bc_read = 0
            cwrow_pe = 0
            ytm_dma = [0, 0]
            ytm_evac = [0, 0]
            for c in range(4):
                prime = (tag == "f1" and c == 0)
                gtok0 = c * 1024
                seq = c // 2
                self.barrier()
                xn_tok = self.ln_from_fm(gtok0, 1024, xn, psf, 6, 7,
                                         lambda tt: acc[:, :, tt * 512:(tt + 1) * 512],
                                         sqb, mstat)
                self.barrier()
                for ti in range(8):
                    for k in range(DJ):
                        w = [(s_dve, xn_tok), (s_wt, tkw)] if (k == 0 and ti == 0) else []
                        E("tensor", lambda e, ti=ti, k=k: e.matmul(
                            ps[:, 4, ti * 16:ti * 16 + 16], xn[:, k, ti * 128:(ti + 1) * 128],
                            wrg[:, k, :], start=(ti == 0 and k == 0), stop=(k == DJ - 1)),
                            waits=w, inc=(s_pe if (k == DJ - 1 and ti == 7) else None))
                ract = E("scalar", lambda e: e.copy(
                    cwtm[:, 0:16, :], ps[:, 4, 0:128].rearrange("p (a b) -> p a b", b=8)),
                    waits=[(s_pe, s_pe.n)], inc=s_act)
                cw_fin = 0
                tke2 = 0
                dbg_this = self.debug_xfm and tag == "f1" and c == 0
                for ti in range(8):
                    lr = cwtm.ap()[:, 2 * ti, :]
                    lg = cwtm.ap()[:, 2 * ti + 1, :]
                    er = cwtm.ap()[:, 16, :]
                    tprod = cwtm.ap()[:, 17, :]
                    w = [(s_act, ract)] if ti == 0 else [(s_act, tke2)]
                    E("vector", lambda e, lg=lg: e.reduce_max(st1[:, 1, :], lg, axis=AXX,
                                                              negate=True), waits=w)
                    E("vector", lambda e, lr=lr: e.reduce_max(st1[:, 0, :], lr, axis=AXX,
                                                              negate=True))
                    self.dve_spacer()
                    E("vector", lambda e: e.tensor_scalar_mul(st1[:, 2, :], st1[:, 1, :], 10.0))
                    self.dve_spacer()
                    tk = self.dve_spacer(inc=s_dve)
                    E("scalar", lambda e, lr=lr, er=er: e.activation(
                        er, lr, AF.Exp, bias=st1[:, 0, :], accum_out=st1[:, 3, :]),
                        waits=[(s_dve, tk)])
                    tke2 = E("scalar", lambda e, lg=lg: e.activation(
                        lg, lg, AF.Exp, bias=st1[:, 2, :], scale=10.0, accum_out=st1[:, 4, :]),
                        inc=s_act)
                    if dbg_this and ti == 0:
                        E("vector", lambda e, er=er: e.tensor_copy(self.dbgf2[:, 0:8], er),
                          waits=[(s_act, tke2)])
                        E("vector", lambda e, lg=lg: e.tensor_copy(self.dbgf2[:, 8:16], lg))
                        E("vector", lambda e, lr=lr: e.tensor_copy(self.dbgf2[:, 16:24], lr))
                    E("vector", lambda e, lg=lg, er=er, tprod=tprod: e.tensor_mul(tprod, er, lg),
                      waits=[(s_act, tke2)])
                    E("vector", lambda e: e.tensor_mul(st1[:, 6, :], st1[:, 3, :], st1[:, 4, :]))
                    self.dve_spacer()
                    E("vector", lambda e, tprod=tprod: e.reduce_sum(st1[:, 5, :], tprod, axis=AXX))
                    self.dve_spacer()
                    self.dve_spacer()
                    E("vector", lambda e: e.scalar_tensor_tensor(
                        st1[:, 6, :], st1[:, 6, :], 1e-9, st1[:, 5, :], ALU.mult, ALU.add))
                    self.dve_spacer()
                    self.dve_spacer()
                    E("vector", lambda e: e.reciprocal(st1[:, 7, :], st1[:, 6, :]))
                    self.dve_spacer()
                    self.dve_spacer()
                    if dbg_this and ti == 0:
                        E("vector", lambda e: e.tensor_copy(
                            self.dbgf2[:, 24:32], st1[:, :, 0].transpose([0, 1]) if False else st1.ap().rearrange("p a b -> p (a b)")))
                        E("vector", lambda e, tprod=tprod: e.tensor_copy(self.dbgf2[:, 32:40], tprod))
                    cw_fin = E("vector", lambda e, ti=ti, tprod=tprod: e.tensor_scalar_mul(
                        cwall[:, ti * 8:(ti + 1) * 8], tprod, st1[:, 7, :]), inc=s_dve)
                    if dbg_this and ti == 0:
                        tdd = E("vector", lambda e: e.tensor_copy(
                            self.dbgf2[:, 40:48], cwall[:, 0:8]), inc=s_dve)
                        E("sync", lambda e: e.dma_start(out=self.dbg_t0.ap(), in_=self.dbgf2[:]),
                          waits=[(s_dve, tdd)], inc=s_st, inc_by=16)
                if dbg_this:
                    tda = E("vector", lambda e: e.tensor_copy(self.dbgf[:, 0:128], cwall[:]),
                            waits=[(s_dve, cw_fin)], inc=s_dve)
                    E("sync", lambda e: e.dma_start(out=self.dbg_s.ap()[:, 0:128], in_=self.dbgf[:, 0:128]),
                      waits=[(s_dve, tda)], inc=s_st, inc_by=16)
                wt2 = [(s_dve, cw_fin)]
                if cwrow_pe:
                    wt2.append((s_pe, cwrow_pe))
                tkt = E("tensor", lambda e: e.transpose(ps[:, 5, 0:128], cwall[:], self.ident_f[:]),
                        waits=wt2, inc=s_pe)
                cw_done = E("scalar", lambda e: e.copy(cwT[:], ps[0:64, 5, 0:128]),
                            waits=[(s_pe, tkt)], inc=s_act)
                self.barrier()
                for ex in range(NE):
                    wv = [(s_act, cw_done)]
                    if cwrow_pe:
                        wv.append((s_pe, cwrow_pe))
                    tkr = None
                    for ti in range(8):
                        tkr = E("sync", lambda e, ex=ex, ti=ti: e.dma_start(
                            out=cwrow[:, ti * 128:(ti + 1) * 128],
                            in_=cwT[ti * 8 + ex:ti * 8 + ex + 1, :]),
                            waits=(wv if ti == 0 else []), inc=s_x, inc_by=16)
                    last = None
                    for q in range(4):
                        w = []
                        if q == 0:
                            w = [(s_x, tkr)]
                            if bc_read:
                                w.append((s_dve, bc_read))
                            if dn_evac:
                                w.append((s_dve, dn_evac))
                        last = E("tensor", lambda e, q=q: e.matmul(
                            psf[:, 4 * 512 + q * 256:4 * 512 + (q + 1) * 256],
                            self.ones_f[0:1, :], cwrow[:, q * 256:(q + 1) * 256],
                            start=True, stop=True), waits=w, inc=(s_pe if q == 3 else None))
                    cwrow_pe = last
                    bc_read = E("vector", lambda e: e.tensor_copy(
                        cwb[:], psf[:, 4 * 512:6 * 512]), waits=[(s_pe, last)], inc=s_dve)
                    if self.debug_xfm and tag == "f1" and c == 0 and ex == 0:
                        E("sync", lambda e: e.dma_start(out=self.dbg_cwT.ap(), in_=cwT[:]),
                          waits=[(s_act, cw_done)], inc=s_st, inc_by=16)
                        E("sync", lambda e: e.dma_start(out=self.dbg_cwb.ap(), in_=cwb[:]),
                          waits=[(s_dve, bc_read)], inc=s_st, inc_by=16)
                        tdx = E("vector", lambda e: e.tensor_copy(self.dbgf[:], xn[:, 0, :]),
                                waits=[(s_dve, xn_tok)], inc=s_dve)
                        E("sync", lambda e: e.dma_start(out=self.dbg_xn.ap(), in_=self.dbgf[:]),
                          waits=[(s_dve, tdx)], inc=s_st, inc_by=16)
                    for fo in range(FJ):
                        mb, mo = fo // 4, fo % 4
                        buf = mb % 2
                        fpar = fo % 2
                        if mo == 0:
                            w = [(s_pe, wg_free[buf])] if wg_free[buf] else []
                            if wg_stok[buf]:
                                w.append((self.s_cv, wg_stok[buf]))
                            wg_tok[buf], wg_stok[buf] = self.wstream(
                                wg[:, buf, :, :], self.wgp.ap()[ex, mb], self.cwg.ap()[ex, mb],
                                ("wg", ex, mb), prime, w)
                            w = [(s_pe, wu_free[buf])] if wu_free[buf] else []
                            if wu_stok[buf]:
                                w.append((self.s_cv, wu_stok[buf]))
                            wu_tok[buf], wu_stok[buf] = self.wstream(
                                wu[:, buf, :, :], self.wup.ap()[ex, mb], self.cwu.ap()[ex, mb],
                                ("wu", ex, mb), prime, w)
                        gb = 2 * fpar
                        ub = 4 + 2 * fpar
                        lastg = None
                        for k in range(DJ):
                            for tt in range(2):
                                w2 = []
                                if k == 0 and tt == 0:
                                    w2 = [(s_wt, wg_tok[buf]), (s_dve, xn_tok)]
                                    if silu_tok[fpar]:
                                        w2.append((s_act, silu_tok[fpar]))
                                lastg = E("tensor", lambda e, mo=mo, k=k, tt=tt, buf=buf, gb=gb: e.matmul(
                                    ps[:, gb + tt, :], wg[:, buf, k, mo * 128:(mo + 1) * 128],
                                    xn[:, k, tt * 512:(tt + 1) * 512],
                                    start=(k == 0), stop=(k == DJ - 1)),
                                    waits=w2, inc=(s_pe if (k == DJ - 1 and tt == 1) else None))
                        lastu = None
                        for k in range(DJ):
                            for tt in range(2):
                                w2 = []
                                if k == 0 and tt == 0:
                                    w2 = [(s_wt, wu_tok[buf])]
                                    if hmul_tok[fpar]:
                                        w2.append((s_dve, hmul_tok[fpar]))
                                lastu = E("tensor", lambda e, mo=mo, k=k, tt=tt, buf=buf, ub=ub: e.matmul(
                                    ps[:, ub + tt, :], wu[:, buf, k, mo * 128:(mo + 1) * 128],
                                    xn[:, k, tt * 512:(tt + 1) * 512],
                                    start=(k == 0), stop=(k == DJ - 1)),
                                    waits=w2, inc=(s_pe if (k == DJ - 1 and tt == 1) else None))
                        if mo == 3:
                            wg_free[buf] = wu_free[buf] = s_pe.n
                        wsi = [(s_pe, lastg)]
                        if hmul_tok[fpar]:
                            wsi.append((s_dve, hmul_tok[fpar]))
                        silu_tok[fpar] = E("scalar", lambda e, fpar=fpar, gb=gb: e.activation(
                            ssb[:, fpar, :], psf[:, gb * 512:(gb + 2) * 512], AF.Silu),
                            waits=wsi, inc=s_act)
                        hmul_tok[fpar] = E("vector", lambda e, fo=fo, fpar=fpar, ub=ub: e.tensor_tensor(
                            hf[:, fo, :], ssb[:, fpar, :], psf[:, ub * 512:(ub + 2) * 512], ALU.mult),
                            waits=[(s_act, silu_tok[fpar]), (s_pe, lastu)], inc=s_dve)
                    h_done = max(hmul_tok)
                    if self.debug_xfm and tag == "f1" and c == 0 and ex == 0:
                        tdh = E("vector", lambda e: e.tensor_copy(self.dbgf[:], hf[:, 0, :]),
                                waits=[(s_dve, h_done)], inc=s_dve)
                        E("sync", lambda e: e.dma_start(out=self.dbg_h.ap(), in_=self.dbgf[:]),
                          waits=[(s_dve, tdh)], inc=s_st, inc_by=16)
                    for o in range(8):
                        buf = o % 2
                        w = [(s_pe, wd_free[buf])] if wd_free[buf] else []
                        if wd_stok[buf]:
                            w.append((self.s_cv, wd_stok[buf]))
                        wd_tok[buf], wd_stok[buf] = self.wstream(
                            wd[:, buf, :, :], self.wdp.ap()[ex, o], self.cwd.ap()[ex, o],
                            ("wd", ex, o), prime, w)
                        opar = o % 2
                        db = 4 + 2 * opar
                        last = None
                        for k in range(FJ):
                            for tt in range(2):
                                w2 = []
                                if k == 0 and tt == 0:
                                    w2 = [(s_wt, wd_tok[buf]), (s_dve, h_done)]
                                    if dn_par[opar]:
                                        w2.append((s_dve, dn_par[opar]))
                                last = E("tensor", lambda e, k=k, tt=tt, buf=buf, db=db: e.matmul(
                                    ps[:, db + tt, :], wd[:, buf, k, :],
                                    hf[:, k, tt * 512:(tt + 1) * 512],
                                    start=(k == 0), stop=(k == FJ - 1)),
                                    waits=w2, inc=(s_pe if (k == FJ - 1 and tt == 1) else None))
                        wd_free[buf] = s_pe.n
                        E("vector", lambda e, db=db: e.tensor_mul(
                            ftmp[:], psf[:, db * 512:(db + 2) * 512], cwb[:]),
                          waits=[(s_pe, last), (s_dve, bc_read)])
                        dn_evac = E("vector", lambda e, o=o: e.tensor_tensor(
                            acc[:, o, :], acc[:, o, :], ftmp[:], ALU.add), inc=s_dve)
                        dn_par[opar] = dn_evac
                if not write_output:
                    for dj in range(DJ):
                        E("scalar", lambda e, dj=dj, gtok0=gtok0: e.dma_start(
                            out=self.xfm.ap()[dj, :, gtok0:gtok0 + 1024], in_=acc[:, dj, :]),
                          waits=[(s_dve, dn_evac)], inc=s_st, inc_by=16)
                else:
                    for j in range(8):
                        par = j % 2
                        last = None
                        for dj in range(DJ):
                            w2 = [(s_dve, dn_evac)] if dj == 0 else []
                            if dj == 0 and ytm_evac[par]:
                                w2.append((s_act, ytm_evac[par]))
                            last = E("tensor", lambda e, j=j, dj=dj, par=par: e.transpose(
                                ps[:, 2 * par + dj // 4, (dj % 4) * 128:(dj % 4 + 1) * 128],
                                acc[:, dj, j * 128:(j + 1) * 128], self.ident_f[:]),
                                waits=w2, inc=(s_pe if dj == DJ - 1 else None))
                        wv = [(s_pe, last)]
                        if ytm_dma[par]:
                            wv.append((s_st, ytm_dma[par]))
                        ytm_evac[par] = E("scalar", lambda e, par=par: e.copy(
                            ytm[:, par, :].rearrange("p (dj t) -> p dj t", t=128),
                            psf[:, 2 * par * 512:(2 * par + 2) * 512].rearrange(
                                "p (dj t) -> p dj t", t=128)),
                            waits=wv, inc=s_act)
                        tb = (gtok0 % T) // 128 + j
                        ytm_dma[par] = E("scalar", lambda e, seq=seq, tb=tb, par=par: e.dma_start(
                            out=self.y_tm[seq, tb], in_=ytm[:, par, :]),
                            waits=[(s_act, ytm_evac[par])], inc=s_st, inc_by=16)
                self.barrier()

    def phase_C(self):
        nc, E = self.nc, self.emit
        s_wt, s_x, s_st, s_pe, s_act, s_dve = (self.s_wt, self.s_x, self.s_st,
                                               self.s_pe, self.s_act, self.s_dve)
        with contextlib.ExitStack() as _es:
            xn = _es.enter_context(nc.sbuf_tensor("c_xn", [128, DJ, T], BF16))
            qsb = _es.enter_context(nc.sbuf_tensor("c_q", [128, 8, T], BF16))
            kb = _es.enter_context(nc.sbuf_tensor("c_kb", [128, 2, T], BF16))
            vb = _es.enter_context(nc.sbuf_tensor("c_vb", [128, 2, T], BF16))
            vt = _es.enter_context(nc.sbuf_tensor("c_vt", [128, 2, 16, 128], BF16))
            klat = _es.enter_context(nc.sbuf_tensor("c_klat", [64, T], BF16))
            vlat = _es.enter_context(nc.sbuf_tensor("c_vlat", [64, T], BF16))
            Ssb = _es.enter_context(nc.sbuf_tensor("c_S", [128, 2, T], F32))
            Pb = _es.enter_context(nc.sbuf_tensor("c_P", [128, 2, T], BF16))
            PT = _es.enter_context(nc.sbuf_tensor("c_PT", [128, 16, 128], BF16))
            yall = _es.enter_context(nc.sbuf_tensor("c_y", [128, 8, 512], BF16))
            woa = _es.enter_context(nc.sbuf_tensor("c_woa", [128, 8, 8, 128], BF16))
            wqs = _es.enter_context(nc.sbuf_tensor("c_wq", [128, 2, 8, 128], BF16))
            wkd = _es.enter_context(nc.sbuf_tensor("c_wkd", [128, 8, 64], BF16))
            wvd = _es.enter_context(nc.sbuf_tensor("c_wvd", [128, 8, 64], BF16))
            kup = _es.enter_context(nc.sbuf_tensor("c_kup", [64, 128], BF16))
            vup = _es.enter_context(nc.sbuf_tensor("c_vup", [64, 128], BF16))
            masks = _es.enter_context(nc.sbuf_tensor("c_mask", [128, 4, 512], F32))
            sqb = _es.enter_context(nc.sbuf_tensor("c_sqb", [128, 2, 512], BF16))
            mstat = _es.enter_context(nc.sbuf_tensor("c_mst", [128, 4, 512], F32))
            xold = _es.enter_context(nc.sbuf_tensor("c_xold", [128, DJ, 512], F32))
            xnw = _es.enter_context(nc.sbuf_tensor("c_xnw", [128, 2, 512], F32))
            st1 = _es.enter_context(nc.sbuf_tensor("c_st1", [128, 4, 1], F32))
            ps = _es.enter_context(nc.psum_tensor("c_ps", [128, 6, 512], F32))
            psb = _es.enter_context(nc.psum_tensor("c_psb", [128, 2, 1024], BF16))
            psf = ps.ap().rearrange("p a b -> p (a b)")
            tk_m = E("sync", lambda e: e.dma_start(out=masks[:], in_=self.maskh.ap().transpose(
                [1, 0, 2])), inc=s_x, inc_by=16)
            tk_woa = E("gpsimd", lambda e: e.dma_start(out=woa[:], in_=self.woap.ap().transpose(
                [1, 0, 2, 3])), inc=s_wt, inc_by=16)
            tk_wkd = E("gpsimd", lambda e: e.dma_start(out=wkd[:], in_=self.wkdp.ap()),
                       inc=s_wt, inc_by=16)
            tk_wvd = E("gpsimd", lambda e: e.dma_start(out=wvd[:], in_=self.wvdp.ap()),
                       inc=s_wt, inc_by=16)
            E("gpsimd", lambda e: e.dma_start(out=kup[0:32, :], in_=self.wkup.ap()),
              inc=s_wt, inc_by=16)
            tk_kup = E("gpsimd", lambda e: e.dma_start(out=kup[32:64, :], in_=self.wkup.ap()),
                       inc=s_wt, inc_by=16)
            E("gpsimd", lambda e: e.dma_start(out=vup[0:32, :], in_=self.wvup.ap()),
              inc=s_wt, inc_by=16)
            tk_vup = E("gpsimd", lambda e: e.dma_start(out=vup[32:64, :], in_=self.wvup.ap()),
                       inc=s_wt, inc_by=16)
            wq_tok = [0, 0]; wq_free = [0, 0]
            for seq in range(NSEQ):
                self.barrier()
                xn_tok = self.ln_from_fm(seq * T, T, xn, psf, 4, 5, lambda tt: xold[:], sqb, mstat, xf_war=True)
                self.barrier()
                qev = 0
                for hb in range(8):
                    buf = hb % 2
                    w = [(s_pe, wq_free[buf])] if wq_free[buf] else []
                    wq_tok[buf] = E("gpsimd", lambda e, hb=hb, buf=buf: e.dma_start(
                        out=wqs[:, buf, :, :], in_=self.wqp.ap()[hb]), waits=w, inc=s_wt, inc_by=16)
                    last = None
                    for k in range(DJ):
                        for tt in range(4):
                            w2 = []
                            if k == 0 and tt == 0:
                                w2 = [(s_wt, wq_tok[buf]), (s_dve, xn_tok)]
                                if qev:
                                    w2.append((s_act, qev))
                            last = E("tensor", lambda e, k=k, tt=tt, buf=buf: e.matmul(
                                ps[:, tt, :], wqs[:, buf, k, :], xn[:, k, tt * 512:(tt + 1) * 512],
                                start=(k == 0), stop=(k == DJ - 1)),
                                waits=w2, inc=(s_pe if (k == DJ - 1 and tt == 3) else None))
                    wq_free[buf] = s_pe.n
                    qev = E("scalar", lambda e, hb=hb: e.copy(
                        qsb[:, hb, :], psf[:, 0:4 * 512]), waits=[(s_pe, last)], inc=s_act)
                klev = 0
                for wsrc, wtok, ldst in [(wkd, tk_wkd, klat), (wvd, tk_wvd, vlat)]:
                    for tt in range(4):
                        last = None
                        for k in range(DJ):
                            w2 = []
                            if k == 0:
                                w2 = [(s_wt, wtok), (s_dve, xn_tok)]
                                if klev:
                                    w2.append((s_act, klev))
                            last = E("tensor", lambda e, k=k, tt=tt, wsrc=wsrc: e.matmul(
                                ps[0:64, 4, :], wsrc[:, k, :], xn[:, k, tt * 512:(tt + 1) * 512],
                                start=(k == 0), stop=(k == DJ - 1)),
                                waits=w2, inc=(s_pe if k == DJ - 1 else None))
                        klev = E("scalar", lambda e, tt=tt, ldst=ldst: e.copy(
                            ldst[:, tt * 512:(tt + 1) * 512], ps[0:64, 4, :]),
                            waits=[(s_pe, last)], inc=s_act)
                upev = 0
                for usrc, utok, latb, dstb in [(kup, tk_kup, klat, kb),
                                               (vup, tk_vup, vlat, vb)]:
                    for g in range(2):
                        for tt in range(4):
                            w2 = [(s_wt, utok), (s_act, klev)]
                            if upev:
                                w2.append((s_act, upev))
                            last = E("tensor", lambda e, tt=tt, usrc=usrc, latb=latb, g=g: e.matmul(
                                ps[:, 5, :], usrc[g * 32:(g + 1) * 32, :],
                                latb[g * 32:(g + 1) * 32, tt * 512:(tt + 1) * 512],
                                start=True, stop=True), waits=w2, inc=s_pe)
                            upev = E("scalar", lambda e, g=g, tt=tt, dstb=dstb: e.copy(
                                dstb[:, g, tt * 512:(tt + 1) * 512], ps[:, 5, :]),
                                waits=[(s_pe, last)], inc=s_act)
                vtev = 0
                for g in range(2):
                    for hw in range(2):
                        last = None
                        for j2 in range(hw * 8, hw * 8 + 8):
                            w2 = []
                            if j2 % 8 == 0:
                                w2 = [(s_act, upev)]
                                if vtev:
                                    w2.append((s_act, vtev))
                            last = E("tensor", lambda e, g=g, j2=j2, hw=hw: e.transpose(
                                psb[:, hw, (j2 % 8) * 128:(j2 % 8 + 1) * 128],
                                vb[:, g, j2 * 128:(j2 + 1) * 128], self.ident_b[:]),
                                waits=w2, inc=(s_pe if j2 % 8 == 7 else None))
                        vtev = E("scalar", lambda e, g=g, hw=hw: e.copy(
                            vt[:, g, hw * 8:hw * 8 + 8, :],
                            psb[:, hw, :].rearrange("p (a b) -> p a b", b=128)),
                            waits=[(s_pe, last)], inc=s_act)
                self.barrier()
                sc_prev = 0
                pb_done = [0, 0]
                pt_evac = 0
                y_evac = 0
                yproj = 0
                prj_dve = 0
                xnw_dma = [0, 0]
                for G in range(4):
                    g0 = seq * T + G * 512
                    wv = [(s_dve, prj_dve)] if prj_dve else []
                    wv.append((s_st, self.store_gate))
                    xold_tok = E("scalar", lambda e, g0=g0: e.dma_start(
                        out=xold[:], in_=self.xfm_t[:, :, g0:g0 + 512]),
                        waits=wv, inc=s_x, inc_by=16)
                    for i in range(4 * G, 4 * G + 4):
                        jmax = i // 4
                        length = (jmax + 1) * 512
                        for h in range(8):
                            par = (i * 8 + h) % 2
                            g = h // 4
                            last = None
                            for j in range(jmax + 1):
                                w2 = []
                                if j == 0 and sc_prev:
                                    w2.append((s_act, sc_prev))
                                last = E("tensor", lambda e, i=i, h=h, j=j, g=g: e.matmul(
                                    ps[:, j, :], qsb[:, h, i * 128:(i + 1) * 128],
                                    kb[:, g, j * 512:(j + 1) * 512], start=True, stop=True),
                                    waits=w2, inc=(s_pe if j == jmax else None))
                            wv2 = [(s_pe, last), (s_x, tk_m)]
                            if pb_done[par]:
                                wv2.append((s_dve, pb_done[par]))
                            sc_prev = E("scalar", lambda e, par=par, length=length: e.activation(
                                Ssb[:, par, 0:length], psf[:, 0:length], AF.Copy, scale=SCALE),
                                waits=wv2, inc=s_act)
                            E("vector", lambda e, par=par, i=i, jmax=jmax: e.tensor_tensor(
                                Ssb[:, par, jmax * 512:(jmax + 1) * 512],
                                Ssb[:, par, jmax * 512:(jmax + 1) * 512],
                                masks[:, i % 4, :], ALU.add), waits=[(s_act, sc_prev)])
                            self.dve_spacer()
                            E("vector", lambda e, par=par, length=length: e.reduce_max(
                                st1[:, 0, :], Ssb[:, par, 0:length], axis=AXX, negate=True))
                            self.dve_spacer()
                            tks = self.dve_spacer(inc=s_dve)
                            tke = E("scalar", lambda e, par=par, length=length: e.activation(
                                Ssb[:, par, 0:length], Ssb[:, par, 0:length], AF.Exp,
                                bias=st1[:, 0, :], accum_out=st1[:, 1, :]),
                                waits=[(s_dve, tks)], inc=s_act)
                            E("vector", lambda e: e.reciprocal(st1[:, 2, :], st1[:, 1, :]),
                              waits=[(s_act, tke)])
                            self.dve_spacer()
                            self.dve_spacer()
                            pb_done[par] = E("vector", lambda e, par=par, length=length:
                                             e.tensor_scalar_mul(
                                Pb[:, par, 0:length], Ssb[:, par, 0:length], st1[:, 2, :]),
                                inc=s_dve)
                            ntile = i + 1
                            for wv3 in range((ntile + 7) // 8):
                                lo = wv3 * 8
                                hi = min(lo + 8, ntile)
                                last = None
                                for j2 in range(lo, hi):
                                    w2 = []
                                    if j2 == lo:
                                        w2 = [(s_dve, pb_done[par])]
                                        if pt_evac:
                                            w2.append((s_act, pt_evac))
                                    last = E("tensor", lambda e, par=par, j2=j2, wv3=wv3: e.transpose(
                                        psb[:, wv3 % 2, (j2 % 8) * 128:(j2 % 8 + 1) * 128],
                                        Pb[:, par, j2 * 128:(j2 + 1) * 128], self.ident_b[:]),
                                        waits=w2, inc=(s_pe if j2 == hi - 1 else None))
                                pt_evac = E("scalar", lambda e, lo=lo, hi=hi, wv3=wv3: e.copy(
                                    PT[:, lo:hi, :], psb[:, wv3 % 2, 0:(hi - lo) * 128].rearrange(
                                        "p (a b) -> p a b", b=128)), waits=[(s_pe, last)], inc=s_act)
                            last = None
                            for j2 in range(ntile):
                                w2 = []
                                if j2 == 0:
                                    w2 = [(s_act, pt_evac), (s_act, vtev)]
                                    if y_evac:
                                        w2.append((s_act, y_evac))
                                last = E("tensor", lambda e, g=g, j2=j2, ntile=ntile: e.matmul(
                                    ps[:, 4, 0:128], vt[:, g, j2, :], PT[:, j2, :],
                                    start=(j2 == 0), stop=(j2 == ntile - 1)),
                                    waits=w2, inc=(s_pe if j2 == ntile - 1 else None))
                            wv4 = [(s_pe, last)]
                            if yproj:
                                wv4.append((s_pe, yproj))
                            y_evac = E("scalar", lambda e, h=h, i=i: e.copy(
                                yall[:, h, (i % 4) * 128:(i % 4 + 1) * 128], ps[:, 4, 0:128]),
                                waits=wv4, inc=s_act)
                    y_done = y_evac
                    for o in range(8):
                        par = o % 2
                        last = None
                        for k in range(8):
                            w2 = []
                            if k == 0:
                                w2 = [(s_act, y_done), (s_wt, tk_woa)]
                                if prj_dve:
                                    w2.append((s_dve, prj_dve))
                            last = E("tensor", lambda e, o=o, k=k: e.matmul(
                                ps[:, 5, :], woa[:, o, k, :], yall[:, k, :],
                                start=(k == 0), stop=(k == 7)),
                                waits=w2, inc=(s_pe if k == 7 else None))
                        if o == 7:
                            yproj = last
                        wv5 = [(s_pe, last), (s_x, xold_tok)]
                        if xnw_dma[par]:
                            wv5.append((s_st, xnw_dma[par]))
                        prj_dve = E("vector", lambda e, o=o, par=par: e.tensor_tensor(
                            xnw[:, par, :], ps[:, 5, :], xold[:, o, :], ALU.add),
                            waits=wv5, inc=s_dve)
                        xnw_dma[par] = E("scalar", lambda e, o=o, par=par, g0=g0: e.dma_start(
                            out=self.xfm.ap()[o, :, g0:g0 + 512], in_=xnw[:, par, :]),
                            waits=[(s_dve, prj_dve)], inc=s_st, inc_by=16)
                self.barrier()

    def build(self):
        E = self.emit
        s_x, s_dve, s_st = self.s_x, self.s_dve, self.s_st
        t0 = E("sync", lambda e: e.dma_start(out=self.ones_f[:], in_=self.onesh.ap()),
               inc=s_x, inc_by=16)
        t1 = E("sync", lambda e: e.dma_start(out=self.ident_f[:], in_=self.identh.ap()),
               inc=s_x, inc_by=16)
        E("vector", lambda e: e.tensor_copy(self.ones_b[:], self.ones_f[:]), waits=[(s_x, t0)])
        E("vector", lambda e: e.tensor_copy(self.ident_b[:], self.ident_f[:]), waits=[(s_x, t1)])
        E("vector", lambda e: e.memset(self.eps[:], 1e-5), inc=s_dve)
        self.cache_tok = {}
        self.barrier()
        if "A" in self.phases:
            self.phase_A()
            self.store_gate = self.s_st.n
        if "F1" in self.phases:
            self.phase_F("f1", write_output=False)
            self.store_gate = self.s_st.n
        if "C" in self.phases:
            self.phase_C()
            self.store_gate = self.s_st.n
        if "F2" in self.phases:
            self.phase_F("f2", write_output=True)
        self.barrier()
        if self.debug_xfm:
            E("sync", lambda e: e.dma_start(out=self.dbg.ap(), in_=self.xfm.ap()),
              waits=[(s_st, s_st.n)], inc=s_st, inc_by=16)
        E("sync", lambda e: e.wait_ge(self.s_st.h, self.s_st.n))
        self.barrier()
        return self.nc


def pack_w(W, w):
    K, M = W.shape
    return np.ascontiguousarray(W.reshape(K // 128, 128, M // w, w).transpose(2, 1, 0, 3))


def pack_bias(b):
    return np.ascontiguousarray(b.reshape(-1, 128).T)


def make_masks():
    m = np.zeros((4, 128, 512), np.float32)
    for v in range(4):
        for q in range(128):
            m[v, q, v * 128 + q + 1:] = NEG
    return m


def host_inputs(full):
    g = {}
    g["W1p"] = pack_w(full["L0_attn_a_W1"], 512)
    g["W2p"] = pack_w(full["L0_attn_a_W2"], 512)
    g["Wzp"] = pack_w(full["L0_attn_a_Wz"], 128)
    g["Whp"] = pack_w(full["L0_attn_a_Wh"], 128)
    g["Wop"] = pack_w(full["L0_attn_a_Wo"], 128)
    g["b1h"] = pack_bias(full["L0_attn_a_b1"])
    g["b2h"] = pack_bias(full["L0_attn_a_b2"])
    g["bzh"] = pack_bias(full["L0_attn_a_bz"])
    g["bhh"] = pack_bias(full["L0_attn_a_bh"])
    g["boh"] = pack_bias(full["L0_attn_a_bo"])
    g["wqp"] = pack_w(full["L0_attn_b_wq"], 128)
    g["wkdp"] = pack_w(full["L0_attn_b_wk_down"], 64)[0]
    g["wvdp"] = pack_w(full["L0_attn_b_wv_down"], 64)[0]
    g["wkup"] = np.ascontiguousarray(full["L0_attn_b_wk_up"])
    g["wvup"] = np.ascontiguousarray(full["L0_attn_b_wv_up"])
    g["woap"] = pack_w(full["L0_attn_b_wo"], 128)
    g["wrgp"] = pack_w(np.concatenate([np.asarray(full["L0_ffn_w_router"]),
                                       np.asarray(full["L0_ffn_w_gating"])], axis=1), 16)[0]
    g["wgp"] = np.stack([pack_w(np.asarray(full["L0_ffn_w_gate"])[e], 512) for e in range(NE)])
    g["wup"] = np.stack([pack_w(np.asarray(full["L0_ffn_w_up"])[e], 512) for e in range(NE)])
    g["wdp"] = np.stack([pack_w(np.asarray(full["L0_ffn_w_down"])[e], 128) for e in range(NE)])
    g["onesh"] = np.ones((128, 128), np.float32)
    g["identh"] = np.eye(128, dtype=np.float32)
    g["maskh"] = make_masks()
    g = {k: np.ascontiguousarray(v, dtype=np.float32) for k, v in g.items()}
    x = np.asarray(full["L0_attn_a_x"])
    xs = [np.ascontiguousarray(x[2 * c:2 * c + 2]) for c in range(8)]
    return g, xs


def kernel(**inputs):
    b = Builder()
    nc = b.build()
    g, xs = host_inputs(inputs)
    in_maps = [dict(g, x=xs[c]) for c in range(8)]
    res = run_bass_kernel_spmd(nc, in_maps, core_ids=list(range(8)))
    return np.concatenate([r["y"] for r in res.results], axis=0)


# version 05
# speedup vs baseline: 1.1097x; 1.0347x over previous
import numpy as np
import contextlib
import concourse.bass as bass
import concourse.mybir as mybir
from concourse.bass_utils import run_bass_kernel_spmd

F32 = mybir.dt.float32
BF16 = mybir.dt.bfloat16
AF = mybir.ActivationFunctionType
ALU = mybir.AluOpType
AXX = mybir.AxisListType.X

D = 1024
DJ = 8
FJ = 32
T = 2048
NSEQ = 2
NTOK = NSEQ * T
NE = 8
NEG = -1.0e30
SCALE = 128.0 ** -0.5


class Sem:
    def __init__(self, nc, name):
        self.h = nc.alloc_semaphore(name)
        self.n = 0
        self.name = name


class Builder:
    ENGINES = ("sync", "gpsimd", "tensor", "scalar", "vector")

    def __init__(self, phases=("A", "F1", "C", "F2"), debug_xfm=False, n_barrier=True):
        self.phases = phases
        self.debug_xfm = debug_xfm
        nc = self.nc = bass.Bass("TRN2", target_bir_lowering=False)
        self.ops = {e: [] for e in self.ENGINES}
        self.hwm = {e: {} for e in self.ENGINES}

        dram = lambda n, s: nc.dram_tensor(n, s, F32, kind="ExternalInput")
        self.x_in = dram("x", [NSEQ, T, D])
        self.W1p = dram("W1p", [8, 128, 8, 512])
        self.W2p = dram("W2p", [8, 128, 8, 512])
        self.Wzp = dram("Wzp", [32, 128, 32, 128])
        self.Whp = dram("Whp", [32, 128, 32, 128])
        self.Wop = dram("Wop", [8, 128, 32, 128])
        self.b1h = dram("b1h", [128, 32])
        self.b2h = dram("b2h", [128, 32])
        self.bzh = dram("bzh", [128, 32])
        self.bhh = dram("bhh", [128, 32])
        self.boh = dram("boh", [128, 8])
        self.wqp = dram("wqp", [8, 128, 8, 128])
        self.wkdp = dram("wkdp", [128, 8, 64])
        self.wvdp = dram("wvdp", [128, 8, 64])
        self.wkup = dram("wkup", [32, 128])
        self.wvup = dram("wvup", [32, 128])
        self.woap = dram("woap", [8, 128, 8, 128])
        self.wrgp = dram("wrgp", [128, 8, 16])
        self.wgp = dram("wgp", [NE, 8, 128, 8, 512])
        self.wup = dram("wup", [NE, 8, 128, 8, 512])
        self.wdp = dram("wdp", [NE, 8, 128, 32, 128])
        self.onesh = dram("onesh", [128, 128])
        self.identh = dram("identh", [128, 128])
        self.maskh = dram("maskh", [4, 128, 512])

        self.y_out = nc.dram_tensor("y", [NSEQ, T, D], F32, kind="ExternalOutput")
        if debug_xfm:
            self.dbg = nc.dram_tensor("dbg_xfm", [DJ, 128, NTOK], F32, kind="ExternalOutput")
            self.dbg_cwT = nc.dram_tensor("dbg_cwT", [64, 128], F32, kind="ExternalOutput")
            self.dbg_cwb = nc.dram_tensor("dbg_cwb", [128, 1024], F32, kind="ExternalOutput")
            self.dbg_xn = nc.dram_tensor("dbg_xn", [128, 1024], F32, kind="ExternalOutput")
            self.dbg_h = nc.dram_tensor("dbg_h", [128, 1024], F32, kind="ExternalOutput")
            self.dbg_s = nc.dram_tensor("dbg_s", [128, 1024], F32, kind="ExternalOutput")
            self.dbg_t0 = nc.dram_tensor("dbg_t0", [128, 48], F32, kind="ExternalOutput")
            self.dbgf2 = nc.alloc_sbuf_tensor("dbgf2", [128, 48], F32)
            self.dbgf = nc.alloc_sbuf_tensor("dbgf", [128, 1024], F32)
        self.xfm = nc.dram_tensor("xfm", [DJ, 128, NTOK], F32, kind="Internal")
        cache = lambda n, s: nc.dram_tensor(n, s, BF16, kind="Internal")
        self.cW1 = cache("cW1", [8, 128, 8, 512])
        self.cW2 = cache("cW2", [8, 128, 8, 512])
        self.cWz = cache("cWz", [32, 128, 32, 128])
        self.cWh = cache("cWh", [32, 128, 32, 128])
        self.cWo = cache("cWo", [8, 128, 32, 128])
        self.cwg = cache("cwg", [NE, 8, 128, 8, 512])
        self.cwu = cache("cwu", [NE, 8, 128, 8, 512])
        self.cwd = cache("cwd", [NE, 8, 128, 32, 128])
        self.cv_tok = {}

        self.x_tm = self.x_in.ap().rearrange("s (tb p) d -> s tb p d", p=128)
        self.y_tm = self.y_out.ap().rearrange("s (tb p) d -> s tb p d", p=128)
        self.xfm_t = self.xfm.ap().transpose([1, 0, 2])

        self.s_wt = Sem(nc, "s_wt")
        self.s_cv = Sem(nc, "s_cv")
        self.s_x = Sem(nc, "s_x")
        self.s_st = Sem(nc, "s_st")
        self.s_pe = Sem(nc, "s_pe")
        self.s_act = Sem(nc, "s_act")
        self.s_dve = Sem(nc, "s_dve")

        self.ones_f = nc.alloc_sbuf_tensor("ones_f", [128, 128], F32)
        self.ones_b = nc.alloc_sbuf_tensor("ones_b", [128, 128], BF16)
        self.ident_f = nc.alloc_sbuf_tensor("ident_f", [128, 128], F32)
        self.ident_b = nc.alloc_sbuf_tensor("ident_b", [128, 128], BF16)
        self.spacer = nc.alloc_sbuf_tensor("spcr", [1, 8], F32)
        self.eps = nc.alloc_sbuf_tensor("epsv", [128, 1], F32)
        self.aspacer = nc.alloc_sbuf_tensor("aspcr", [1, 8], F32)
        self.carry = nc.alloc_sbuf_tensor("carry", [128, 32], F32)

        self.store_gate = 0

    def emit(self, eng, fn, waits=(), inc=None, inc_by=1):
        self.ops[eng].append((fn, tuple(waits), inc, inc_by))
        if inc is not None:
            inc.n += inc_by
            return inc.n
        return None

    def barrier(self):
        ops = self.ops
        self.ops = {e: [] for e in self.ENGINES}
        if not any(ops.values()):
            return
        with self.nc.Block() as blk:
            for eng in self.ENGINES:
                lst = ops[eng]
                if not lst:
                    continue

                def body(E, lst=lst, eng=eng):
                    hw = self.hwm[eng]
                    for fn, waits, inc, k in lst:
                        for sem, val in waits:
                            if val <= 0 or val <= hw.get(sem.name, 0):
                                continue
                            E.wait_ge(sem.h, val)
                            hw[sem.name] = val
                        ins = fn(E)
                        if inc is not None:
                            ins.then_inc(inc.h, k)

                getattr(blk, eng)(body)

    def wstream(self, dst, fp32_ap, cache_ap, key, prime, waits):
        E = self.emit
        if prime:
            tok = E("gpsimd", lambda e, dst=dst, fp32_ap=fp32_ap: e.dma_start(
                out=dst, in_=fp32_ap), waits=waits, inc=self.s_wt, inc_by=16)
            stok = E("sync", lambda e, dst=dst, cache_ap=cache_ap: e.dma_start(
                out=cache_ap, in_=dst), waits=[(self.s_wt, tok)], inc=self.s_cv, inc_by=16)
            self.cache_tok[key] = stok
            return tok, stok
        tok = E("sync", lambda e, dst=dst, cache_ap=cache_ap: e.dma_start(
            out=dst, in_=cache_ap), waits=list(waits) + [(self.s_cv, self.cache_tok[key])],
            inc=self.s_wt, inc_by=16)
        return tok, 0

    def dve_spacer(self, inc=None):
        return self.emit("vector", lambda v: v.memset(self.spacer[:], 0.0), inc=inc)

    def ln_from_fm(self, gtok0, tlen, xn_dst, psf, ba, bb, load_dst, sqb, mstat,
                   xf_war=False, load_guard=(), cast_guard=(), stats_guard=()):
        E = self.emit
        s_x, s_act, s_dve, s_pe, s_st = self.s_x, self.s_act, self.s_dve, self.s_pe, self.s_st
        psa = psf[:, ba * 512:(ba + 1) * 512]
        psb_ = psf[:, bb * 512:(bb + 1) * 512]
        sq_free = [0, 0]
        stats_free = 0
        tkc_prev = 0
        for tt in range(tlen // 512):
            g0 = gtok0 + tt * 512
            sl = slice(tt * 512, (tt + 1) * 512)
            dst = load_dst(tt)
            w = [(s_st, self.store_gate)] + list(load_guard)
            if xf_war and tkc_prev:
                w.append((s_act, tkc_prev))
            tok = E("scalar", lambda e, g0=g0, dst=dst: e.dma_start(
                out=dst, in_=self.xfm_t[:, :, g0:g0 + 512]), waits=w, inc=s_x, inc_by=16)
            tkc = E("scalar", lambda e, sl=sl, dst=dst: e.copy(xn_dst[:, :, sl], dst),
                    waits=[(s_x, tok)] + list(cast_guard), inc=s_act)
            tkc_prev = tkc
            for dj in range(DJ):
                w2 = [(s_act, tkc)] if dj == 0 else []
                if dj == 0 and stats_free:
                    w2.append((s_dve, stats_free))
                if dj == 0:
                    w2 += list(stats_guard)
                E("tensor", lambda e, dj=dj, sl=sl, psa=psa: e.matmul(
                    psa, self.ones_b[:], xn_dst[:, dj, sl],
                    start=(dj == 0), stop=(dj == DJ - 1)), waits=w2)
            last = None
            for dj in range(DJ):
                w2 = [(s_act, tkc)]
                if sq_free[dj % 2]:
                    w2.append((s_pe, sq_free[dj % 2]))
                tk2 = E("scalar", lambda e, dj=dj, sl=sl: e.activation(
                    sqb[:, dj % 2, :], xn_dst[:, dj, sl], AF.Square), waits=w2, inc=s_act)
                last = E("tensor", lambda e, dj=dj, psb_=psb_: e.matmul(
                    psb_, self.ones_b[:], sqb[:, dj % 2, :],
                    start=(dj == 0), stop=(dj == DJ - 1)), waits=[(s_act, tk2)], inc=s_pe)
                sq_free[dj % 2] = last
            E("vector", lambda e, psa=psa: e.tensor_scalar_mul(mstat[:, 0, :], psa, 1.0 / D),
              waits=[(s_pe, last)])
            E("vector", lambda e, psb_=psb_: e.tensor_scalar_mul(mstat[:, 1, :], psb_, 1.0 / D))
            E("vector", lambda e: e.tensor_mul(mstat[:, 2, :], mstat[:, 0, :], mstat[:, 0, :]))
            tk3 = E("vector", lambda e: e.tensor_sub(mstat[:, 1, :], mstat[:, 1, :], mstat[:, 2, :]),
                    inc=s_dve)
            stats_free = tk3
            tk4 = E("scalar", lambda e: e.activation(mstat[:, 3, :], mstat[:, 1, :], AF.Sqrt,
                                                     bias=self.eps[:]), waits=[(s_dve, tk3)], inc=s_act)
            E("vector", lambda e: e.reciprocal(mstat[:, 1, :], mstat[:, 3, :]), waits=[(s_act, tk4)])
            self.dve_spacer()
            for dj in range(DJ):
                E("vector", lambda e, dj=dj, sl=sl: e.tensor_sub(
                    mstat[:, 2, :], xn_dst[:, dj, sl], mstat[:, 0, :]))
                E("vector", lambda e, dj=dj, sl=sl: e.tensor_mul(
                    xn_dst[:, dj, sl], mstat[:, 2, :], mstat[:, 1, :]),
                  inc=(s_dve if dj == DJ - 1 else None))
        return s_dve.n


    def phase_A(self):
        nc, E = self.nc, self.emit
        s_wt, s_x, s_st, s_pe, s_act, s_dve = (self.s_wt, self.s_x, self.s_st,
                                               self.s_pe, self.s_act, self.s_dve)
        with contextlib.ExitStack() as _es:
            xn = _es.enter_context(nc.sbuf_tensor("a_xn", [128, DJ, 512], BF16))
            x_in = _es.enter_context(nc.sbuf_tensor("a_xin", [128, FJ, 512], BF16))
            t1s = _es.enter_context(nc.sbuf_tensor("a_t1", [128, FJ, 512], BF16))
            zbuf = _es.enter_context(nc.sbuf_tensor("a_zb", [128, 2, 512], F32))
            abuf = _es.enter_context(nc.sbuf_tensor("a_ab", [128, 2, 512], F32))
            bbuf = _es.enter_context(nc.sbuf_tensor("a_bb", [128, 2, 512], F32))
            hbuf = _es.enter_context(nc.sbuf_tensor("a_hb", [128, 2, 512], F32))
            wA = _es.enter_context(nc.sbuf_tensor("a_wA", [128, 2, 8, 512], BF16))
            wz = _es.enter_context(nc.sbuf_tensor("a_wz", [128, 2, 32, 128], BF16))
            wh = _es.enter_context(nc.sbuf_tensor("a_wh", [128, 2, 32, 128], BF16))
            xtm = _es.enter_context(nc.sbuf_tensor("a_xtm", [128, 2, 1024], F32))
            sqb = _es.enter_context(nc.sbuf_tensor("a_sqb", [128, 2, 512], BF16))
            mstat = _es.enter_context(nc.sbuf_tensor("a_mst", [128, 6, 512], F32))
            xold = _es.enter_context(nc.sbuf_tensor("a_xold", [128, DJ, 512], F32))
            xnew = _es.enter_context(nc.sbuf_tensor("a_xnew", [128, 2, 512], F32))
            abias = _es.enter_context(nc.sbuf_tensor("a_bias", [128, 5, 32], F32))
            aboT = _es.enter_context(nc.sbuf_tensor("a_bo", [128, 8], F32))
            ps = _es.enter_context(nc.psum_tensor("a_ps", [128, 8, 512], F32))
            psf = ps.ap().rearrange("p a b -> p (a b)")
            for src, di in [(self.b1h, 0), (self.b2h, 1), (self.bzh, 2), (self.bhh, 4)]:
                E("sync", lambda e, src=src, di=di: e.dma_start(out=abias[:, di, :], in_=src.ap()),
                  inc=s_x, inc_by=16)
            tkb = E("sync", lambda e: e.dma_start(out=aboT[:], in_=self.boh.ap()), inc=s_x, inc_by=16)
            E("vector", lambda e: e.tensor_scalar_mul(abias[:, 3, :], abias[:, 2, :], -1.0),
              waits=[(s_x, s_x.n)], inc=s_dve)

            wA_tok = [0, 0]; wA_free = [0, 0]; wA_stok = [0, 0]
            wz_tok = [0, 0]; wz_free = [0, 0]; wz_stok = [0, 0]
            wh_tok = [0, 0]; wh_free = [0, 0]; wh_stok = [0, 0]
            ln_evac = [0, 0]
            xtm_pe = [0, 0]
            step3_last = 0
            st12_evac = [0, 0]
            st4_evac = [0, 0]
            st4_stored = [0, 0]
            st4_dma = [0, 0]
            stats_read = 0
            sq_free = [0, 0]

            for p in range(8):
                s = p // 4
                tok0 = (p % 4) * 512
                gtok0 = s * T + tok0
                xb = x_in.ap()[:, 0:8, :]
                for j in range(4):
                    par = j % 2
                    w = [(s_act, ln_evac[par])] if ln_evac[par] else []
                    if xtm_pe[par]:
                        w.append((s_pe, xtm_pe[par]))
                    tok = E("scalar", lambda e, s=s, j=j, par=par, tok0=tok0: e.dma_start(
                        out=xtm[:, par, :], in_=self.x_tm[s, tok0 // 128 + j]),
                        waits=w, inc=s_x, inc_by=16)
                    last = None
                    for dj in range(DJ):
                        w2 = [(s_x, tok)] if dj == 0 else []
                        if j == 0 and dj == 0:
                            for t_ in (st4_stored[0], st4_stored[1]):
                                if t_:
                                    w2.append((s_dve, t_))
                            for t_ in (st4_evac[0], st4_evac[1]):
                                if t_:
                                    w2.append((s_act, t_))
                        last = E("tensor", lambda e, par=par, dj=dj: e.transpose(
                            ps[:, 2 * par + dj // 4, (dj % 4) * 128:(dj % 4 + 1) * 128],
                            xtm[:, par, dj * 128:(dj + 1) * 128], self.ident_f[:]),
                            waits=w2, inc=(s_pe if dj == DJ - 1 else None))
                    xtm_pe[par] = last
                    wle = [(s_pe, last)]
                    if j == 0 and step3_last:
                        wle.append((s_pe, step3_last))
                    ln_evac[par] = E("scalar", lambda e, par=par, j=j, xb=xb: e.copy(
                        xb[:, :, j * 128:(j + 1) * 128],
                        psf[:, 2 * par * 512:(2 * par + 2) * 512].rearrange(
                            "p (dj t) -> p dj t", t=128)),
                        waits=wle, inc=s_act)
                xb_done = s_act.n
                for dj in range(DJ):
                    w2 = [(s_act, xb_done)] if dj == 0 else []
                    if dj == 0 and stats_read:
                        w2.append((s_dve, stats_read))
                    E("tensor", lambda e, dj=dj, xb=xb: e.matmul(
                        ps[:, 6, :], self.ones_b[:], xb[:, dj, :],
                        start=(dj == 0), stop=(dj == DJ - 1)), waits=w2)
                last = None
                for dj in range(DJ):
                    w2 = [(s_act, xb_done)]
                    if sq_free[dj % 2]:
                        w2.append((s_pe, sq_free[dj % 2]))
                    tk2 = E("scalar", lambda e, dj=dj, xb=xb: e.activation(
                        sqb[:, dj % 2, :], xb[:, dj, :], AF.Square), waits=w2, inc=s_act)
                    last = E("tensor", lambda e, dj=dj: e.matmul(
                        ps[:, 7, :], self.ones_b[:], sqb[:, dj % 2, :],
                        start=(dj == 0), stop=(dj == DJ - 1)), waits=[(s_act, tk2)], inc=s_pe)
                    sq_free[dj % 2] = last
                E("vector", lambda e: e.tensor_scalar_mul(mstat[:, 0, :], ps[:, 6, :], 1.0 / D),
                  waits=[(s_pe, last)])
                E("vector", lambda e: e.tensor_scalar_mul(mstat[:, 1, :], ps[:, 7, :], 1.0 / D))
                E("vector", lambda e: e.tensor_mul(mstat[:, 2, :], mstat[:, 0, :], mstat[:, 0, :]))
                tk3 = E("vector", lambda e: e.tensor_sub(mstat[:, 3, :], mstat[:, 1, :],
                                                         mstat[:, 2, :]), inc=s_dve)
                stats_read = tk3
                tk4 = E("scalar", lambda e: e.activation(mstat[:, 4, :], mstat[:, 3, :], AF.Sqrt,
                                                         bias=self.eps[:]), waits=[(s_dve, tk3)], inc=s_act)
                E("vector", lambda e: e.reciprocal(mstat[:, 5, :], mstat[:, 4, :]),
                  waits=[(s_act, tk4)])
                self.dve_spacer()
                for dj in range(DJ):
                    E("vector", lambda e, dj=dj, xb=xb: e.tensor_sub(
                        mstat[:, 2, :], xb[:, dj, :], mstat[:, 0, :]))
                    E("vector", lambda e, dj=dj: e.tensor_mul(
                        xn[:, dj, :], mstat[:, 2, :], mstat[:, 5, :]),
                      inc=(s_dve if dj == DJ - 1 else None))
                xn_tok = s_dve.n
                for wsrc, wsrcp, dstbuf, bslot in [(self.cW1, self.W1p, x_in, 0), (self.cW2, self.W2p, t1s, 1)]:
                    for mb in range(8):
                        buf = mb % 2
                        w = [(s_pe, wA_free[buf])] if wA_free[buf] else []
                        if wA_stok[buf]:
                            w.append((self.s_cv, wA_stok[buf]))
                        wA_tok[buf], wA_stok[buf] = self.wstream(
                            wA[:, buf, :, :], wsrcp.ap()[mb], wsrc.ap()[mb],
                            ("A", bslot, mb), p == 0, w)
                        for mo in range(4):
                            fo = mb * 4 + mo
                            par = fo % 2
                            last = None
                            for k in range(DJ):
                                w2 = []
                                if k == 0:
                                    w2 = [(s_wt, wA_tok[buf]), (s_dve, xn_tok)]
                                    if st12_evac[par]:
                                        w2.append((s_act, st12_evac[par]))
                                    if st4_stored[par]:
                                        w2.append((s_dve, st4_stored[par]))
                                last = E("tensor", lambda e, mo=mo, k=k, buf=buf, par=par: e.matmul(
                                    ps[:, par, :], wA[:, buf, k, mo * 128:(mo + 1) * 128],
                                    xn[:, k, :], start=(k == 0), stop=(k == DJ - 1)),
                                    waits=w2, inc=(s_pe if k == DJ - 1 else None))
                            st12_evac[par] = E("scalar", lambda e, fo=fo, par=par, dstbuf=dstbuf,
                                               bslot=bslot: e.activation(
                                dstbuf[:, fo, :], ps[:, par, :], AF.Silu,
                                bias=abias[:, bslot, fo:fo + 1]),
                                waits=[(s_pe, last)], inc=s_act)
                        wA_free[buf] = s_pe.n
                xin_done = s_act.n
                zread = [0, 0]
                bread = [0, 0]
                scand = [0, 0]
                for fo in range(FJ):
                    par = fo % 2
                    buf = fo % 2
                    w = [(s_pe, wz_free[buf])] if wz_free[buf] else []
                    if wz_stok[buf]:
                        w.append((self.s_cv, wz_stok[buf]))
                    wz_tok[buf], wz_stok[buf] = self.wstream(
                        wz[:, buf, :, :], self.Wzp.ap()[fo], self.cWz.ap()[fo],
                        ("Wz", fo), p == 0, w)
                    w = [(s_pe, wh_free[buf])] if wh_free[buf] else []
                    if wh_stok[buf]:
                        w.append((self.s_cv, wh_stok[buf]))
                    wh_tok[buf], wh_stok[buf] = self.wstream(
                        wh[:, buf, :, :], self.Whp.ap()[fo], self.cWh.ap()[fo],
                        ("Wh", fo), p == 0, w)
                    lastz = None
                    for k in range(FJ):
                        w2 = []
                        if k == 0:
                            w2 = [(s_wt, wz_tok[buf]), (s_act, xin_done)]
                            if zread[par]:
                                w2.append((s_act, zread[par]))
                        lastz = E("tensor", lambda e, k=k, buf=buf, par=par: e.matmul(
                            ps[:, par, :], wz[:, buf, k, :], x_in[:, k, :],
                            start=(k == 0), stop=(k == FJ - 1)),
                            waits=w2, inc=(s_pe if k == FJ - 1 else None))
                    lasth = None
                    for k in range(FJ):
                        w2 = []
                        if k == 0:
                            w2 = [(s_wt, wh_tok[buf]), (s_act, xin_done)]
                            if bread[par]:
                                w2.append((s_dve, bread[par]))
                        lasth = E("tensor", lambda e, k=k, buf=buf, par=par: e.matmul(
                            ps[:, 2 + par, :], wh[:, buf, k, :], x_in[:, k, :],
                            start=(k == 0), stop=(k == FJ - 1)),
                            waits=w2, inc=(s_pe if k == FJ - 1 else None))
                    wz_free[buf] = wh_free[buf] = s_pe.n
                    wz1 = [(s_pe, lastz)]
                    if bread[par]:
                        wz1.append((s_dve, bread[par]))
                    E("scalar", lambda e, fo=fo, par=par: e.activation(
                        zbuf[:, par, :], ps[:, par, :], AF.Sigmoid, bias=abias[:, 2, fo:fo + 1]),
                        waits=wz1)
                    wa1 = []
                    if scand[par]:
                        wa1.append((s_dve, scand[par]))
                    zread[par] = E("scalar", lambda e, fo=fo, par=par: e.activation(
                        abuf[:, par, :], ps[:, par, :], AF.Sigmoid, bias=abias[:, 3, fo:fo + 1],
                        scale=-1.0), waits=wa1, inc=s_act)
                    bread[par] = E("vector", lambda e, fo=fo, par=par: e.scalar_tensor_tensor(
                        bbuf[:, par, :], ps[:, 2 + par, :], abias[:, 4, fo:fo + 1], zbuf[:, par, :],
                        ALU.add, ALU.mult), waits=[(s_pe, lasth), (s_act, zread[par])], inc=s_dve)
                    init = 0.0 if (p % 4) == 0 else self.carry.ap()[:, fo:fo + 1]
                    scand[par] = E("vector", lambda e, par=par, init=init: e.tensor_tensor_scan(
                        hbuf[:, par, :], abuf[:, par, :], bbuf[:, par, :], init, ALU.mult, ALU.add),
                        inc=s_dve)
                    E("vector", lambda e, fo=fo, par=par: e.tensor_tensor(
                        t1s[:, fo, :], t1s[:, fo, :], hbuf[:, par, :], ALU.add),
                      waits=[(s_act, xin_done)])
                    E("vector", lambda e, fo=fo, par=par: e.tensor_copy(
                        self.carry[:, fo:fo + 1], hbuf[:, par, 511:512]), inc=s_dve)
                t1_done = s_dve.n
                step3_pe = s_pe.n
                for j in range(4):
                    par = j % 2
                    w = [(s_act, st4_evac[par])] if st4_evac[par] else []
                    if xtm_pe[par]:
                        w.append((s_pe, xtm_pe[par]))
                    tok = E("scalar", lambda e, s=s, j=j, par=par, tok0=tok0: e.dma_start(
                        out=xtm[:, par, :], in_=self.x_tm[s, tok0 // 128 + j]),
                        waits=w, inc=s_x, inc_by=16)
                    last = None
                    for dj in range(DJ):
                        w2 = [(s_x, tok)] if dj == 0 else []
                        if j == 0 and dj == 0:
                            w2.append((s_dve, t1_done))
                        last = E("tensor", lambda e, par=par, dj=dj: e.transpose(
                            ps[:, 2 + 2 * par + dj // 4, (dj % 4) * 128:(dj % 4 + 1) * 128],
                            xtm[:, par, dj * 128:(dj + 1) * 128], self.ident_f[:]),
                            waits=w2, inc=(s_pe if dj == DJ - 1 else None))
                    xtm_pe[par] = last
                    st4_evac[par] = E("scalar", lambda e, par=par, j=j: e.copy(
                        xold[:, :, j * 128:(j + 1) * 128],
                        psf[:, (2 + 2 * par) * 512:(4 + 2 * par) * 512].rearrange(
                            "p (dj t) -> p dj t", t=128)),
                        waits=[(s_pe, last)], inc=s_act)
                step3_last = step3_pe
                xold_done = s_act.n
                for ob in range(8):
                    buf = ob % 2
                    par = ob % 2
                    w = [(s_pe, wz_free[buf])] if wz_free[buf] else []
                    if wz_stok[buf]:
                        w.append((self.s_cv, wz_stok[buf]))
                    wz_tok[buf], wz_stok[buf] = self.wstream(
                        wz[:, buf, :, :], self.Wop.ap()[ob], self.cWo.ap()[ob],
                        ("Wo", ob), p == 0, w)
                    last = None
                    for k in range(FJ):
                        w2 = []
                        if k == 0:
                            w2 = [(s_wt, wz_tok[buf]), (s_dve, t1_done)]
                            if st4_stored[par]:
                                w2.append((s_dve, st4_stored[par]))
                        last = E("tensor", lambda e, k=k, buf=buf, par=par: e.matmul(
                            ps[:, par, :], wz[:, buf, k, :], t1s[:, k, :],
                            start=(k == 0), stop=(k == FJ - 1)),
                            waits=w2, inc=(s_pe if k == FJ - 1 else None))
                    wz_free[buf] = s_pe.n
                    wv = [(s_pe, last), (s_act, xold_done)]
                    if st4_dma[par]:
                        wv.append((s_st, st4_dma[par]))
                    st4_stored[par] = E("vector", lambda e, ob=ob, par=par: e.scalar_tensor_tensor(
                        xnew[:, par, :], ps[:, par, :], aboT[:, ob:ob + 1], xold[:, ob, :],
                        ALU.add, ALU.add), waits=wv, inc=s_dve)
                    st4_dma[par] = E("scalar", lambda e, ob=ob, par=par, gtok0=gtok0: e.dma_start(
                        out=self.xfm.ap()[ob, :, gtok0:gtok0 + 512], in_=xnew[:, par, :]),
                        waits=[(s_dve, st4_stored[par])], inc=s_st, inc_by=16)
            self.barrier()

    def phase_F(self, tag, write_output):
        nc, E = self.nc, self.emit
        s_wt, s_x, s_st, s_pe, s_act, s_dve = (self.s_wt, self.s_x, self.s_st,
                                               self.s_pe, self.s_act, self.s_dve)
        with contextlib.ExitStack() as _es:
            xn = _es.enter_context(nc.sbuf_tensor(tag + "_xn", [128, DJ, 1024], BF16))
            acc = _es.enter_context(nc.sbuf_tensor(tag + "_acc", [128, DJ, 1024], F32))
            hf = _es.enter_context(nc.sbuf_tensor(tag + "_h", [128, FJ, 1024], BF16))
            wg = _es.enter_context(nc.sbuf_tensor(tag + "_wg", [128, 2, 8, 512], BF16))
            wu = _es.enter_context(nc.sbuf_tensor(tag + "_wu", [128, 2, 8, 512], BF16))
            wd = _es.enter_context(nc.sbuf_tensor(tag + "_wd", [128, 2, 32, 128], BF16))
            wrg = _es.enter_context(nc.sbuf_tensor(tag + "_wrg", [128, 8, 16], BF16))
            sqb = _es.enter_context(nc.sbuf_tensor(tag + "_sqb", [128, 2, 512], BF16))
            mstat = _es.enter_context(nc.sbuf_tensor(tag + "_mst", [128, 4, 512], F32))
            ssb = _es.enter_context(nc.sbuf_tensor(tag + "_s", [128, 2, 1024], F32))
            cwtm = _es.enter_context(nc.sbuf_tensor(tag + "_cwt", [128, 18, 8], F32))
            cwall = _es.enter_context(nc.sbuf_tensor(tag + "_cwa", [128, 128], F32))
            cwT = _es.enter_context(nc.sbuf_tensor(tag + "_cwT", [64, 128], F32))
            cwrow = _es.enter_context(nc.sbuf_tensor(tag + "_cwr", [1, 1024], F32))
            cwb = _es.enter_context(nc.sbuf_tensor(tag + "_cwb", [128, 1024], F32))
            ftmp = _es.enter_context(nc.sbuf_tensor(tag + "_tmp", [128, 1024], F32))
            st1 = _es.enter_context(nc.sbuf_tensor(tag + "_st1", [128, 8, 1], F32))
            ytm = _es.enter_context(nc.sbuf_tensor(tag + "_ytm", [128, 2, 1024], F32))
            ps = _es.enter_context(nc.psum_tensor(tag + "_ps", [128, 8, 512], F32))
            psf = ps.ap().rearrange("p a b -> p (a b)")
            tkw = E("gpsimd", lambda e: e.dma_start(out=wrg[:], in_=self.wrgp.ap()),
                    inc=s_wt, inc_by=16)
            wg_tok = [0, 0]; wg_free = [0, 0]; wg_stok = [0, 0]
            wu_tok = [0, 0]; wu_free = [0, 0]; wu_stok = [0, 0]
            wd_tok = [0, 0]; wd_free = [0, 0]; wd_stok = [0, 0]
            silu_tok = [0, 0]
            hmul_tok = [0, 0]
            dn_evac = 0
            dn_par = [0, 0]
            bc_read = 0
            cwrow_pe = 0
            ytm_dma = [0, 0]
            ytm_evac = [0, 0]
            xn_reader = 0
            wb_pe = 0
            last_tkr = 0
            for c in range(4):
                prime = (tag == "f1" and c == 0)
                gtok0 = c * 1024
                seq = c // 2
                lg = [(s_pe, wb_pe)] if wb_pe else []
                cg = [(s_pe, xn_reader)] if xn_reader else []
                sg = [(s_dve, dn_evac)] if dn_evac else []
                xn_tok = self.ln_from_fm(gtok0, 1024, xn, psf, 6, 7,
                                         lambda tt: acc[:, :, tt * 512:(tt + 1) * 512],
                                         sqb, mstat, load_guard=lg, cast_guard=cg,
                                         stats_guard=sg)
                for ti in range(8):
                    for k in range(DJ):
                        w = [(s_dve, xn_tok), (s_wt, tkw)] if (k == 0 and ti == 0) else []
                        if k == 0 and ti == 0 and dn_evac:
                            w.append((s_dve, dn_evac))
                        E("tensor", lambda e, ti=ti, k=k: e.matmul(
                            ps[:, 4, ti * 16:ti * 16 + 16], xn[:, k, ti * 128:(ti + 1) * 128],
                            wrg[:, k, :], start=(ti == 0 and k == 0), stop=(k == DJ - 1)),
                            waits=w, inc=(s_pe if (k == DJ - 1 and ti == 7) else None))
                ract = E("scalar", lambda e: e.copy(
                    cwtm[:, 0:16, :], ps[:, 4, 0:128].rearrange("p (a b) -> p a b", b=8)),
                    waits=[(s_pe, s_pe.n)], inc=s_act)
                cw_fin = 0
                tke2 = 0
                dbg_this = self.debug_xfm and tag == "f1" and c == 0
                for ti in range(8):
                    lr = cwtm.ap()[:, 2 * ti, :]
                    lg = cwtm.ap()[:, 2 * ti + 1, :]
                    er = cwtm.ap()[:, 16, :]
                    tprod = cwtm.ap()[:, 17, :]
                    w = [(s_act, ract)] if ti == 0 else [(s_act, tke2)]
                    E("vector", lambda e, lg=lg: e.reduce_max(st1[:, 1, :], lg, axis=AXX,
                                                              negate=True), waits=w)
                    E("vector", lambda e, lr=lr: e.reduce_max(st1[:, 0, :], lr, axis=AXX,
                                                              negate=True))
                    self.dve_spacer()
                    E("vector", lambda e: e.tensor_scalar_mul(st1[:, 2, :], st1[:, 1, :], 10.0))
                    self.dve_spacer()
                    tk = self.dve_spacer(inc=s_dve)
                    E("scalar", lambda e, lr=lr, er=er: e.activation(
                        er, lr, AF.Exp, bias=st1[:, 0, :], accum_out=st1[:, 3, :]),
                        waits=[(s_dve, tk)])
                    E("scalar", lambda e, lg=lg: e.activation(
                        lg, lg, AF.Exp, bias=st1[:, 2, :], scale=10.0, accum_out=st1[:, 4, :]))
                    tke2 = E("scalar", lambda e: e.copy(self.aspacer[:], self.aspacer[:]),
                             inc=s_act)
                    if dbg_this and ti == 0:
                        E("vector", lambda e, er=er: e.tensor_copy(self.dbgf2[:, 0:8], er),
                          waits=[(s_act, tke2)])
                        E("vector", lambda e, lg=lg: e.tensor_copy(self.dbgf2[:, 8:16], lg))
                        E("vector", lambda e, lr=lr: e.tensor_copy(self.dbgf2[:, 16:24], lr))
                    E("vector", lambda e, lg=lg, er=er, tprod=tprod: e.tensor_mul(tprod, er, lg),
                      waits=[(s_act, tke2)])
                    E("vector", lambda e: e.tensor_mul(st1[:, 6, :], st1[:, 3, :], st1[:, 4, :]))
                    self.dve_spacer()
                    E("vector", lambda e, tprod=tprod: e.reduce_sum(st1[:, 5, :], tprod, axis=AXX))
                    self.dve_spacer()
                    self.dve_spacer()
                    E("vector", lambda e: e.scalar_tensor_tensor(
                        st1[:, 6, :], st1[:, 6, :], 1e-9, st1[:, 5, :], ALU.mult, ALU.add))
                    self.dve_spacer()
                    self.dve_spacer()
                    E("vector", lambda e: e.reciprocal(st1[:, 7, :], st1[:, 6, :]))
                    self.dve_spacer()
                    self.dve_spacer()
                    if dbg_this and ti == 0:
                        E("vector", lambda e: e.tensor_copy(
                            self.dbgf2[:, 24:32], st1[:, :, 0].transpose([0, 1]) if False else st1.ap().rearrange("p a b -> p (a b)")))
                        E("vector", lambda e, tprod=tprod: e.tensor_copy(self.dbgf2[:, 32:40], tprod))
                    cw_fin = E("vector", lambda e, ti=ti, tprod=tprod: e.tensor_scalar_mul(
                        cwall[:, ti * 8:(ti + 1) * 8], tprod, st1[:, 7, :]), inc=s_dve)
                    if dbg_this and ti == 0:
                        tdd = E("vector", lambda e: e.tensor_copy(
                            self.dbgf2[:, 40:48], cwall[:, 0:8]), inc=s_dve)
                        E("sync", lambda e: e.dma_start(out=self.dbg_t0.ap(), in_=self.dbgf2[:]),
                          waits=[(s_dve, tdd)], inc=s_st, inc_by=16)
                if dbg_this:
                    tda = E("vector", lambda e: e.tensor_copy(self.dbgf[:, 0:128], cwall[:]),
                            waits=[(s_dve, cw_fin)], inc=s_dve)
                    E("sync", lambda e: e.dma_start(out=self.dbg_s.ap()[:, 0:128], in_=self.dbgf[:, 0:128]),
                      waits=[(s_dve, tda)], inc=s_st, inc_by=16)
                wt2 = [(s_dve, cw_fin)]
                if cwrow_pe:
                    wt2.append((s_pe, cwrow_pe))
                if dn_evac:
                    wt2.append((s_dve, dn_evac))
                tkt = E("tensor", lambda e: e.transpose(ps[:, 5, 0:128], cwall[:], self.ident_f[:]),
                        waits=wt2, inc=s_pe)
                wcd = [(s_pe, tkt)]
                if last_tkr:
                    wcd.append((s_x, last_tkr))
                cw_done = E("scalar", lambda e: e.copy(cwT[:], ps[0:64, 5, 0:128]),
                            waits=wcd, inc=s_act)
                for ex in range(NE):
                    wv = [(s_act, cw_done)]
                    if cwrow_pe:
                        wv.append((s_pe, cwrow_pe))
                    tkr = None
                    for ti in range(8):
                        tkr = E("sync", lambda e, ex=ex, ti=ti: e.dma_start(
                            out=cwrow[:, ti * 128:(ti + 1) * 128],
                            in_=cwT[ti * 8 + ex:ti * 8 + ex + 1, :]),
                            waits=(wv if ti == 0 else []), inc=s_x, inc_by=16)
                    last_tkr = tkr
                    last = None
                    for q in range(4):
                        w = []
                        if q == 0:
                            w = [(s_x, tkr)]
                            if bc_read:
                                w.append((s_dve, bc_read))
                            if dn_evac:
                                w.append((s_dve, dn_evac))
                        last = E("tensor", lambda e, q=q: e.matmul(
                            psf[:, 4 * 512 + q * 256:4 * 512 + (q + 1) * 256],
                            self.ones_f[0:1, :], cwrow[:, q * 256:(q + 1) * 256],
                            start=True, stop=True), waits=w, inc=(s_pe if q == 3 else None))
                    cwrow_pe = last
                    bc_read = E("vector", lambda e: e.tensor_copy(
                        cwb[:], psf[:, 4 * 512:6 * 512]), waits=[(s_pe, last)], inc=s_dve)
                    if self.debug_xfm and tag == "f1" and c == 0 and ex == 0:
                        E("sync", lambda e: e.dma_start(out=self.dbg_cwT.ap(), in_=cwT[:]),
                          waits=[(s_act, cw_done)], inc=s_st, inc_by=16)
                        E("sync", lambda e: e.dma_start(out=self.dbg_cwb.ap(), in_=cwb[:]),
                          waits=[(s_dve, bc_read)], inc=s_st, inc_by=16)
                        tdx = E("vector", lambda e: e.tensor_copy(self.dbgf[:], xn[:, 0, :]),
                                waits=[(s_dve, xn_tok)], inc=s_dve)
                        E("sync", lambda e: e.dma_start(out=self.dbg_xn.ap(), in_=self.dbgf[:]),
                          waits=[(s_dve, tdx)], inc=s_st, inc_by=16)
                    for fo in range(FJ):
                        mb, mo = fo // 4, fo % 4
                        buf = mb % 2
                        fpar = fo % 2
                        if mo == 0:
                            w = [(s_pe, wg_free[buf])] if wg_free[buf] else []
                            if wg_stok[buf]:
                                w.append((self.s_cv, wg_stok[buf]))
                            wg_tok[buf], wg_stok[buf] = self.wstream(
                                wg[:, buf, :, :], self.wgp.ap()[ex, mb], self.cwg.ap()[ex, mb],
                                ("wg", ex, mb), prime, w)
                            w = [(s_pe, wu_free[buf])] if wu_free[buf] else []
                            if wu_stok[buf]:
                                w.append((self.s_cv, wu_stok[buf]))
                            wu_tok[buf], wu_stok[buf] = self.wstream(
                                wu[:, buf, :, :], self.wup.ap()[ex, mb], self.cwu.ap()[ex, mb],
                                ("wu", ex, mb), prime, w)
                        gb = 2 * fpar
                        ub = 4 + 2 * fpar
                        lastg = None
                        for k in range(DJ):
                            for tt in range(2):
                                w2 = []
                                if k == 0 and tt == 0:
                                    w2 = [(s_wt, wg_tok[buf]), (s_dve, xn_tok)]
                                    if silu_tok[fpar]:
                                        w2.append((s_act, silu_tok[fpar]))
                                    if max(ytm_evac):
                                        w2.append((s_act, max(ytm_evac)))
                                lastg = E("tensor", lambda e, mo=mo, k=k, tt=tt, buf=buf, gb=gb: e.matmul(
                                    ps[:, gb + tt, :], wg[:, buf, k, mo * 128:(mo + 1) * 128],
                                    xn[:, k, tt * 512:(tt + 1) * 512],
                                    start=(k == 0), stop=(k == DJ - 1)),
                                    waits=w2, inc=(s_pe if (k == DJ - 1 and tt == 1) else None))
                        lastu = None
                        for k in range(DJ):
                            for tt in range(2):
                                w2 = []
                                if k == 0 and tt == 0:
                                    w2 = [(s_wt, wu_tok[buf])]
                                    if hmul_tok[fpar]:
                                        w2.append((s_dve, hmul_tok[fpar]))
                                lastu = E("tensor", lambda e, mo=mo, k=k, tt=tt, buf=buf, ub=ub: e.matmul(
                                    ps[:, ub + tt, :], wu[:, buf, k, mo * 128:(mo + 1) * 128],
                                    xn[:, k, tt * 512:(tt + 1) * 512],
                                    start=(k == 0), stop=(k == DJ - 1)),
                                    waits=w2, inc=(s_pe if (k == DJ - 1 and tt == 1) else None))
                        if mo == 3:
                            wg_free[buf] = wu_free[buf] = s_pe.n
                        wsi = [(s_pe, lastg)]
                        if hmul_tok[fpar]:
                            wsi.append((s_dve, hmul_tok[fpar]))
                        silu_tok[fpar] = E("scalar", lambda e, fpar=fpar, gb=gb: e.activation(
                            ssb[:, fpar, :], psf[:, gb * 512:(gb + 2) * 512], AF.Silu),
                            waits=wsi, inc=s_act)
                        hmul_tok[fpar] = E("vector", lambda e, fo=fo, fpar=fpar, ub=ub: e.tensor_tensor(
                            hf[:, fo, :], ssb[:, fpar, :], psf[:, ub * 512:(ub + 2) * 512], ALU.mult),
                            waits=[(s_act, silu_tok[fpar]), (s_pe, lastu)], inc=s_dve)
                    h_done = max(hmul_tok)
                    if self.debug_xfm and tag == "f1" and c == 0 and ex == 0:
                        tdh = E("vector", lambda e: e.tensor_copy(self.dbgf[:], hf[:, 0, :]),
                                waits=[(s_dve, h_done)], inc=s_dve)
                        E("sync", lambda e: e.dma_start(out=self.dbg_h.ap(), in_=self.dbgf[:]),
                          waits=[(s_dve, tdh)], inc=s_st, inc_by=16)
                    for o in range(8):
                        buf = o % 2
                        w = [(s_pe, wd_free[buf])] if wd_free[buf] else []
                        if wd_stok[buf]:
                            w.append((self.s_cv, wd_stok[buf]))
                        wd_tok[buf], wd_stok[buf] = self.wstream(
                            wd[:, buf, :, :], self.wdp.ap()[ex, o], self.cwd.ap()[ex, o],
                            ("wd", ex, o), prime, w)
                        opar = o % 2
                        db = 4 + 2 * opar
                        last = None
                        for k in range(FJ):
                            for tt in range(2):
                                w2 = []
                                if k == 0 and tt == 0:
                                    w2 = [(s_wt, wd_tok[buf]), (s_dve, h_done)]
                                    if dn_par[opar]:
                                        w2.append((s_dve, dn_par[opar]))
                                last = E("tensor", lambda e, k=k, tt=tt, buf=buf, db=db: e.matmul(
                                    ps[:, db + tt, :], wd[:, buf, k, :],
                                    hf[:, k, tt * 512:(tt + 1) * 512],
                                    start=(k == 0), stop=(k == FJ - 1)),
                                    waits=w2, inc=(s_pe if (k == FJ - 1 and tt == 1) else None))
                        wd_free[buf] = s_pe.n
                        E("vector", lambda e, db=db: e.tensor_mul(
                            ftmp[:], psf[:, db * 512:(db + 2) * 512], cwb[:]),
                          waits=[(s_pe, last), (s_dve, bc_read)])
                        dn_evac = E("vector", lambda e, o=o: e.tensor_tensor(
                            acc[:, o, :], acc[:, o, :], ftmp[:], ALU.add), inc=s_dve)
                        dn_par[opar] = dn_evac
                xn_reader = s_pe.n
                if not write_output:
                    for dj in range(DJ):
                        E("scalar", lambda e, dj=dj, gtok0=gtok0: e.dma_start(
                            out=self.xfm.ap()[dj, :, gtok0:gtok0 + 1024], in_=acc[:, dj, :]),
                          waits=[(s_dve, dn_evac)], inc=s_st, inc_by=16)
                else:
                    for j in range(8):
                        par = j % 2
                        last = None
                        for dj in range(DJ):
                            w2 = [(s_dve, dn_evac)] if dj == 0 else []
                            if dj == 0 and ytm_evac[par]:
                                w2.append((s_act, ytm_evac[par]))
                            last = E("tensor", lambda e, j=j, dj=dj, par=par: e.transpose(
                                ps[:, 2 * par + dj // 4, (dj % 4) * 128:(dj % 4 + 1) * 128],
                                acc[:, dj, j * 128:(j + 1) * 128], self.ident_f[:]),
                                waits=w2, inc=(s_pe if dj == DJ - 1 else None))
                        wv = [(s_pe, last)]
                        if ytm_dma[par]:
                            wv.append((s_st, ytm_dma[par]))
                        ytm_evac[par] = E("scalar", lambda e, par=par: e.copy(
                            ytm[:, par, :].rearrange("p (dj t) -> p dj t", t=128),
                            psf[:, 2 * par * 512:(2 * par + 2) * 512].rearrange(
                                "p (dj t) -> p dj t", t=128)),
                            waits=wv, inc=s_act)
                        tb = (gtok0 % T) // 128 + j
                        ytm_dma[par] = E("scalar", lambda e, seq=seq, tb=tb, par=par: e.dma_start(
                            out=self.y_tm[seq, tb], in_=ytm[:, par, :]),
                            waits=[(s_act, ytm_evac[par])], inc=s_st, inc_by=16)
                wb_pe = s_pe.n
                self.store_gate = s_st.n
            self.barrier()

    def phase_C(self):
        nc, E = self.nc, self.emit
        s_wt, s_x, s_st, s_pe, s_act, s_dve = (self.s_wt, self.s_x, self.s_st,
                                               self.s_pe, self.s_act, self.s_dve)
        with contextlib.ExitStack() as _es:
            xn = _es.enter_context(nc.sbuf_tensor("c_xn", [128, DJ, T], BF16))
            qsb = _es.enter_context(nc.sbuf_tensor("c_q", [128, 8, T], BF16))
            kb = _es.enter_context(nc.sbuf_tensor("c_kb", [128, 2, T], BF16))
            vb = _es.enter_context(nc.sbuf_tensor("c_vb", [128, 2, T], BF16))
            vt = _es.enter_context(nc.sbuf_tensor("c_vt", [128, 2, 16, 128], BF16))
            klat = _es.enter_context(nc.sbuf_tensor("c_klat", [64, T], BF16))
            vlat = _es.enter_context(nc.sbuf_tensor("c_vlat", [64, T], BF16))
            Ssb = _es.enter_context(nc.sbuf_tensor("c_S", [128, 2, T], F32))
            Pb = _es.enter_context(nc.sbuf_tensor("c_P", [128, 2, T], BF16))
            PT = _es.enter_context(nc.sbuf_tensor("c_PT", [128, 16, 128], BF16))
            yall = _es.enter_context(nc.sbuf_tensor("c_y", [128, 8, 512], BF16))
            woa = _es.enter_context(nc.sbuf_tensor("c_woa", [128, 8, 8, 128], BF16))
            wqs = _es.enter_context(nc.sbuf_tensor("c_wq", [128, 2, 8, 128], BF16))
            wkd = _es.enter_context(nc.sbuf_tensor("c_wkd", [128, 8, 64], BF16))
            wvd = _es.enter_context(nc.sbuf_tensor("c_wvd", [128, 8, 64], BF16))
            kup = _es.enter_context(nc.sbuf_tensor("c_kup", [64, 128], BF16))
            vup = _es.enter_context(nc.sbuf_tensor("c_vup", [64, 128], BF16))
            masks = _es.enter_context(nc.sbuf_tensor("c_mask", [128, 4, 512], F32))
            sqb = _es.enter_context(nc.sbuf_tensor("c_sqb", [128, 2, 512], BF16))
            mstat = _es.enter_context(nc.sbuf_tensor("c_mst", [128, 4, 512], F32))
            xold = _es.enter_context(nc.sbuf_tensor("c_xold", [128, DJ, 512], F32))
            xnw = _es.enter_context(nc.sbuf_tensor("c_xnw", [128, 2, 512], F32))
            st1 = _es.enter_context(nc.sbuf_tensor("c_st1", [128, 4, 1], F32))
            ps = _es.enter_context(nc.psum_tensor("c_ps", [128, 6, 512], F32))
            psb = _es.enter_context(nc.psum_tensor("c_psb", [128, 2, 1024], BF16))
            psf = ps.ap().rearrange("p a b -> p (a b)")
            tk_m = E("sync", lambda e: e.dma_start(out=masks[:], in_=self.maskh.ap().transpose(
                [1, 0, 2])), inc=s_x, inc_by=16)
            tk_woa = E("gpsimd", lambda e: e.dma_start(out=woa[:], in_=self.woap.ap().transpose(
                [1, 0, 2, 3])), inc=s_wt, inc_by=16)
            tk_wkd = E("gpsimd", lambda e: e.dma_start(out=wkd[:], in_=self.wkdp.ap()),
                       inc=s_wt, inc_by=16)
            tk_wvd = E("gpsimd", lambda e: e.dma_start(out=wvd[:], in_=self.wvdp.ap()),
                       inc=s_wt, inc_by=16)
            E("gpsimd", lambda e: e.dma_start(out=kup[0:32, :], in_=self.wkup.ap()),
              inc=s_wt, inc_by=16)
            tk_kup = E("gpsimd", lambda e: e.dma_start(out=kup[32:64, :], in_=self.wkup.ap()),
                       inc=s_wt, inc_by=16)
            E("gpsimd", lambda e: e.dma_start(out=vup[0:32, :], in_=self.wvup.ap()),
              inc=s_wt, inc_by=16)
            tk_vup = E("gpsimd", lambda e: e.dma_start(out=vup[32:64, :], in_=self.wvup.ap()),
                       inc=s_wt, inc_by=16)
            wq_tok = [0, 0]; wq_free = [0, 0]
            for seq in range(NSEQ):
                self.barrier()
                xn_tok = self.ln_from_fm(seq * T, T, xn, psf, 4, 5, lambda tt: xold[:], sqb, mstat, xf_war=True)
                self.barrier()
                qev = 0
                for hb in range(8):
                    buf = hb % 2
                    w = [(s_pe, wq_free[buf])] if wq_free[buf] else []
                    wq_tok[buf] = E("gpsimd", lambda e, hb=hb, buf=buf: e.dma_start(
                        out=wqs[:, buf, :, :], in_=self.wqp.ap()[hb]), waits=w, inc=s_wt, inc_by=16)
                    last = None
                    for k in range(DJ):
                        for tt in range(4):
                            w2 = []
                            if k == 0 and tt == 0:
                                w2 = [(s_wt, wq_tok[buf]), (s_dve, xn_tok)]
                                if qev:
                                    w2.append((s_act, qev))
                            last = E("tensor", lambda e, k=k, tt=tt, buf=buf: e.matmul(
                                ps[:, tt, :], wqs[:, buf, k, :], xn[:, k, tt * 512:(tt + 1) * 512],
                                start=(k == 0), stop=(k == DJ - 1)),
                                waits=w2, inc=(s_pe if (k == DJ - 1 and tt == 3) else None))
                    wq_free[buf] = s_pe.n
                    qev = E("scalar", lambda e, hb=hb: e.copy(
                        qsb[:, hb, :], psf[:, 0:4 * 512]), waits=[(s_pe, last)], inc=s_act)
                klev = 0
                for wsrc, wtok, ldst in [(wkd, tk_wkd, klat), (wvd, tk_wvd, vlat)]:
                    for tt in range(4):
                        last = None
                        for k in range(DJ):
                            w2 = []
                            if k == 0:
                                w2 = [(s_wt, wtok), (s_dve, xn_tok)]
                                if klev:
                                    w2.append((s_act, klev))
                            last = E("tensor", lambda e, k=k, tt=tt, wsrc=wsrc: e.matmul(
                                ps[0:64, 4, :], wsrc[:, k, :], xn[:, k, tt * 512:(tt + 1) * 512],
                                start=(k == 0), stop=(k == DJ - 1)),
                                waits=w2, inc=(s_pe if k == DJ - 1 else None))
                        klev = E("scalar", lambda e, tt=tt, ldst=ldst: e.copy(
                            ldst[:, tt * 512:(tt + 1) * 512], ps[0:64, 4, :]),
                            waits=[(s_pe, last)], inc=s_act)
                upev = 0
                for usrc, utok, latb, dstb in [(kup, tk_kup, klat, kb),
                                               (vup, tk_vup, vlat, vb)]:
                    for g in range(2):
                        for tt in range(4):
                            w2 = [(s_wt, utok), (s_act, klev)]
                            if upev:
                                w2.append((s_act, upev))
                            last = E("tensor", lambda e, tt=tt, usrc=usrc, latb=latb, g=g: e.matmul(
                                ps[:, 5, :], usrc[g * 32:(g + 1) * 32, :],
                                latb[g * 32:(g + 1) * 32, tt * 512:(tt + 1) * 512],
                                start=True, stop=True), waits=w2, inc=s_pe)
                            upev = E("scalar", lambda e, g=g, tt=tt, dstb=dstb: e.copy(
                                dstb[:, g, tt * 512:(tt + 1) * 512], ps[:, 5, :]),
                                waits=[(s_pe, last)], inc=s_act)
                vtev = 0
                for g in range(2):
                    for hw in range(2):
                        last = None
                        for j2 in range(hw * 8, hw * 8 + 8):
                            w2 = []
                            if j2 % 8 == 0:
                                w2 = [(s_act, upev)]
                                if vtev:
                                    w2.append((s_act, vtev))
                            last = E("tensor", lambda e, g=g, j2=j2, hw=hw: e.transpose(
                                psb[:, hw, (j2 % 8) * 128:(j2 % 8 + 1) * 128],
                                vb[:, g, j2 * 128:(j2 + 1) * 128], self.ident_b[:]),
                                waits=w2, inc=(s_pe if j2 % 8 == 7 else None))
                        vtev = E("scalar", lambda e, g=g, hw=hw: e.copy(
                            vt[:, g, hw * 8:hw * 8 + 8, :],
                            psb[:, hw, :].rearrange("p (a b) -> p a b", b=128)),
                            waits=[(s_pe, last)], inc=s_act)
                self.barrier()
                sc_prev = 0
                y_ev_par = [0, 0]
                pb_done = [0, 0]
                pt_evac = 0
                y_evac = 0
                yproj = 0
                prj_dve = 0
                xnw_dma = [0, 0]
                for G in range(4):
                    g0 = seq * T + G * 512
                    wv = [(s_dve, prj_dve)] if prj_dve else []
                    wv.append((s_st, self.store_gate))
                    xold_tok = E("scalar", lambda e, g0=g0: e.dma_start(
                        out=xold[:], in_=self.xfm_t[:, :, g0:g0 + 512]),
                        waits=wv, inc=s_x, inc_by=16)
                    for i in range(4 * G, 4 * G + 4):
                        jmax = i // 4
                        length = (i + 1) * 128
                        mw = length - jmax * 512
                        for h in range(8):
                            par = (i * 8 + h) % 2
                            g = h // 4
                            last = None
                            for j in range(jmax + 1):
                                w2 = []
                                if j == 0 and sc_prev:
                                    w2.append((s_act, sc_prev))
                                nn = 512 if j < jmax else mw
                                last = E("tensor", lambda e, i=i, h=h, j=j, g=g, nn=nn: e.matmul(
                                    ps[:, j, 0:nn], qsb[:, h, i * 128:(i + 1) * 128],
                                    kb[:, g, j * 512:j * 512 + nn], start=True, stop=True),
                                    waits=w2, inc=(s_pe if j == jmax else None))
                            wv2 = [(s_pe, last), (s_x, tk_m)]
                            if pb_done[par]:
                                wv2.append((s_dve, pb_done[par]))
                            sc_prev = E("scalar", lambda e, par=par, length=length: e.activation(
                                Ssb[:, par, 0:length], psf[:, 0:length], AF.Copy, scale=SCALE),
                                waits=wv2, inc=s_act)
                            E("vector", lambda e, par=par, i=i, jmax=jmax, mw=mw: e.tensor_tensor(
                                Ssb[:, par, jmax * 512:jmax * 512 + mw],
                                Ssb[:, par, jmax * 512:jmax * 512 + mw],
                                masks[:, i % 4, 0:mw], ALU.add), waits=[(s_act, sc_prev)])
                            self.dve_spacer()
                            E("vector", lambda e, par=par, length=length: e.reduce_max(
                                st1[:, 0, :], Ssb[:, par, 0:length], axis=AXX, negate=True))
                            self.dve_spacer()
                            tks = self.dve_spacer(inc=s_dve)
                            E("scalar", lambda e, par=par, length=length: e.activation(
                                Ssb[:, par, 0:length], Ssb[:, par, 0:length], AF.Exp,
                                bias=st1[:, 0, :], accum_out=st1[:, 1, :]),
                                waits=[(s_dve, tks)])
                            tke = E("scalar", lambda e: e.copy(self.aspacer[:], self.aspacer[:]),
                                    inc=s_act)
                            E("vector", lambda e: e.reciprocal(st1[:, 2, :], st1[:, 1, :]),
                              waits=[(s_act, tke)])
                            self.dve_spacer()
                            self.dve_spacer()
                            pb_done[par] = E("vector", lambda e, par=par, length=length:
                                             e.tensor_scalar_mul(
                                Pb[:, par, 0:length], Ssb[:, par, 0:length], st1[:, 2, :]),
                                inc=s_dve)
                            ntile = i + 1
                            for wv3 in range((ntile + 7) // 8):
                                lo = wv3 * 8
                                hi = min(lo + 8, ntile)
                                last = None
                                for j2 in range(lo, hi):
                                    w2 = []
                                    if j2 == lo:
                                        w2 = [(s_dve, pb_done[par])]
                                        if pt_evac:
                                            w2.append((s_act, pt_evac))
                                    last = E("tensor", lambda e, par=par, j2=j2, wv3=wv3: e.transpose(
                                        psb[:, wv3 % 2, (j2 % 8) * 128:(j2 % 8 + 1) * 128],
                                        Pb[:, par, j2 * 128:(j2 + 1) * 128], self.ident_b[:]),
                                        waits=w2, inc=(s_pe if j2 == hi - 1 else None))
                                pt_evac = E("scalar", lambda e, lo=lo, hi=hi, wv3=wv3: e.copy(
                                    PT[:, lo:hi, :], psb[:, wv3 % 2, 0:(hi - lo) * 128].rearrange(
                                        "p (a b) -> p a b", b=128)), waits=[(s_pe, last)], inc=s_act)
                            ypar = (i * 8 + h) % 2
                            yb = 4 + ypar
                            last = None
                            for j2 in range(ntile):
                                w2 = []
                                if j2 == 0:
                                    w2 = [(s_act, pt_evac), (s_act, vtev)]
                                    if y_ev_par[ypar]:
                                        w2.append((s_act, y_ev_par[ypar]))
                                    if yb == 5 and prj_dve:
                                        w2.append((s_dve, prj_dve))
                                last = E("tensor", lambda e, g=g, j2=j2, ntile=ntile, yb=yb: e.matmul(
                                    ps[:, yb, 0:128], vt[:, g, j2, :], PT[:, j2, :],
                                    start=(j2 == 0), stop=(j2 == ntile - 1)),
                                    waits=w2, inc=(s_pe if j2 == ntile - 1 else None))
                            wv4 = [(s_pe, last)]
                            if yproj:
                                wv4.append((s_pe, yproj))
                            y_evac = E("scalar", lambda e, h=h, i=i, yb=yb: e.copy(
                                yall[:, h, (i % 4) * 128:(i % 4 + 1) * 128], ps[:, yb, 0:128]),
                                waits=wv4, inc=s_act)
                            y_ev_par[ypar] = y_evac
                    y_done = y_evac
                    for o in range(8):
                        par = o % 2
                        last = None
                        for k in range(8):
                            w2 = []
                            if k == 0:
                                w2 = [(s_act, y_done), (s_wt, tk_woa)]
                                if prj_dve:
                                    w2.append((s_dve, prj_dve))
                            last = E("tensor", lambda e, o=o, k=k: e.matmul(
                                ps[:, 5, :], woa[:, o, k, :], yall[:, k, :],
                                start=(k == 0), stop=(k == 7)),
                                waits=w2, inc=(s_pe if k == 7 else None))
                        if o == 7:
                            yproj = last
                        wv5 = [(s_pe, last), (s_x, xold_tok)]
                        if xnw_dma[par]:
                            wv5.append((s_st, xnw_dma[par]))
                        prj_dve = E("vector", lambda e, o=o, par=par: e.tensor_tensor(
                            xnw[:, par, :], ps[:, 5, :], xold[:, o, :], ALU.add),
                            waits=wv5, inc=s_dve)
                        xnw_dma[par] = E("scalar", lambda e, o=o, par=par, g0=g0: e.dma_start(
                            out=self.xfm.ap()[o, :, g0:g0 + 512], in_=xnw[:, par, :]),
                            waits=[(s_dve, prj_dve)], inc=s_st, inc_by=16)
                self.barrier()

    def build(self):
        E = self.emit
        s_x, s_dve, s_st = self.s_x, self.s_dve, self.s_st
        t0 = E("sync", lambda e: e.dma_start(out=self.ones_f[:], in_=self.onesh.ap()),
               inc=s_x, inc_by=16)
        t1 = E("sync", lambda e: e.dma_start(out=self.ident_f[:], in_=self.identh.ap()),
               inc=s_x, inc_by=16)
        E("vector", lambda e: e.tensor_copy(self.ones_b[:], self.ones_f[:]), waits=[(s_x, t0)])
        E("vector", lambda e: e.tensor_copy(self.ident_b[:], self.ident_f[:]), waits=[(s_x, t1)])
        E("vector", lambda e: e.memset(self.eps[:], 1e-5), inc=s_dve)
        self.cache_tok = {}
        self.barrier()
        if "A" in self.phases:
            self.phase_A()
            self.store_gate = self.s_st.n
        if "F1" in self.phases:
            self.phase_F("f1", write_output=False)
            self.store_gate = self.s_st.n
        if "C" in self.phases:
            self.phase_C()
            self.store_gate = self.s_st.n
        if "F2" in self.phases:
            self.phase_F("f2", write_output=True)
        self.barrier()
        if self.debug_xfm:
            E("sync", lambda e: e.dma_start(out=self.dbg.ap(), in_=self.xfm.ap()),
              waits=[(s_st, s_st.n)], inc=s_st, inc_by=16)
        E("sync", lambda e: e.wait_ge(self.s_st.h, self.s_st.n))
        self.barrier()
        return self.nc


def pack_w(W, w):
    K, M = W.shape
    return np.ascontiguousarray(W.reshape(K // 128, 128, M // w, w).transpose(2, 1, 0, 3))


def pack_bias(b):
    return np.ascontiguousarray(b.reshape(-1, 128).T)


def make_masks():
    m = np.zeros((4, 128, 512), np.float32)
    for v in range(4):
        for q in range(128):
            m[v, q, v * 128 + q + 1:] = NEG
    return m


def host_inputs(full):
    g = {}
    g["W1p"] = pack_w(full["L0_attn_a_W1"], 512)
    g["W2p"] = pack_w(full["L0_attn_a_W2"], 512)
    g["Wzp"] = pack_w(full["L0_attn_a_Wz"], 128)
    g["Whp"] = pack_w(full["L0_attn_a_Wh"], 128)
    g["Wop"] = pack_w(full["L0_attn_a_Wo"], 128)
    g["b1h"] = pack_bias(full["L0_attn_a_b1"])
    g["b2h"] = pack_bias(full["L0_attn_a_b2"])
    g["bzh"] = pack_bias(full["L0_attn_a_bz"])
    g["bhh"] = pack_bias(full["L0_attn_a_bh"])
    g["boh"] = pack_bias(full["L0_attn_a_bo"])
    g["wqp"] = pack_w(full["L0_attn_b_wq"], 128)
    g["wkdp"] = pack_w(full["L0_attn_b_wk_down"], 64)[0]
    g["wvdp"] = pack_w(full["L0_attn_b_wv_down"], 64)[0]
    g["wkup"] = np.ascontiguousarray(full["L0_attn_b_wk_up"])
    g["wvup"] = np.ascontiguousarray(full["L0_attn_b_wv_up"])
    g["woap"] = pack_w(full["L0_attn_b_wo"], 128)
    g["wrgp"] = pack_w(np.concatenate([np.asarray(full["L0_ffn_w_router"]),
                                       np.asarray(full["L0_ffn_w_gating"])], axis=1), 16)[0]
    g["wgp"] = np.stack([pack_w(np.asarray(full["L0_ffn_w_gate"])[e], 512) for e in range(NE)])
    g["wup"] = np.stack([pack_w(np.asarray(full["L0_ffn_w_up"])[e], 512) for e in range(NE)])
    g["wdp"] = np.stack([pack_w(np.asarray(full["L0_ffn_w_down"])[e], 128) for e in range(NE)])
    g["onesh"] = np.ones((128, 128), np.float32)
    g["identh"] = np.eye(128, dtype=np.float32)
    g["maskh"] = make_masks()
    g = {k: np.ascontiguousarray(v, dtype=np.float32) for k, v in g.items()}
    x = np.asarray(full["L0_attn_a_x"])
    xs = [np.ascontiguousarray(x[2 * c:2 * c + 2]) for c in range(8)]
    return g, xs


def kernel(**inputs):
    b = Builder()
    nc = b.build()
    g, xs = host_inputs(inputs)
    in_maps = [dict(g, x=xs[c]) for c in range(8)]
    res = run_bass_kernel_spmd(nc, in_maps, core_ids=list(range(8)))
    return np.concatenate([r["y"] for r in res.results], axis=0)


# version 07 (final)
# speedup vs baseline: 1.2931x; 1.1653x over previous
import numpy as np
import contextlib
import concourse.bass as bass
import concourse.mybir as mybir
from concourse.bass_utils import run_bass_kernel_spmd

F32 = mybir.dt.float32
BF16 = mybir.dt.bfloat16
AF = mybir.ActivationFunctionType
ALU = mybir.AluOpType
AXX = mybir.AxisListType.X

D = 1024
DJ = 8
FJ = 32
T = 2048
NSEQ = 2
NTOK = NSEQ * T
NE = 8
NEG = -1.0e30
SCALE = 128.0 ** -0.5


class Sem:
    def __init__(self, nc, name):
        self.h = nc.alloc_semaphore(name)
        self.n = 0
        self.name = name


class Builder:
    ENGINES = ("sync", "gpsimd", "tensor", "scalar", "vector")

    def __init__(self, phases=("A", "F1", "C", "F2"), debug_xfm=False, n_barrier=True):
        self.phases = phases
        self.debug_xfm = debug_xfm
        nc = self.nc = bass.Bass("TRN2", target_bir_lowering=False)
        self.ops = {e: [] for e in self.ENGINES}
        self.hwm = {e: {} for e in self.ENGINES}

        dram = lambda n, s: nc.dram_tensor(n, s, F32, kind="ExternalInput")
        self.x_in = dram("x", [NSEQ, T, D])
        self.W1p = dram("W1p", [8, 128, 8, 512])
        self.W2p = dram("W2p", [8, 128, 8, 512])
        self.Wzp = dram("Wzp", [32, 128, 32, 128])
        self.Whp = dram("Whp", [32, 128, 32, 128])
        self.Wop = dram("Wop", [8, 128, 32, 128])
        self.b1h = dram("b1h", [128, 32])
        self.b2h = dram("b2h", [128, 32])
        self.bzh = dram("bzh", [128, 32])
        self.bhh = dram("bhh", [128, 32])
        self.boh = dram("boh", [128, 8])
        self.wqp = dram("wqp", [8, 128, 8, 128])
        self.wkdp = dram("wkdp", [128, 8, 64])
        self.wvdp = dram("wvdp", [128, 8, 64])
        self.wkup = dram("wkup", [32, 128])
        self.wvup = dram("wvup", [32, 128])
        self.woap = dram("woap", [8, 128, 8, 128])
        self.wrgp = dram("wrgp", [128, 8, 16])
        self.wgp = dram("wgp", [NE, 8, 128, 8, 512])
        self.wup = dram("wup", [NE, 8, 128, 8, 512])
        self.wdp = dram("wdp", [NE, 8, 128, 32, 128])
        self.onesh = dram("onesh", [128, 128])
        self.identh = dram("identh", [128, 128])
        self.maskh = dram("maskh", [4, 128, 512])

        self.y_out = nc.dram_tensor("y", [NSEQ, T, D], F32, kind="ExternalOutput")
        if debug_xfm:
            self.dbg = nc.dram_tensor("dbg_xfm", [DJ, 128, NTOK], F32, kind="ExternalOutput")
            self.dbg_cwT = nc.dram_tensor("dbg_cwT", [64, 128], F32, kind="ExternalOutput")
            self.dbg_cwb = nc.dram_tensor("dbg_cwb", [128, 1024], F32, kind="ExternalOutput")
            self.dbg_xn = nc.dram_tensor("dbg_xn", [128, 1024], F32, kind="ExternalOutput")
            self.dbg_h = nc.dram_tensor("dbg_h", [128, 1024], F32, kind="ExternalOutput")
            self.dbg_s = nc.dram_tensor("dbg_s", [128, 1024], F32, kind="ExternalOutput")
            self.dbg_t0 = nc.dram_tensor("dbg_t0", [128, 48], F32, kind="ExternalOutput")
            self.dbgf2 = nc.alloc_sbuf_tensor("dbgf2", [128, 48], F32)
            self.dbgf = nc.alloc_sbuf_tensor("dbgf", [128, 1024], F32)
        self.xfm = nc.dram_tensor("xfm", [DJ, 128, NTOK], F32, kind="Internal")
        cache = lambda n, s: nc.dram_tensor(n, s, BF16, kind="Internal")
        self.cW1 = cache("cW1", [8, 128, 8, 512])
        self.cW2 = cache("cW2", [8, 128, 8, 512])
        self.cWz = cache("cWz", [32, 128, 32, 128])
        self.cWh = cache("cWh", [32, 128, 32, 128])
        self.cWo = cache("cWo", [8, 128, 32, 128])
        self.cwg = cache("cwg", [NE, 8, 128, 8, 512])
        self.cwu = cache("cwu", [NE, 8, 128, 8, 512])
        self.cwd = cache("cwd", [NE, 8, 128, 32, 128])
        self.cv_tok = {}

        self.x_tm = self.x_in.ap().rearrange("s (tb p) d -> s tb p d", p=128)
        self.y_tm = self.y_out.ap().rearrange("s (tb p) d -> s tb p d", p=128)
        self.xfm_t = self.xfm.ap().transpose([1, 0, 2])

        self.s_wt = Sem(nc, "s_wt")
        self.s_cv = Sem(nc, "s_cv")
        self.s_x = Sem(nc, "s_x")
        self.s_st = Sem(nc, "s_st")
        self.s_pe = Sem(nc, "s_pe")
        self.s_act = Sem(nc, "s_act")
        self.s_dve = Sem(nc, "s_dve")

        self.ones_f = nc.alloc_sbuf_tensor("ones_f", [128, 128], F32)
        self.ones_b = nc.alloc_sbuf_tensor("ones_b", [128, 128], BF16)
        self.ident_f = nc.alloc_sbuf_tensor("ident_f", [128, 128], F32)
        self.ident_b = nc.alloc_sbuf_tensor("ident_b", [128, 128], BF16)
        self.spacer = nc.alloc_sbuf_tensor("spcr", [1, 8], F32)
        self.eps = nc.alloc_sbuf_tensor("epsv", [128, 1], F32)
        self.aspacer = nc.alloc_sbuf_tensor("aspcr", [1, 8], F32)
        self.carry = nc.alloc_sbuf_tensor("carry", [128, 32], F32)

        self.store_gate = 0

    def emit(self, eng, fn, waits=(), inc=None, inc_by=1):
        self.ops[eng].append((fn, tuple(waits), inc, inc_by))
        if inc is not None:
            inc.n += inc_by
            return inc.n
        return None

    def barrier(self):
        ops = self.ops
        self.ops = {e: [] for e in self.ENGINES}
        if not any(ops.values()):
            return
        with self.nc.Block() as blk:
            for eng in self.ENGINES:
                lst = ops[eng]
                if not lst:
                    continue

                def body(E, lst=lst, eng=eng):
                    hw = self.hwm[eng]
                    for fn, waits, inc, k in lst:
                        for sem, val in waits:
                            if val <= 0 or val <= hw.get(sem.name, 0):
                                continue
                            E.wait_ge(sem.h, val)
                            hw[sem.name] = val
                        ins = fn(E)
                        if inc is not None:
                            ins.then_inc(inc.h, k)

                getattr(blk, eng)(body)

    def wstream(self, dst, fp32_ap, cache_ap, key, prime, waits):
        E = self.emit
        if prime:
            tok = E("gpsimd", lambda e, dst=dst, fp32_ap=fp32_ap: e.dma_start(
                out=dst, in_=fp32_ap), waits=waits, inc=self.s_wt, inc_by=16)
            stok = E("sync", lambda e, dst=dst, cache_ap=cache_ap: e.dma_start(
                out=cache_ap, in_=dst), waits=[(self.s_wt, tok)], inc=self.s_cv, inc_by=16)
            self.cache_tok[key] = stok
            return tok, stok
        tok = E("sync", lambda e, dst=dst, cache_ap=cache_ap: e.dma_start(
            out=dst, in_=cache_ap), waits=list(waits) + [(self.s_cv, self.cache_tok[key])],
            inc=self.s_wt, inc_by=16)
        return tok, 0

    def dve_spacer(self, inc=None):
        return self.emit("vector", lambda v: v.memset(self.spacer[:], 0.0), inc=inc)

    def ln_from_fm(self, gtok0, tlen, xn_dst, psf, ba, bb, load_dst, sqb, mstat,
                   xf_war=False):
        E = self.emit
        s_x, s_act, s_dve, s_pe, s_st = self.s_x, self.s_act, self.s_dve, self.s_pe, self.s_st
        psa = psf[:, ba * 512:(ba + 1) * 512]
        psb_ = psf[:, bb * 512:(bb + 1) * 512]
        sq_free = [0, 0]
        stats_free = 0
        tkc_prev = 0
        for tt in range(tlen // 512):
            g0 = gtok0 + tt * 512
            sl = slice(tt * 512, (tt + 1) * 512)
            dst = load_dst(tt)
            w = [(s_st, self.store_gate)]
            if xf_war and tkc_prev:
                w.append((s_act, tkc_prev))
            tok = E("scalar", lambda e, g0=g0, dst=dst: e.dma_start(
                out=dst, in_=self.xfm_t[:, :, g0:g0 + 512]), waits=w, inc=s_x, inc_by=16)
            tkc = E("scalar", lambda e, sl=sl, dst=dst: e.copy(xn_dst[:, :, sl], dst),
                    waits=[(s_x, tok)], inc=s_act)
            tkc_prev = tkc
            for dj in range(DJ):
                w2 = [(s_act, tkc)] if dj == 0 else []
                if dj == 0 and stats_free:
                    w2.append((s_dve, stats_free))
                E("tensor", lambda e, dj=dj, sl=sl, psa=psa: e.matmul(
                    psa, self.ones_b[:], xn_dst[:, dj, sl],
                    start=(dj == 0), stop=(dj == DJ - 1)), waits=w2)
            last = None
            for dj in range(DJ):
                w2 = [(s_act, tkc)]
                if sq_free[dj % 2]:
                    w2.append((s_pe, sq_free[dj % 2]))
                tk2 = E("scalar", lambda e, dj=dj, sl=sl: e.activation(
                    sqb[:, dj % 2, :], xn_dst[:, dj, sl], AF.Square), waits=w2, inc=s_act)
                last = E("tensor", lambda e, dj=dj, psb_=psb_: e.matmul(
                    psb_, self.ones_b[:], sqb[:, dj % 2, :],
                    start=(dj == 0), stop=(dj == DJ - 1)), waits=[(s_act, tk2)], inc=s_pe)
                sq_free[dj % 2] = last
            E("vector", lambda e, psa=psa: e.tensor_scalar_mul(mstat[:, 0, :], psa, 1.0 / D),
              waits=[(s_pe, last)])
            E("vector", lambda e, psb_=psb_: e.tensor_scalar_mul(mstat[:, 1, :], psb_, 1.0 / D))
            E("vector", lambda e: e.tensor_mul(mstat[:, 2, :], mstat[:, 0, :], mstat[:, 0, :]))
            tk3 = E("vector", lambda e: e.tensor_sub(mstat[:, 1, :], mstat[:, 1, :], mstat[:, 2, :]),
                    inc=s_dve)
            stats_free = tk3
            tk4 = E("scalar", lambda e: e.activation(mstat[:, 3, :], mstat[:, 1, :], AF.Sqrt,
                                                     bias=self.eps[:]), waits=[(s_dve, tk3)], inc=s_act)
            E("vector", lambda e: e.reciprocal(mstat[:, 1, :], mstat[:, 3, :]), waits=[(s_act, tk4)])
            self.dve_spacer()
            for dj in range(DJ):
                E("vector", lambda e, dj=dj, sl=sl: e.tensor_sub(
                    mstat[:, 2, :], xn_dst[:, dj, sl], mstat[:, 0, :]))
                E("vector", lambda e, dj=dj, sl=sl: e.tensor_mul(
                    xn_dst[:, dj, sl], mstat[:, 2, :], mstat[:, 1, :]),
                  inc=(s_dve if dj == DJ - 1 else None))
        return s_dve.n


    def phase_A(self):
        nc, E = self.nc, self.emit
        s_wt, s_x, s_st, s_pe, s_act, s_dve = (self.s_wt, self.s_x, self.s_st,
                                               self.s_pe, self.s_act, self.s_dve)
        with contextlib.ExitStack() as _es:
            xn = _es.enter_context(nc.sbuf_tensor("a_xn", [128, DJ, 512], BF16))
            x_in = _es.enter_context(nc.sbuf_tensor("a_xin", [128, FJ, 512], BF16))
            t1s = _es.enter_context(nc.sbuf_tensor("a_t1", [128, FJ, 512], BF16))
            zbuf = _es.enter_context(nc.sbuf_tensor("a_zb", [128, 2, 512], F32))
            abuf = _es.enter_context(nc.sbuf_tensor("a_ab", [128, 2, 512], F32))
            bbuf = _es.enter_context(nc.sbuf_tensor("a_bb", [128, 2, 512], F32))
            hbuf = _es.enter_context(nc.sbuf_tensor("a_hb", [128, 2, 512], F32))
            wA = _es.enter_context(nc.sbuf_tensor("a_wA", [128, 2, 8, 512], BF16))
            wz = _es.enter_context(nc.sbuf_tensor("a_wz", [128, 2, 32, 128], BF16))
            wh = _es.enter_context(nc.sbuf_tensor("a_wh", [128, 2, 32, 128], BF16))
            xtm = _es.enter_context(nc.sbuf_tensor("a_xtm", [128, 2, 1024], F32))
            sqb = _es.enter_context(nc.sbuf_tensor("a_sqb", [128, 2, 512], BF16))
            mstat = _es.enter_context(nc.sbuf_tensor("a_mst", [128, 6, 512], F32))
            xold = _es.enter_context(nc.sbuf_tensor("a_xold", [128, DJ, 512], F32))
            xnew = _es.enter_context(nc.sbuf_tensor("a_xnew", [128, 2, 512], F32))
            abias = _es.enter_context(nc.sbuf_tensor("a_bias", [128, 5, 32], F32))
            aboT = _es.enter_context(nc.sbuf_tensor("a_bo", [128, 8], F32))
            ps = _es.enter_context(nc.psum_tensor("a_ps", [128, 8, 512], F32))
            psf = ps.ap().rearrange("p a b -> p (a b)")
            for src, di in [(self.b1h, 0), (self.b2h, 1), (self.bzh, 2), (self.bhh, 4)]:
                E("sync", lambda e, src=src, di=di: e.dma_start(out=abias[:, di, :], in_=src.ap()),
                  inc=s_x, inc_by=16)
            tkb = E("sync", lambda e: e.dma_start(out=aboT[:], in_=self.boh.ap()), inc=s_x, inc_by=16)
            E("vector", lambda e: e.tensor_scalar_mul(abias[:, 3, :], abias[:, 2, :], -1.0),
              waits=[(s_x, s_x.n)], inc=s_dve)

            wA_tok = [0, 0]; wA_free = [0, 0]; wA_stok = [0, 0]
            wz_tok = [0, 0]; wz_free = [0, 0]; wz_stok = [0, 0]
            wh_tok = [0, 0]; wh_free = [0, 0]; wh_stok = [0, 0]
            ln_evac = [0, 0]
            st12_evac = [0, 0]
            st4_evac = [0, 0]
            st4_stored = [0, 0]
            st4_dma = [0, 0]
            stats_read = 0
            sq_free = [0, 0]

            for p in range(8):
                s = p // 4
                tok0 = (p % 4) * 512
                gtok0 = s * T + tok0
                self.barrier()
                xb = x_in.ap()[:, 0:8, :]
                for j in range(4):
                    par = j % 2
                    w = [(s_act, ln_evac[par])] if ln_evac[par] else []
                    tok = E("scalar", lambda e, s=s, j=j, par=par, tok0=tok0: e.dma_start(
                        out=xtm[:, par, :], in_=self.x_tm[s, tok0 // 128 + j]),
                        waits=w, inc=s_x, inc_by=16)
                    last = None
                    for dj in range(DJ):
                        w2 = [(s_x, tok)] if dj == 0 else []
                        last = E("tensor", lambda e, par=par, dj=dj: e.transpose(
                            ps[:, 2 * par + dj // 4, (dj % 4) * 128:(dj % 4 + 1) * 128],
                            xtm[:, par, dj * 128:(dj + 1) * 128], self.ident_f[:]),
                            waits=w2, inc=(s_pe if dj == DJ - 1 else None))
                    ln_evac[par] = E("scalar", lambda e, par=par, j=j, xb=xb: e.copy(
                        xb[:, :, j * 128:(j + 1) * 128],
                        psf[:, 2 * par * 512:(2 * par + 2) * 512].rearrange(
                            "p (dj t) -> p dj t", t=128)),
                        waits=[(s_pe, last)], inc=s_act)
                xb_done = s_act.n
                for dj in range(DJ):
                    w2 = [(s_act, xb_done)] if dj == 0 else []
                    if dj == 0 and stats_read:
                        w2.append((s_dve, stats_read))
                    E("tensor", lambda e, dj=dj, xb=xb: e.matmul(
                        ps[:, 6, :], self.ones_b[:], xb[:, dj, :],
                        start=(dj == 0), stop=(dj == DJ - 1)), waits=w2)
                last = None
                for dj in range(DJ):
                    w2 = [(s_act, xb_done)]
                    if sq_free[dj % 2]:
                        w2.append((s_pe, sq_free[dj % 2]))
                    tk2 = E("scalar", lambda e, dj=dj, xb=xb: e.activation(
                        sqb[:, dj % 2, :], xb[:, dj, :], AF.Square), waits=w2, inc=s_act)
                    last = E("tensor", lambda e, dj=dj: e.matmul(
                        ps[:, 7, :], self.ones_b[:], sqb[:, dj % 2, :],
                        start=(dj == 0), stop=(dj == DJ - 1)), waits=[(s_act, tk2)], inc=s_pe)
                    sq_free[dj % 2] = last
                E("vector", lambda e: e.tensor_scalar_mul(mstat[:, 0, :], ps[:, 6, :], 1.0 / D),
                  waits=[(s_pe, last)])
                E("vector", lambda e: e.tensor_scalar_mul(mstat[:, 1, :], ps[:, 7, :], 1.0 / D))
                E("vector", lambda e: e.tensor_mul(mstat[:, 2, :], mstat[:, 0, :], mstat[:, 0, :]))
                tk3 = E("vector", lambda e: e.tensor_sub(mstat[:, 3, :], mstat[:, 1, :],
                                                         mstat[:, 2, :]), inc=s_dve)
                stats_read = tk3
                tk4 = E("scalar", lambda e: e.activation(mstat[:, 4, :], mstat[:, 3, :], AF.Sqrt,
                                                         bias=self.eps[:]), waits=[(s_dve, tk3)], inc=s_act)
                E("vector", lambda e: e.reciprocal(mstat[:, 5, :], mstat[:, 4, :]),
                  waits=[(s_act, tk4)])
                self.dve_spacer()
                for dj in range(DJ):
                    E("vector", lambda e, dj=dj, xb=xb: e.tensor_sub(
                        mstat[:, 2, :], xb[:, dj, :], mstat[:, 0, :]))
                    E("vector", lambda e, dj=dj: e.tensor_mul(
                        xn[:, dj, :], mstat[:, 2, :], mstat[:, 5, :]),
                      inc=(s_dve if dj == DJ - 1 else None))
                xn_tok = s_dve.n
                self.barrier()
                for wsrc, wsrcp, dstbuf, bslot in [(self.cW1, self.W1p, x_in, 0), (self.cW2, self.W2p, t1s, 1)]:
                    for mb in range(8):
                        buf = mb % 2
                        w = [(s_pe, wA_free[buf])] if wA_free[buf] else []
                        if wA_stok[buf]:
                            w.append((self.s_cv, wA_stok[buf]))
                        wA_tok[buf], wA_stok[buf] = self.wstream(
                            wA[:, buf, :, :], wsrcp.ap()[mb], wsrc.ap()[mb],
                            ("A", bslot, mb), p == 0, w)
                        for mo in range(4):
                            fo = mb * 4 + mo
                            par = fo % 2
                            last = None
                            for k in range(DJ):
                                w2 = []
                                if k == 0:
                                    w2 = [(s_wt, wA_tok[buf]), (s_dve, xn_tok)]
                                    if st12_evac[par]:
                                        w2.append((s_act, st12_evac[par]))
                                last = E("tensor", lambda e, mo=mo, k=k, buf=buf, par=par: e.matmul(
                                    ps[:, par, :], wA[:, buf, k, mo * 128:(mo + 1) * 128],
                                    xn[:, k, :], start=(k == 0), stop=(k == DJ - 1)),
                                    waits=w2, inc=(s_pe if k == DJ - 1 else None))
                            st12_evac[par] = E("scalar", lambda e, fo=fo, par=par, dstbuf=dstbuf,
                                               bslot=bslot: e.activation(
                                dstbuf[:, fo, :], ps[:, par, :], AF.Silu,
                                bias=abias[:, bslot, fo:fo + 1]),
                                waits=[(s_pe, last)], inc=s_act)
                        wA_free[buf] = s_pe.n
                xin_done = s_act.n
                self.barrier()
                zread = [0, 0]
                bread = [0, 0]
                scand = [0, 0]
                for fo in range(FJ):
                    par = fo % 2
                    buf = fo % 2
                    w = [(s_pe, wz_free[buf])] if wz_free[buf] else []
                    if wz_stok[buf]:
                        w.append((self.s_cv, wz_stok[buf]))
                    wz_tok[buf], wz_stok[buf] = self.wstream(
                        wz[:, buf, :, :], self.Wzp.ap()[fo], self.cWz.ap()[fo],
                        ("Wz", fo), p == 0, w)
                    w = [(s_pe, wh_free[buf])] if wh_free[buf] else []
                    if wh_stok[buf]:
                        w.append((self.s_cv, wh_stok[buf]))
                    wh_tok[buf], wh_stok[buf] = self.wstream(
                        wh[:, buf, :, :], self.Whp.ap()[fo], self.cWh.ap()[fo],
                        ("Wh", fo), p == 0, w)
                    lastz = None
                    for k in range(FJ):
                        w2 = []
                        if k == 0:
                            w2 = [(s_wt, wz_tok[buf]), (s_act, xin_done)]
                            if zread[par]:
                                w2.append((s_act, zread[par]))
                        lastz = E("tensor", lambda e, k=k, buf=buf, par=par: e.matmul(
                            ps[:, par, :], wz[:, buf, k, :], x_in[:, k, :],
                            start=(k == 0), stop=(k == FJ - 1)),
                            waits=w2, inc=(s_pe if k == FJ - 1 else None))
                    lasth = None
                    for k in range(FJ):
                        w2 = []
                        if k == 0:
                            w2 = [(s_wt, wh_tok[buf])]
                            if bread[par]:
                                w2.append((s_dve, bread[par]))
                        lasth = E("tensor", lambda e, k=k, buf=buf, par=par: e.matmul(
                            ps[:, 2 + par, :], wh[:, buf, k, :], x_in[:, k, :],
                            start=(k == 0), stop=(k == FJ - 1)),
                            waits=w2, inc=(s_pe if k == FJ - 1 else None))
                    wz_free[buf] = wh_free[buf] = s_pe.n
                    wz1 = [(s_pe, lastz)]
                    if bread[par]:
                        wz1.append((s_dve, bread[par]))
                    E("scalar", lambda e, fo=fo, par=par: e.activation(
                        zbuf[:, par, :], ps[:, par, :], AF.Sigmoid, bias=abias[:, 2, fo:fo + 1]),
                        waits=wz1)
                    wa1 = []
                    if scand[par]:
                        wa1.append((s_dve, scand[par]))
                    zread[par] = E("scalar", lambda e, fo=fo, par=par: e.activation(
                        abuf[:, par, :], ps[:, par, :], AF.Sigmoid, bias=abias[:, 3, fo:fo + 1],
                        scale=-1.0), waits=wa1, inc=s_act)
                    bread[par] = E("vector", lambda e, fo=fo, par=par: e.scalar_tensor_tensor(
                        bbuf[:, par, :], ps[:, 2 + par, :], abias[:, 4, fo:fo + 1], zbuf[:, par, :],
                        ALU.add, ALU.mult), waits=[(s_pe, lasth), (s_act, zread[par])], inc=s_dve)
                    init = 0.0 if (p % 4) == 0 else self.carry.ap()[:, fo:fo + 1]
                    scand[par] = E("vector", lambda e, par=par, init=init: e.tensor_tensor_scan(
                        hbuf[:, par, :], abuf[:, par, :], bbuf[:, par, :], init, ALU.mult, ALU.add),
                        inc=s_dve)
                    E("vector", lambda e, fo=fo, par=par: e.tensor_tensor(
                        t1s[:, fo, :], t1s[:, fo, :], hbuf[:, par, :], ALU.add),
                      waits=[(s_act, xin_done)])
                    E("vector", lambda e, fo=fo, par=par: e.tensor_copy(
                        self.carry[:, fo:fo + 1], hbuf[:, par, 511:512]), inc=s_dve)
                t1_done = s_dve.n
                self.barrier()
                for j in range(4):
                    par = j % 2
                    w = [(s_act, st4_evac[par])] if st4_evac[par] else []
                    tok = E("scalar", lambda e, s=s, j=j, par=par, tok0=tok0: e.dma_start(
                        out=xtm[:, par, :], in_=self.x_tm[s, tok0 // 128 + j]),
                        waits=w, inc=s_x, inc_by=16)
                    last = None
                    for dj in range(DJ):
                        w2 = [(s_x, tok)] if dj == 0 else []
                        if j == 0 and dj == 0:
                            w2.append((s_dve, t1_done))
                        last = E("tensor", lambda e, par=par, dj=dj: e.transpose(
                            ps[:, 2 + 2 * par + dj // 4, (dj % 4) * 128:(dj % 4 + 1) * 128],
                            xtm[:, par, dj * 128:(dj + 1) * 128], self.ident_f[:]),
                            waits=w2, inc=(s_pe if dj == DJ - 1 else None))
                    st4_evac[par] = E("scalar", lambda e, par=par, j=j: e.copy(
                        xold[:, :, j * 128:(j + 1) * 128],
                        psf[:, (2 + 2 * par) * 512:(4 + 2 * par) * 512].rearrange(
                            "p (dj t) -> p dj t", t=128)),
                        waits=[(s_pe, last)], inc=s_act)
                xold_done = s_act.n
                for ob in range(8):
                    buf = ob % 2
                    par = ob % 2
                    w = [(s_pe, wz_free[buf])] if wz_free[buf] else []
                    if wz_stok[buf]:
                        w.append((self.s_cv, wz_stok[buf]))
                    wz_tok[buf], wz_stok[buf] = self.wstream(
                        wz[:, buf, :, :], self.Wop.ap()[ob], self.cWo.ap()[ob],
                        ("Wo", ob), p == 0, w)
                    last = None
                    for k in range(FJ):
                        w2 = []
                        if k == 0:
                            w2 = [(s_wt, wz_tok[buf]), (s_dve, t1_done)]
                            if st4_stored[par]:
                                w2.append((s_dve, st4_stored[par]))
                        last = E("tensor", lambda e, k=k, buf=buf, par=par: e.matmul(
                            ps[:, par, :], wz[:, buf, k, :], t1s[:, k, :],
                            start=(k == 0), stop=(k == FJ - 1)),
                            waits=w2, inc=(s_pe if k == FJ - 1 else None))
                    wz_free[buf] = s_pe.n
                    wv = [(s_pe, last), (s_act, xold_done)]
                    if st4_dma[par]:
                        wv.append((s_st, st4_dma[par]))
                    st4_stored[par] = E("vector", lambda e, ob=ob, par=par: e.scalar_tensor_tensor(
                        xnew[:, par, :], ps[:, par, :], aboT[:, ob:ob + 1], xold[:, ob, :],
                        ALU.add, ALU.add), waits=wv, inc=s_dve)
                    st4_dma[par] = E("scalar", lambda e, ob=ob, par=par, gtok0=gtok0: e.dma_start(
                        out=self.xfm.ap()[ob, :, gtok0:gtok0 + 512], in_=xnew[:, par, :]),
                        waits=[(s_dve, st4_stored[par])], inc=s_st, inc_by=16)
                self.barrier()

    def phase_F(self, tag, write_output):
        nc, E = self.nc, self.emit
        s_wt, s_x, s_st, s_pe, s_act, s_dve = (self.s_wt, self.s_x, self.s_st,
                                               self.s_pe, self.s_act, self.s_dve)
        with contextlib.ExitStack() as _es:
            xn = _es.enter_context(nc.sbuf_tensor(tag + "_xn", [128, DJ, 1024], BF16))
            acc = _es.enter_context(nc.sbuf_tensor(tag + "_acc", [128, DJ, 1024], F32))
            hf = _es.enter_context(nc.sbuf_tensor(tag + "_h", [128, FJ, 1024], BF16))
            wg = _es.enter_context(nc.sbuf_tensor(tag + "_wg", [128, 2, 8, 512], BF16))
            wu = _es.enter_context(nc.sbuf_tensor(tag + "_wu", [128, 2, 8, 512], BF16))
            wd = _es.enter_context(nc.sbuf_tensor(tag + "_wd", [128, 2, 32, 128], BF16))
            wrg = _es.enter_context(nc.sbuf_tensor(tag + "_wrg", [128, 8, 16], BF16))
            sqb = _es.enter_context(nc.sbuf_tensor(tag + "_sqb", [128, 2, 512], BF16))
            mstat = _es.enter_context(nc.sbuf_tensor(tag + "_mst", [128, 4, 512], F32))
            ssb = _es.enter_context(nc.sbuf_tensor(tag + "_s", [128, 2, 1024], F32))
            cwtm = _es.enter_context(nc.sbuf_tensor(tag + "_cwt", [128, 18, 8], F32))
            cwall = _es.enter_context(nc.sbuf_tensor(tag + "_cwa", [128, 128], F32))
            cwT = _es.enter_context(nc.sbuf_tensor(tag + "_cwT", [64, 128], F32))
            cwrow = _es.enter_context(nc.sbuf_tensor(tag + "_cwr", [1, 1024], F32))
            cwb = _es.enter_context(nc.sbuf_tensor(tag + "_cwb", [128, 1024], F32))
            ftmp = _es.enter_context(nc.sbuf_tensor(tag + "_tmp", [128, 1024], F32))
            st1 = _es.enter_context(nc.sbuf_tensor(tag + "_st1", [128, 8, 1], F32))
            ytm = _es.enter_context(nc.sbuf_tensor(tag + "_ytm", [128, 2, 1024], F32))
            ps = _es.enter_context(nc.psum_tensor(tag + "_ps", [128, 8, 512], F32))
            psf = ps.ap().rearrange("p a b -> p (a b)")
            tkw = E("gpsimd", lambda e: e.dma_start(out=wrg[:], in_=self.wrgp.ap()),
                    inc=s_wt, inc_by=16)
            wg_tok = [0, 0]; wg_free = [0, 0]; wg_stok = [0, 0]
            wu_tok = [0, 0]; wu_free = [0, 0]; wu_stok = [0, 0]
            wd_tok = [0, 0]; wd_free = [0, 0]; wd_stok = [0, 0]
            silu_tok = [0, 0]
            hmul_tok = [0, 0]
            dn_evac = 0
            dn_par = [0, 0]
            bc_read = 0
            cwrow_pe = 0
            ytm_dma = [0, 0]
            ytm_evac = [0, 0]
            for c in range(4):
                prime = (tag == "f1" and c == 0)
                gtok0 = c * 1024
                seq = c // 2
                self.barrier()
                xn_tok = self.ln_from_fm(gtok0, 1024, xn, psf, 6, 7,
                                         lambda tt: acc[:, :, tt * 512:(tt + 1) * 512],
                                         sqb, mstat)
                self.barrier()
                for ti in range(8):
                    for k in range(DJ):
                        w = [(s_dve, xn_tok), (s_wt, tkw)] if (k == 0 and ti == 0) else []
                        E("tensor", lambda e, ti=ti, k=k: e.matmul(
                            ps[:, 4, ti * 16:ti * 16 + 16], xn[:, k, ti * 128:(ti + 1) * 128],
                            wrg[:, k, :], start=(ti == 0 and k == 0), stop=(k == DJ - 1)),
                            waits=w, inc=(s_pe if (k == DJ - 1 and ti == 7) else None))
                ract = E("scalar", lambda e: e.copy(
                    cwtm[:, 0:16, :], ps[:, 4, 0:128].rearrange("p (a b) -> p a b", b=8)),
                    waits=[(s_pe, s_pe.n)], inc=s_act)
                cw_fin = 0
                tke2 = 0
                dbg_this = self.debug_xfm and tag == "f1" and c == 0
                for ti in range(8):
                    lr = cwtm.ap()[:, 2 * ti, :]
                    lg = cwtm.ap()[:, 2 * ti + 1, :]
                    er = cwtm.ap()[:, 16, :]
                    tprod = cwtm.ap()[:, 17, :]
                    w = [(s_act, ract)] if ti == 0 else [(s_act, tke2)]
                    E("vector", lambda e, lg=lg: e.reduce_max(st1[:, 1, :], lg, axis=AXX,
                                                              negate=True), waits=w)
                    E("vector", lambda e, lr=lr: e.reduce_max(st1[:, 0, :], lr, axis=AXX,
                                                              negate=True))
                    self.dve_spacer()
                    E("vector", lambda e: e.tensor_scalar_mul(st1[:, 2, :], st1[:, 1, :], 10.0))
                    self.dve_spacer()
                    tk = self.dve_spacer(inc=s_dve)
                    E("scalar", lambda e, lr=lr, er=er: e.activation(
                        er, lr, AF.Exp, bias=st1[:, 0, :], accum_out=st1[:, 3, :]),
                        waits=[(s_dve, tk)])
                    tke2 = E("scalar", lambda e, lg=lg: e.activation(
                        lg, lg, AF.Exp, bias=st1[:, 2, :], scale=10.0, accum_out=st1[:, 4, :]),
                        inc=s_act)
                    if dbg_this and ti == 0:
                        E("vector", lambda e, er=er: e.tensor_copy(self.dbgf2[:, 0:8], er),
                          waits=[(s_act, tke2)])
                        E("vector", lambda e, lg=lg: e.tensor_copy(self.dbgf2[:, 8:16], lg))
                        E("vector", lambda e, lr=lr: e.tensor_copy(self.dbgf2[:, 16:24], lr))
                    E("vector", lambda e, lg=lg, er=er, tprod=tprod: e.tensor_mul(tprod, er, lg),
                      waits=[(s_act, tke2)])
                    E("vector", lambda e: e.tensor_mul(st1[:, 6, :], st1[:, 3, :], st1[:, 4, :]))
                    self.dve_spacer()
                    E("vector", lambda e, tprod=tprod: e.reduce_sum(st1[:, 5, :], tprod, axis=AXX))
                    self.dve_spacer()
                    self.dve_spacer()
                    E("vector", lambda e: e.scalar_tensor_tensor(
                        st1[:, 6, :], st1[:, 6, :], 1e-9, st1[:, 5, :], ALU.mult, ALU.add))
                    self.dve_spacer()
                    self.dve_spacer()
                    E("vector", lambda e: e.reciprocal(st1[:, 7, :], st1[:, 6, :]))
                    self.dve_spacer()
                    self.dve_spacer()
                    if dbg_this and ti == 0:
                        E("vector", lambda e: e.tensor_copy(
                            self.dbgf2[:, 24:32], st1[:, :, 0].transpose([0, 1]) if False else st1.ap().rearrange("p a b -> p (a b)")))
                        E("vector", lambda e, tprod=tprod: e.tensor_copy(self.dbgf2[:, 32:40], tprod))
                    cw_fin = E("vector", lambda e, ti=ti, tprod=tprod: e.tensor_scalar_mul(
                        cwall[:, ti * 8:(ti + 1) * 8], tprod, st1[:, 7, :]), inc=s_dve)
                    if dbg_this and ti == 0:
                        tdd = E("vector", lambda e: e.tensor_copy(
                            self.dbgf2[:, 40:48], cwall[:, 0:8]), inc=s_dve)
                        E("sync", lambda e: e.dma_start(out=self.dbg_t0.ap(), in_=self.dbgf2[:]),
                          waits=[(s_dve, tdd)], inc=s_st, inc_by=16)
                if dbg_this:
                    tda = E("vector", lambda e: e.tensor_copy(self.dbgf[:, 0:128], cwall[:]),
                            waits=[(s_dve, cw_fin)], inc=s_dve)
                    E("sync", lambda e: e.dma_start(out=self.dbg_s.ap()[:, 0:128], in_=self.dbgf[:, 0:128]),
                      waits=[(s_dve, tda)], inc=s_st, inc_by=16)
                wt2 = [(s_dve, cw_fin)]
                if cwrow_pe:
                    wt2.append((s_pe, cwrow_pe))
                tkt = E("tensor", lambda e: e.transpose(ps[:, 5, 0:128], cwall[:], self.ident_f[:]),
                        waits=wt2, inc=s_pe)
                cw_done = E("scalar", lambda e: e.copy(cwT[:], ps[0:64, 5, 0:128]),
                            waits=[(s_pe, tkt)], inc=s_act)
                self.barrier()
                for ex in range(NE):
                    wv = [(s_act, cw_done)]
                    if cwrow_pe:
                        wv.append((s_pe, cwrow_pe))
                    tkr = None
                    for ti in range(8):
                        tkr = E("sync", lambda e, ex=ex, ti=ti: e.dma_start(
                            out=cwrow[:, ti * 128:(ti + 1) * 128],
                            in_=cwT[ti * 8 + ex:ti * 8 + ex + 1, :]),
                            waits=(wv if ti == 0 else []), inc=s_x, inc_by=16)
                    last = None
                    for q in range(4):
                        w = []
                        if q == 0:
                            w = [(s_x, tkr)]
                            if bc_read:
                                w.append((s_dve, bc_read))
                            if dn_evac:
                                w.append((s_dve, dn_evac))
                        last = E("tensor", lambda e, q=q: e.matmul(
                            psf[:, 4 * 512 + q * 256:4 * 512 + (q + 1) * 256],
                            self.ones_f[0:1, :], cwrow[:, q * 256:(q + 1) * 256],
                            start=True, stop=True), waits=w, inc=(s_pe if q == 3 else None))
                    cwrow_pe = last
                    bc_read = E("vector", lambda e: e.tensor_copy(
                        cwb[:], psf[:, 4 * 512:6 * 512]), waits=[(s_pe, last)], inc=s_dve)
                    if self.debug_xfm and tag == "f1" and c == 0 and ex == 0:
                        E("sync", lambda e: e.dma_start(out=self.dbg_cwT.ap(), in_=cwT[:]),
                          waits=[(s_act, cw_done)], inc=s_st, inc_by=16)
                        E("sync", lambda e: e.dma_start(out=self.dbg_cwb.ap(), in_=cwb[:]),
                          waits=[(s_dve, bc_read)], inc=s_st, inc_by=16)
                        tdx = E("vector", lambda e: e.tensor_copy(self.dbgf[:], xn[:, 0, :]),
                                waits=[(s_dve, xn_tok)], inc=s_dve)
                        E("sync", lambda e: e.dma_start(out=self.dbg_xn.ap(), in_=self.dbgf[:]),
                          waits=[(s_dve, tdx)], inc=s_st, inc_by=16)
                    for fo in range(FJ):
                        mb, mo = fo // 4, fo % 4
                        buf = mb % 2
                        fpar = fo % 2
                        if mo == 0:
                            w = [(s_pe, wg_free[buf])] if wg_free[buf] else []
                            if wg_stok[buf]:
                                w.append((self.s_cv, wg_stok[buf]))
                            wg_tok[buf], wg_stok[buf] = self.wstream(
                                wg[:, buf, :, :], self.wgp.ap()[ex, mb], self.cwg.ap()[ex, mb],
                                ("wg", ex, mb), prime, w)
                            w = [(s_pe, wu_free[buf])] if wu_free[buf] else []
                            if wu_stok[buf]:
                                w.append((self.s_cv, wu_stok[buf]))
                            wu_tok[buf], wu_stok[buf] = self.wstream(
                                wu[:, buf, :, :], self.wup.ap()[ex, mb], self.cwu.ap()[ex, mb],
                                ("wu", ex, mb), prime, w)
                        gb = 2 * fpar
                        ub = 4 + 2 * fpar
                        lastg = None
                        for k in range(DJ):
                            for tt in range(2):
                                w2 = []
                                if k == 0 and tt == 0:
                                    w2 = [(s_wt, wg_tok[buf]), (s_dve, xn_tok)]
                                    if silu_tok[fpar]:
                                        w2.append((s_act, silu_tok[fpar]))
                                lastg = E("tensor", lambda e, mo=mo, k=k, tt=tt, buf=buf, gb=gb: e.matmul(
                                    ps[:, gb + tt, :], wg[:, buf, k, mo * 128:(mo + 1) * 128],
                                    xn[:, k, tt * 512:(tt + 1) * 512],
                                    start=(k == 0), stop=(k == DJ - 1)),
                                    waits=w2, inc=(s_pe if (k == DJ - 1 and tt == 1) else None))
                        lastu = None
                        for k in range(DJ):
                            for tt in range(2):
                                w2 = []
                                if k == 0 and tt == 0:
                                    w2 = [(s_wt, wu_tok[buf])]
                                    if hmul_tok[fpar]:
                                        w2.append((s_dve, hmul_tok[fpar]))
                                lastu = E("tensor", lambda e, mo=mo, k=k, tt=tt, buf=buf, ub=ub: e.matmul(
                                    ps[:, ub + tt, :], wu[:, buf, k, mo * 128:(mo + 1) * 128],
                                    xn[:, k, tt * 512:(tt + 1) * 512],
                                    start=(k == 0), stop=(k == DJ - 1)),
                                    waits=w2, inc=(s_pe if (k == DJ - 1 and tt == 1) else None))
                        if mo == 3:
                            wg_free[buf] = wu_free[buf] = s_pe.n
                        wsi = [(s_pe, lastg)]
                        if hmul_tok[fpar]:
                            wsi.append((s_dve, hmul_tok[fpar]))
                        silu_tok[fpar] = E("scalar", lambda e, fpar=fpar, gb=gb: e.activation(
                            ssb[:, fpar, :], psf[:, gb * 512:(gb + 2) * 512], AF.Silu),
                            waits=wsi, inc=s_act)
                        hmul_tok[fpar] = E("vector", lambda e, fo=fo, fpar=fpar, ub=ub: e.tensor_tensor(
                            hf[:, fo, :], ssb[:, fpar, :], psf[:, ub * 512:(ub + 2) * 512], ALU.mult),
                            waits=[(s_act, silu_tok[fpar]), (s_pe, lastu)], inc=s_dve)
                    h_done = max(hmul_tok)
                    if self.debug_xfm and tag == "f1" and c == 0 and ex == 0:
                        tdh = E("vector", lambda e: e.tensor_copy(self.dbgf[:], hf[:, 0, :]),
                                waits=[(s_dve, h_done)], inc=s_dve)
                        E("sync", lambda e: e.dma_start(out=self.dbg_h.ap(), in_=self.dbgf[:]),
                          waits=[(s_dve, tdh)], inc=s_st, inc_by=16)
                    for o in range(8):
                        buf = o % 2
                        w = [(s_pe, wd_free[buf])] if wd_free[buf] else []
                        if wd_stok[buf]:
                            w.append((self.s_cv, wd_stok[buf]))
                        wd_tok[buf], wd_stok[buf] = self.wstream(
                            wd[:, buf, :, :], self.wdp.ap()[ex, o], self.cwd.ap()[ex, o],
                            ("wd", ex, o), prime, w)
                        opar = o % 2
                        db = 4 + 2 * opar
                        last = None
                        for k in range(FJ):
                            for tt in range(2):
                                w2 = []
                                if k == 0 and tt == 0:
                                    w2 = [(s_wt, wd_tok[buf]), (s_dve, h_done)]
                                    if dn_par[opar]:
                                        w2.append((s_dve, dn_par[opar]))
                                last = E("tensor", lambda e, k=k, tt=tt, buf=buf, db=db: e.matmul(
                                    ps[:, db + tt, :], wd[:, buf, k, :],
                                    hf[:, k, tt * 512:(tt + 1) * 512],
                                    start=(k == 0), stop=(k == FJ - 1)),
                                    waits=w2, inc=(s_pe if (k == FJ - 1 and tt == 1) else None))
                        wd_free[buf] = s_pe.n
                        E("vector", lambda e, db=db: e.tensor_mul(
                            ftmp[:], psf[:, db * 512:(db + 2) * 512], cwb[:]),
                          waits=[(s_pe, last), (s_dve, bc_read)])
                        dn_evac = E("vector", lambda e, o=o: e.tensor_tensor(
                            acc[:, o, :], acc[:, o, :], ftmp[:], ALU.add), inc=s_dve)
                        dn_par[opar] = dn_evac
                if not write_output:
                    for dj in range(DJ):
                        E("scalar", lambda e, dj=dj, gtok0=gtok0: e.dma_start(
                            out=self.xfm.ap()[dj, :, gtok0:gtok0 + 1024], in_=acc[:, dj, :]),
                          waits=[(s_dve, dn_evac)], inc=s_st, inc_by=16)
                else:
                    for j in range(8):
                        par = j % 2
                        last = None
                        for dj in range(DJ):
                            w2 = [(s_dve, dn_evac)] if dj == 0 else []
                            if dj == 0 and ytm_evac[par]:
                                w2.append((s_act, ytm_evac[par]))
                            last = E("tensor", lambda e, j=j, dj=dj, par=par: e.transpose(
                                ps[:, 2 * par + dj // 4, (dj % 4) * 128:(dj % 4 + 1) * 128],
                                acc[:, dj, j * 128:(j + 1) * 128], self.ident_f[:]),
                                waits=w2, inc=(s_pe if dj == DJ - 1 else None))
                        wv = [(s_pe, last)]
                        if ytm_dma[par]:
                            wv.append((s_st, ytm_dma[par]))
                        ytm_evac[par] = E("scalar", lambda e, par=par: e.copy(
                            ytm[:, par, :].rearrange("p (dj t) -> p dj t", t=128),
                            psf[:, 2 * par * 512:(2 * par + 2) * 512].rearrange(
                                "p (dj t) -> p dj t", t=128)),
                            waits=wv, inc=s_act)
                        tb = (gtok0 % T) // 128 + j
                        ytm_dma[par] = E("scalar", lambda e, seq=seq, tb=tb, par=par: e.dma_start(
                            out=self.y_tm[seq, tb], in_=ytm[:, par, :]),
                            waits=[(s_act, ytm_evac[par])], inc=s_st, inc_by=16)
                self.barrier()

    def phase_C(self):
        nc, E = self.nc, self.emit
        s_wt, s_x, s_st, s_pe, s_act, s_dve = (self.s_wt, self.s_x, self.s_st,
                                               self.s_pe, self.s_act, self.s_dve)
        with contextlib.ExitStack() as _es:
            xn = _es.enter_context(nc.sbuf_tensor("c_xn", [128, DJ, T], BF16))
            qsb = _es.enter_context(nc.sbuf_tensor("c_q", [128, 8, T], BF16))
            kb = _es.enter_context(nc.sbuf_tensor("c_kb", [128, 2, T], BF16))
            vb = _es.enter_context(nc.sbuf_tensor("c_vb", [128, 2, T], BF16))
            vt = _es.enter_context(nc.sbuf_tensor("c_vt", [128, 2, 16, 128], BF16))
            klat = _es.enter_context(nc.sbuf_tensor("c_klat", [64, T], BF16))
            vlat = _es.enter_context(nc.sbuf_tensor("c_vlat", [64, T], BF16))
            Ssb = _es.enter_context(nc.sbuf_tensor("c_S", [128, 2, T], F32))
            Pb = _es.enter_context(nc.sbuf_tensor("c_P", [128, 2, T], BF16))
            PT = _es.enter_context(nc.sbuf_tensor("c_PT", [128, 16, 128], BF16))
            yall = _es.enter_context(nc.sbuf_tensor("c_y", [128, 8, 512], BF16))
            woa = _es.enter_context(nc.sbuf_tensor("c_woa", [128, 8, 8, 128], BF16))
            wqs = _es.enter_context(nc.sbuf_tensor("c_wq", [128, 2, 8, 128], BF16))
            wkd = _es.enter_context(nc.sbuf_tensor("c_wkd", [128, 8, 64], BF16))
            wvd = _es.enter_context(nc.sbuf_tensor("c_wvd", [128, 8, 64], BF16))
            kup = _es.enter_context(nc.sbuf_tensor("c_kup", [64, 128], BF16))
            vup = _es.enter_context(nc.sbuf_tensor("c_vup", [64, 128], BF16))
            masks = _es.enter_context(nc.sbuf_tensor("c_mask", [128, 4, 512], F32))
            sqb = _es.enter_context(nc.sbuf_tensor("c_sqb", [128, 2, 512], BF16))
            mstat = _es.enter_context(nc.sbuf_tensor("c_mst", [128, 4, 512], F32))
            xold = _es.enter_context(nc.sbuf_tensor("c_xold", [128, DJ, 512], F32))
            xnw = _es.enter_context(nc.sbuf_tensor("c_xnw", [128, 2, 512], F32))
            st1 = _es.enter_context(nc.sbuf_tensor("c_st1", [128, 8, 1], F32))
            ps = _es.enter_context(nc.psum_tensor("c_ps", [128, 6, 512], F32))
            psb = _es.enter_context(nc.psum_tensor("c_psb", [128, 2, 1024], BF16))
            psf = ps.ap().rearrange("p a b -> p (a b)")
            tk_m = E("sync", lambda e: e.dma_start(out=masks[:], in_=self.maskh.ap().transpose(
                [1, 0, 2])), inc=s_x, inc_by=16)
            tk_woa = E("gpsimd", lambda e: e.dma_start(out=woa[:], in_=self.woap.ap().transpose(
                [1, 0, 2, 3])), inc=s_wt, inc_by=16)
            tk_wkd = E("gpsimd", lambda e: e.dma_start(out=wkd[:], in_=self.wkdp.ap()),
                       inc=s_wt, inc_by=16)
            tk_wvd = E("gpsimd", lambda e: e.dma_start(out=wvd[:], in_=self.wvdp.ap()),
                       inc=s_wt, inc_by=16)
            E("gpsimd", lambda e: e.dma_start(out=kup[0:32, :], in_=self.wkup.ap()),
              inc=s_wt, inc_by=16)
            tk_kup = E("gpsimd", lambda e: e.dma_start(out=kup[32:64, :], in_=self.wkup.ap()),
                       inc=s_wt, inc_by=16)
            E("gpsimd", lambda e: e.dma_start(out=vup[0:32, :], in_=self.wvup.ap()),
              inc=s_wt, inc_by=16)
            tk_vup = E("gpsimd", lambda e: e.dma_start(out=vup[32:64, :], in_=self.wvup.ap()),
                       inc=s_wt, inc_by=16)
            wq_tok = [0, 0]; wq_free = [0, 0]
            for seq in range(NSEQ):
                self.barrier()
                xn_tok = self.ln_from_fm(seq * T, T, xn, psf, 4, 5, lambda tt: xold[:], sqb, mstat, xf_war=True)
                self.barrier()
                qev = 0
                for hb in range(8):
                    buf = hb % 2
                    w = [(s_pe, wq_free[buf])] if wq_free[buf] else []
                    wq_tok[buf] = E("gpsimd", lambda e, hb=hb, buf=buf: e.dma_start(
                        out=wqs[:, buf, :, :], in_=self.wqp.ap()[hb]), waits=w, inc=s_wt, inc_by=16)
                    last = None
                    for k in range(DJ):
                        for tt in range(4):
                            w2 = []
                            if k == 0 and tt == 0:
                                w2 = [(s_wt, wq_tok[buf]), (s_dve, xn_tok)]
                                if qev:
                                    w2.append((s_act, qev))
                            last = E("tensor", lambda e, k=k, tt=tt, buf=buf: e.matmul(
                                ps[:, tt, :], wqs[:, buf, k, :], xn[:, k, tt * 512:(tt + 1) * 512],
                                start=(k == 0), stop=(k == DJ - 1)),
                                waits=w2, inc=(s_pe if (k == DJ - 1 and tt == 3) else None))
                    wq_free[buf] = s_pe.n
                    qev = E("scalar", lambda e, hb=hb: e.copy(
                        qsb[:, hb, :], psf[:, 0:4 * 512]), waits=[(s_pe, last)], inc=s_act)
                klev = 0
                for wsrc, wtok, ldst in [(wkd, tk_wkd, klat), (wvd, tk_wvd, vlat)]:
                    for tt in range(4):
                        last = None
                        for k in range(DJ):
                            w2 = []
                            if k == 0:
                                w2 = [(s_wt, wtok), (s_dve, xn_tok)]
                                if klev:
                                    w2.append((s_act, klev))
                            last = E("tensor", lambda e, k=k, tt=tt, wsrc=wsrc: e.matmul(
                                ps[0:64, 4, :], wsrc[:, k, :], xn[:, k, tt * 512:(tt + 1) * 512],
                                start=(k == 0), stop=(k == DJ - 1)),
                                waits=w2, inc=(s_pe if k == DJ - 1 else None))
                        klev = E("scalar", lambda e, tt=tt, ldst=ldst: e.copy(
                            ldst[:, tt * 512:(tt + 1) * 512], ps[0:64, 4, :]),
                            waits=[(s_pe, last)], inc=s_act)
                upev = 0
                for usrc, utok, latb, dstb in [(kup, tk_kup, klat, kb),
                                               (vup, tk_vup, vlat, vb)]:
                    for g in range(2):
                        for tt in range(4):
                            w2 = [(s_wt, utok), (s_act, klev)]
                            if upev:
                                w2.append((s_act, upev))
                            last = E("tensor", lambda e, tt=tt, usrc=usrc, latb=latb, g=g: e.matmul(
                                ps[:, 5, :], usrc[g * 32:(g + 1) * 32, :],
                                latb[g * 32:(g + 1) * 32, tt * 512:(tt + 1) * 512],
                                start=True, stop=True), waits=w2, inc=s_pe)
                            upev = E("scalar", lambda e, g=g, tt=tt, dstb=dstb: e.copy(
                                dstb[:, g, tt * 512:(tt + 1) * 512], ps[:, 5, :]),
                                waits=[(s_pe, last)], inc=s_act)
                vtev = 0
                for g in range(2):
                    for hw in range(2):
                        last = None
                        for j2 in range(hw * 8, hw * 8 + 8):
                            w2 = []
                            if j2 % 8 == 0:
                                w2 = [(s_act, upev)]
                                if vtev:
                                    w2.append((s_act, vtev))
                            last = E("tensor", lambda e, g=g, j2=j2, hw=hw: e.transpose(
                                psb[:, hw, (j2 % 8) * 128:(j2 % 8 + 1) * 128],
                                vb[:, g, j2 * 128:(j2 + 1) * 128], self.ident_b[:]),
                                waits=w2, inc=(s_pe if j2 % 8 == 7 else None))
                        vtev = E("scalar", lambda e, g=g, hw=hw: e.copy(
                            vt[:, g, hw * 8:hw * 8 + 8, :],
                            psb[:, hw, :].rearrange("p (a b) -> p a b", b=128)),
                            waits=[(s_pe, last)], inc=s_act)
                self.barrier()
                units = []
                for G in range(4):
                    for i in range(4 * G, 4 * G + 4):
                        for h in range(8):
                            units.append((G, i, h))
                NU = len(units)
                sc_evac = [0] * NU
                tks_t = [0] * NU
                pb_t = [0] * NU
                pt_t = [0] * NU
                ymm_t = [0] * NU
                trans_pe = [0] * NU
                yev_t = [0] * NU
                prj_dve = 0
                xold_tok = [0] * 4
                proj_done_pe = 0
                last_pt_evac = 0

                def stage_A(n):
                    nonlocal proj_done_pe
                    G, i, h = units[n]
                    jmax = i // 4
                    length = (i + 1) * 128
                    mw = length - jmax * 512
                    par = n % 2
                    g = h // 4
                    if n % 32 == 2:
                        wv = [(s_dve, prj_dve)] if prj_dve else []
                        wv.append((s_st, self.store_gate))
                        g0 = seq * T + G * 512
                        xold_tok[G] = E("scalar", lambda e, g0=g0: e.dma_start(
                            out=xold[:], in_=self.xfm_t[:, :, g0:g0 + 512]),
                            waits=wv, inc=s_x, inc_by=16)
                    last = None
                    for j in range(jmax + 1):
                        w2 = []
                        if j == 0 and n > 0 and sc_evac[n - 1]:
                            w2.append((s_act, sc_evac[n - 1]))
                        nn = 512 if j < jmax else mw
                        last = E("tensor", lambda e, i=i, h=h, j=j, g=g, nn=nn: e.matmul(
                            ps[:, j, 0:nn], qsb[:, h, i * 128:(i + 1) * 128],
                            kb[:, g, j * 512:j * 512 + nn], start=True, stop=True),
                            waits=w2, inc=(s_pe if j == jmax else None))
                    wv2 = [(s_pe, last), (s_x, tk_m)]
                    if n >= 2 and pb_t[n - 2]:
                        wv2.append((s_dve, pb_t[n - 2]))
                    sc_evac[n] = E("scalar", lambda e, par=par, length=length: e.activation(
                        Ssb[:, par, 0:length], psf[:, 0:length], AF.Copy, scale=SCALE),
                        waits=wv2, inc=s_act)

                def stage_B(n):
                    G, i, h = units[n]
                    jmax = i // 4
                    length = (i + 1) * 128
                    mw = length - jmax * 512
                    par = n % 2
                    s0, s1, s2 = 4 * par, 4 * par + 1, 4 * par + 2
                    E("vector", lambda e, par=par, i=i, jmax=jmax, mw=mw: e.tensor_tensor(
                        Ssb[:, par, jmax * 512:jmax * 512 + mw],
                        Ssb[:, par, jmax * 512:jmax * 512 + mw],
                        masks[:, i % 4, 0:mw], ALU.add), waits=[(s_act, sc_evac[n])])
                    self.dve_spacer()
                    E("vector", lambda e, par=par, length=length, s0=s0: e.reduce_max(
                        st1[:, s0, :], Ssb[:, par, 0:length], axis=AXX, negate=True))
                    self.dve_spacer()
                    tks_t[n] = self.dve_spacer(inc=s_dve)
                    E("scalar", lambda e, par=par, length=length, s0=s0, s1=s1: e.activation(
                        Ssb[:, par, 0:length], Ssb[:, par, 0:length], AF.Exp,
                        bias=st1[:, s0, :], accum_out=st1[:, s1, :]),
                        waits=[(s_dve, tks_t[n])])
                    tke = E("scalar", lambda e: e.copy(self.aspacer[:], self.aspacer[:]),
                            inc=s_act)
                    E("vector", lambda e, s1=s1, s2=s2: e.reciprocal(st1[:, s2, :], st1[:, s1, :]),
                      waits=[(s_act, tke)])
                    self.dve_spacer()
                    self.dve_spacer()
                    wpb = []
                    if n >= 2 and trans_pe[n - 2]:
                        wpb.append((s_pe, trans_pe[n - 2]))
                    pb_t[n] = E("vector", lambda e, par=par, length=length, s2=s2: e.tensor_scalar_mul(
                        Pb[:, par, 0:length], Ssb[:, par, 0:length], st1[:, s2, :]),
                        waits=wpb, inc=s_dve)

                def stage_C(n):
                    nonlocal last_pt_evac, prj_dve, proj_done_pe
                    G, i, h = units[n]
                    par = n % 2
                    g = h // 4
                    ntile = i + 1
                    for wv3 in range((ntile + 7) // 8):
                        lo = wv3 * 8
                        hi = min(lo + 8, ntile)
                        last = None
                        for j2 in range(lo, hi):
                            w2 = []
                            if j2 == lo:
                                w2 = [(s_dve, pb_t[n])]
                                if last_pt_evac:
                                    w2.append((s_act, last_pt_evac))
                            last = E("tensor", lambda e, par=par, j2=j2, wv3=wv3: e.transpose(
                                psb[:, wv3 % 2, (j2 % 8) * 128:(j2 % 8 + 1) * 128],
                                Pb[:, par, j2 * 128:(j2 + 1) * 128], self.ident_b[:]),
                                waits=w2, inc=(s_pe if j2 == hi - 1 else None))
                        trans_pe[n] = last
                        last_pt_evac = E("scalar", lambda e, lo=lo, hi=hi, wv3=wv3: e.copy(
                            PT[:, lo:hi, :], psb[:, wv3 % 2, 0:(hi - lo) * 128].rearrange(
                                "p (a b) -> p a b", b=128)), waits=[(s_pe, last)], inc=s_act)
                    last = None
                    for j2 in range(ntile):
                        w2 = []
                        if j2 == 0:
                            w2 = [(s_act, last_pt_evac), (s_act, vtev)]
                            if n >= 1 and yev_t[n - 1]:
                                w2.append((s_act, yev_t[n - 1]))
                        last = E("tensor", lambda e, g=g, j2=j2, ntile=ntile: e.matmul(
                            ps[:, 4, 0:128], vt[:, g, j2, :], PT[:, j2, :],
                            start=(j2 == 0), stop=(j2 == ntile - 1)),
                            waits=w2, inc=(s_pe if j2 == ntile - 1 else None))
                    ymm_t[n] = last
                    wv4 = [(s_pe, last)]
                    if proj_done_pe:
                        wv4.append((s_pe, proj_done_pe))
                    yev_t[n] = E("scalar", lambda e, h=h, i=i: e.copy(
                        yall[:, h, (i % 4) * 128:(i % 4 + 1) * 128], ps[:, 4, 0:128]),
                        waits=wv4, inc=s_act)

                def proj(G):
                    nonlocal prj_dve, proj_done_pe
                    g0 = seq * T + G * 512
                    y_done = yev_t[G * 32 + 31]
                    for o in range(8):
                        par = o % 2
                        last = None
                        for k in range(8):
                            w2 = []
                            if k == 0:
                                w2 = [(s_act, y_done), (s_wt, tk_woa)]
                                if prj_dve:
                                    w2.append((s_dve, prj_dve))
                            last = E("tensor", lambda e, o=o, k=k: e.matmul(
                                ps[:, 5, :], woa[:, o, k, :], yall[:, k, :],
                                start=(k == 0), stop=(k == 7)),
                                waits=w2, inc=(s_pe if k == 7 else None))
                        if o == 7:
                            proj_done_pe = last
                        wv5 = [(s_pe, last), (s_x, xold_tok[G])]
                        if xnw_dma[par]:
                            wv5.append((s_st, xnw_dma[par]))
                        prj_dve = E("vector", lambda e, o=o, par=par: e.tensor_tensor(
                            xnw[:, par, :], ps[:, 5, :], xold[:, o, :], ALU.add),
                            waits=wv5, inc=s_dve)
                        xnw_dma[par] = E("scalar", lambda e, o=o, par=par, g0=g0: e.dma_start(
                            out=self.xfm.ap()[o, :, g0:g0 + 512], in_=xnw[:, par, :]),
                            waits=[(s_dve, prj_dve)], inc=s_st, inc_by=16)

                xnw_dma = [0, 0]
                for n in range(NU + 2):
                    if n < NU:
                        stage_A(n)
                    if 0 <= n - 1 < NU:
                        stage_B(n - 1)
                    if 0 <= n - 2 < NU:
                        stage_C(n - 2)
                        if (n - 2) % 32 == 31:
                            proj((n - 2) // 32)
                self.barrier()

    def build(self):
        E = self.emit
        s_x, s_dve, s_st = self.s_x, self.s_dve, self.s_st
        t0 = E("sync", lambda e: e.dma_start(out=self.ones_f[:], in_=self.onesh.ap()),
               inc=s_x, inc_by=16)
        t1 = E("sync", lambda e: e.dma_start(out=self.ident_f[:], in_=self.identh.ap()),
               inc=s_x, inc_by=16)
        E("vector", lambda e: e.tensor_copy(self.ones_b[:], self.ones_f[:]), waits=[(s_x, t0)])
        E("vector", lambda e: e.tensor_copy(self.ident_b[:], self.ident_f[:]), waits=[(s_x, t1)])
        E("vector", lambda e: e.memset(self.eps[:], 1e-5), inc=s_dve)
        self.cache_tok = {}
        self.barrier()
        if "A" in self.phases:
            self.phase_A()
            self.store_gate = self.s_st.n
        if "F1" in self.phases:
            self.phase_F("f1", write_output=False)
            self.store_gate = self.s_st.n
        if "C" in self.phases:
            self.phase_C()
            self.store_gate = self.s_st.n
        if "F2" in self.phases:
            self.phase_F("f2", write_output=True)
        self.barrier()
        if self.debug_xfm:
            E("sync", lambda e: e.dma_start(out=self.dbg.ap(), in_=self.xfm.ap()),
              waits=[(s_st, s_st.n)], inc=s_st, inc_by=16)
        E("sync", lambda e: e.wait_ge(self.s_st.h, self.s_st.n))
        self.barrier()
        return self.nc


def pack_w(W, w):
    K, M = W.shape
    return np.ascontiguousarray(W.reshape(K // 128, 128, M // w, w).transpose(2, 1, 0, 3))


def pack_bias(b):
    return np.ascontiguousarray(b.reshape(-1, 128).T)


def make_masks():
    m = np.zeros((4, 128, 512), np.float32)
    for v in range(4):
        for q in range(128):
            m[v, q, v * 128 + q + 1:] = NEG
    return m


def host_inputs(full):
    g = {}
    g["W1p"] = pack_w(full["L0_attn_a_W1"], 512)
    g["W2p"] = pack_w(full["L0_attn_a_W2"], 512)
    g["Wzp"] = pack_w(full["L0_attn_a_Wz"], 128)
    g["Whp"] = pack_w(full["L0_attn_a_Wh"], 128)
    g["Wop"] = pack_w(full["L0_attn_a_Wo"], 128)
    g["b1h"] = pack_bias(full["L0_attn_a_b1"])
    g["b2h"] = pack_bias(full["L0_attn_a_b2"])
    g["bzh"] = pack_bias(full["L0_attn_a_bz"])
    g["bhh"] = pack_bias(full["L0_attn_a_bh"])
    g["boh"] = pack_bias(full["L0_attn_a_bo"])
    g["wqp"] = pack_w(full["L0_attn_b_wq"], 128)
    g["wkdp"] = pack_w(full["L0_attn_b_wk_down"], 64)[0]
    g["wvdp"] = pack_w(full["L0_attn_b_wv_down"], 64)[0]
    g["wkup"] = np.ascontiguousarray(full["L0_attn_b_wk_up"])
    g["wvup"] = np.ascontiguousarray(full["L0_attn_b_wv_up"])
    g["woap"] = pack_w(full["L0_attn_b_wo"], 128)
    g["wrgp"] = pack_w(np.concatenate([np.asarray(full["L0_ffn_w_router"]),
                                       np.asarray(full["L0_ffn_w_gating"])], axis=1), 16)[0]
    g["wgp"] = np.stack([pack_w(np.asarray(full["L0_ffn_w_gate"])[e], 512) for e in range(NE)])
    g["wup"] = np.stack([pack_w(np.asarray(full["L0_ffn_w_up"])[e], 512) for e in range(NE)])
    g["wdp"] = np.stack([pack_w(np.asarray(full["L0_ffn_w_down"])[e], 128) for e in range(NE)])
    g["onesh"] = np.ones((128, 128), np.float32)
    g["identh"] = np.eye(128, dtype=np.float32)
    g["maskh"] = make_masks()
    g = {k: np.ascontiguousarray(v, dtype=np.float32) for k, v in g.items()}
    x = np.asarray(full["L0_attn_a_x"])
    xs = [np.ascontiguousarray(x[2 * c:2 * c + 2]) for c in range(8)]
    return g, xs


def kernel(**inputs):
    b = Builder()
    nc = b.build()
    g, xs = host_inputs(inputs)
    in_maps = [dict(g, x=xs[c]) for c in range(8)]
    res = run_bass_kernel_spmd(nc, in_maps, core_ids=list(range(8)))
    return np.concatenate([r["y"] for r in res.results], axis=0)
